# version 06
# baseline (speedup 1.0000x reference)
import math
import contextlib
import numpy as np
import concourse.bass as bass
import concourse.mybir as mybir
from concourse.bass_utils import run_bass_kernel_spmd

F32 = mybir.dt.float32
BF16 = mybir.dt.bfloat16
AF = mybir.ActivationFunctionType
ALU = mybir.AluOpType
AX = mybir.AxisListType

H = 2048
NFH = 4
FHD = 512
INTER = 5632
TOWN = 1024
NTB = TOWN // 512
EPS_LN = 1e-5
EPS_RMS = 1e-6
BASE_LR_INV = 0.001 + float(np.log(-np.expm1(-0.001)))
ASCALE = 1.0 / math.sqrt(128.0)
GROUPS = [[0, 1], [2, 3], [4, 5], [6, 7]]

DEBUG = True
LAST_RESULTS = [None]

ENGINES = ("pe", "dve", "act", "sync", "pool")
U = 512

DMA_SEMS = ("sIN", "sX0", "sX1", "sX2", "sX3", "sWA", "sKS", "sKH", "sVH", "sOW",
            "sOT", "sW0", "sW1", "sLR", "sAW", "sFW", "sST", "sFK", "sFV", "sFQ",
            "sRS", "sRL", "sON", "sAT", "sWD", "sXW", "sYW", "sOC", "sIP")
ENG_SEMS = ("PE", "DV", "AC", "CC", "GO")


class B:
    def __init__(self, nc):
        self.nc = nc
        self.th = {e: [] for e in ENGINES}
        self.sems = {}
        self.c = {}
        self.wm = {e: {} for e in ENGINES}
        self.flush_fn = None
        self.last_flush = {}

    def add_sem(self, name, handle):
        self.sems[name] = handle
        self.c[name] = 0

    def raw(self, eng, fn):
        self.th[eng].append(fn)

    def op(self, eng, build, sem=None, n=1):
        if sem is not None:
            self.c[sem] += n
            s, k = self.sems[sem], n

            def th(e, b=build, s=s, k=k):
                b(e).then_inc(s, k)
        else:
            def th(e, b=build):
                b(e)
        self.th[eng].append(th)
        return self.c[sem] if sem is not None else None

    def dma(self, eng, sem, build):
        return self.op(eng, build, sem=sem, n=16)

    def wait(self, eng, sem, val):
        assert val <= self.c[sem], f"wait {sem}>={val} emitted={self.c[sem]}"
        if val <= 0 or self.wm[eng].get(sem, -1) >= val:
            return
        if self.flush_fn is not None and val == self.c[sem] and \
                ((sem == "DV" and eng != "dve") or (sem == "AC" and eng != "act")):
            if self.last_flush.get(sem, -1) < val:
                self.flush_fn[sem]()
                self.last_flush[sem] = self.c[sem]
            val = self.c[sem]
            if self.wm[eng].get(sem, -1) >= val:
                return
        self.wm[eng][sem] = val
        self.th[eng].append(lambda e, s=self.sems[sem], v=val: e.wait_ge(s, v))

    def dwait(self, eng, sem):
        self.wait(eng, sem, self.c[sem])

    def barrier(self):
        snap = dict(self.c)
        for eng in ENGINES:
            for s, v in snap.items():
                self.wait(eng, s, v)


def build_program():
    nc = bass.Bass("TRN2", target_bir_lowering=False,
                   detect_race_conditions=False)

    def din(name, shape, dt=F32):
        return nc.dram_tensor(name, shape, dt, kind="ExternalInput").ap()

    I = {}
    for name, shape in (
        ("x_own", [TOWN, H]), ("wqkvT", [H, 3 * H]), ("woT", [H, H]),
        ("wmlpT", [H, 3 * H]), ("lrwT", [H, 12]), ("wmoT", [H, H]),
        ("w0T", [NFH, FHD, FHD]), ("w1om", [NFH, FHD, FHD]),
        ("w1T", [NFH, FHD, FHD]), ("w2T", [NFH, FHD, FHD]),
        ("gT", [H, INTER]), ("uT", [H, INTER]), ("dT", [INTER, H]),
        ("qnw", [128, 16]), ("knw", [128, 16]), ("mlpnw", [128, 16]),
        ("tttnw", [128, NFH]), ("gate_s", [128, 44]), ("up_s", [128, 44]),
        ("down_row", [1, H]), ("lrb_row", [1, 12]), ("eye_in", [128, 128]),
        ("psw_in", [128, 128]), ("cosT_in", [128, TOWN]), ("sinT_in", [128, TOWN]),
        ("mlo_in", [128, 128]), ("mhi_in", [128, 128]), ("hb_in", [128, 1]),
        ("ga_in", [1, 1]), ("gb_in", [1, 1]), ("gout_in", [1, 1]),
    ):
        I[name] = din(name, shape)

    y_own = nc.dram_tensor("y_own", [TOWN, H], F32, kind="ExternalOutput").ap()
    DBG = {}
    if DEBUG:
        for nm, shape, dt in (("dbg_ks", [H, TOWN], BF16), ("dbg_oT", [H, TOWN], BF16),
                              ("dbg_att", [TOWN, H], BF16), ("dbg_fq", [H, TOWN], BF16),
                              ("dbg_fk", [H, TOWN], BF16), ("dbg_fv", [H, TOWN], BF16),
                              ("dbg_oc", [H, TOWN], BF16), ("dbg_xn", [TOWN, H], F32),
                              ("dbg_rg", [16 * FHD, FHD], BF16)):
            DBG[nm] = nc.dram_tensor(nm, shape, dt, kind="ExternalOutput").ap()
        DBG["dbg_scr"] = nc.dram_tensor("dbg_scr", [128, 2048], BF16,
                                        kind="ExternalOutput").ap()
        DBG["dbg_lnx"] = nc.dram_tensor("dbg_lnx", [128, 8192], BF16,
                                        kind="ExternalOutput").ap()
        DBG["dbg_cols"] = nc.dram_tensor("dbg_cols", [128, 8], F32,
                                         kind="ExternalOutput").ap()

    S = {}
    for name, shape, dt in (
        ("ksend", [H, TOWN], BF16), ("vsend", [TOWN, H], BF16),
        ("kgat", [2 * H, TOWN], BF16), ("vgat", [2 * TOWN, H], BF16),
        ("oT_s", [H, TOWN], BF16), ("att_s", [TOWN, H], BF16),
        ("fqT_s", [H, TOWN], BF16), ("fkT_s", [H, TOWN], BF16),
        ("fvT_s", [H, TOWN], BF16), ("oc_s", [H, TOWN], BF16),
        ("xnew_s", [TOWN, H], F32),
        ("rsend", [16 * FHD, FHD], BF16), ("rgat", [16 * FHD, FHD], BF16),
    ):
        S[name] = nc.dram_tensor(name, shape, dt, kind="Internal").ap()

    st = contextlib.ExitStack()
    with st:
        def sb(name, shape, dt):
            return st.enter_context(nc.sbuf_tensor(name, shape, dt))

        sm = {}
        sm["eye"] = sb("eye", [128, 128], BF16)
        sm["psw"] = sb("psw", [128, 128], BF16)
        sm["mlo"] = sb("mlo", [128, 128], BF16)
        sm["mhi"] = sb("mhi", [128, 128], BF16)
        sm["cosT"] = sb("cosT", [128, TOWN], BF16)
        sm["sinT"] = sb("sinT", [128, TOWN], BF16)
        sm["onescol"] = sb("onescol", [128, 1], BF16)
        sm["onesrow32"] = sb("onesrow32", [1, 128], F32)
        for n_, w_ in (("qnw", 16), ("knw", 16), ("mlpnw", 16), ("tttnw", NFH),
                       ("gate_s", 44), ("up_s", 44)):
            sm[n_] = sb("g_" + n_, [128, w_], F32)
        sm["ds_bc"] = sb("ds_bc", [128, H], F32)
        sm["lrb_bc"] = sb("lrb_bc", [128, 12], F32)
        sm["hbcol"] = sb("hbcol", [128, 1], F32)
        sm["gcols"] = sb("gcols", [128, 4], F32)
        sm["lr_tok"] = sb("lr_tok", [128, 8 * 12], F32)
        sm["mucol"] = sb("mucol", [128, 16], F32)
        sm["rscol"] = sb("rscol", [128, 16], F32)
        sm["row"] = sb("row", [1, 2560], F32)
        sm["tiny"] = sb("tiny", [1, 16], F32)
        sm["w12"] = sb("w12", [128, 16], F32)
        sm["s12"] = sb("s12", [128, 16], F32)
        sm["stat"] = sb("stat", [128, 16], F32)
        sm["rbc"] = sb("rbc", [128, 512], F32)
        sm["rbc2"] = sb("rbc2", [128, 512], F32)
        sm["cb"] = sb("cb", [128, 3], F32)
        sm["dbgc"] = sb("dbgc", [128, 8], F32)

        AR = sb("AR", [128, 142 * U], BF16)
        A32 = sb("A32", [128, 8704], F32)

        ps = [st.enter_context(nc.psum_tensor(f"ps{i}", [128, 512], F32))
              for i in range(6)]
        pst = [st.enter_context(nc.psum_tensor("pstA", [128, 1024], BF16)),
               st.enter_context(nc.psum_tensor("pstB", [128, 1024], BF16))]

        bld = B(nc)
        for s_ in ENG_SEMS + DMA_SEMS:
            bld.add_sem(s_, st.enter_context(nc.semaphore(s_)))

        emit_all(bld, I, S, sm, AR, A32, ps, pst, y_own, DBG)

        blk = st.enter_context(nc.Block())
        for eng_name, key in (("tensor", "pe"), ("vector", "dve"), ("scalar", "act"),
                              ("sync", "sync"), ("gpsimd", "pool")):
            def body(e, lst=bld.th[key]):
                for t in lst:
                    t(e)
            getattr(blk, eng_name)(body)

    return nc


def emit_all(b, I, S, sm, AR, A32, ps, pst, y_own, DBG):
    PE, DV, AC, CC = "PE", "DV", "AC", "CC"

    def ar(u0, nu):
        assert u0 + nu <= 142
        return AR[:, u0 * U:(u0 + nu) * U]

    def arv(u0, ntiles, w=512):
        nu = (ntiles * w + U - 1) // U
        assert u0 + nu <= 142
        t = AR[:, u0 * U:u0 * U + ntiles * w]
        return t.rearrange("p (a c) -> p a c", c=w)

    def mm(out, lhsT, rhs, start, stop, sem=None):
        return b.op("pe", lambda e, o=out, l=lhsT, r=rhs, s1=start, s2=stop:
                    e.matmul(o, l, r, start=s1, stop=s2), sem=sem)

    def tr(out, in_, sem=None):
        return b.op("pe", lambda e, o=out, i=in_: e.transpose(o, i, sm["eye"][:]),
                    sem=sem)

    def pslot(i):
        bank = pst[i % 2]
        c0 = 128 * ((i // 2) % 4)
        return bank[:, c0:c0 + 128]

    def dve(fn, sem=None):
        return b.op("dve", fn, sem=sem)

    def act(fn, sem=None):
        return b.op("act", fn, sem=sem)

    def _dv_flush():
        b.op("dve", lambda e: e.tensor_copy(sm["dbgc"][:, 7:8], sm["cb"][:, 0:1]),
             sem="DV")

    def _ac_flush():
        b.op("act", lambda e: e.copy(sm["dbgc"][:, 6:7], sm["cb"][:, 0:1]), sem="AC")

    b.flush_fn = {"DV": _dv_flush, "AC": _ac_flush}

    def rsqrt_row(n, scale):
        r = sm["row"][0:1, 2048:2048 + n]
        act(lambda e: e.activation(r, r, AF.Ln, scale=scale, bias=sm["cb"][0:1, 0:1]),
            sem=AC)
        b.wait("act", AC, b.c[AC])
        return act(lambda e: e.activation(r, r, AF.Exp, scale=-0.5), sem=AC)

    def bcast_row(n, psbank, dst):
        b.wait("pe", AC, b.c[AC])
        b.wait("pe", DV, b.c[DV])
        mm(psbank[:, 0:n], sm["onesrow32"][:], sm["row"][0:1, 2048:2048 + n],
           True, True, sem=PE)
        b.wait("dve", PE, b.c[PE])
        dve(lambda e: e.tensor_copy(dst[:, 0:n], psbank[:, 0:n]), sem=DV)
        b.wait("pe", DV, b.c[DV])

    def load_w(dst_u, srcT, rows, col0, w, sem, split=2):
        nft = rows // 128
        dst = arv(dst_u, nft, w)
        view = srcT.rearrange("(ft p) c -> p ft c", p=128)
        step = (nft + split - 1) // split
        for s0 in range(0, nft, step):
            s1 = min(nft, s0 + step)
            b.dma("pool", sem, lambda e, a=s0, c=s1: e.dma_start(
                out=dst[:, a:c, :], in_=view[:, a:c, col0:col0 + w]))

    def load_const():
        for dst, src in ((sm["qnw"], I["qnw"]), (sm["knw"], I["knw"]),
                         (sm["mlpnw"], I["mlpnw"]), (sm["tttnw"], I["tttnw"]),
                         (sm["gate_s"], I["gate_s"]), (sm["up_s"], I["up_s"]),
                         (sm["hbcol"], I["hb_in"])):
            b.dma("sync", "sIN", lambda e, d=dst, s=src: e.dma_start(out=d[:], in_=s))
        b.dma("sync", "sIN", lambda e: e.dma_start(out=sm["row"][0:1, 0:H],
                                                   in_=I["down_row"]))
        b.dma("sync", "sIN", lambda e: e.dma_start(out=sm["tiny"][0:1, 0:12],
                                                   in_=I["lrb_row"]))
        b.dma("sync", "sIN", lambda e: e.dma_start(out=sm["tiny"][0:1, 12:13],
                                                   in_=I["ga_in"]))
        b.dma("sync", "sIN", lambda e: e.dma_start(out=sm["tiny"][0:1, 13:14],
                                                   in_=I["gb_in"]))
        b.dma("sync", "sIN", lambda e: e.dma_start(out=sm["tiny"][0:1, 14:15],
                                                   in_=I["gout_in"]))
        for dst, src in ((sm["eye"], I["eye_in"]), (sm["psw"], I["psw_in"]),
                         (sm["cosT"], I["cosT_in"]), (sm["sinT"], I["sinT_in"]),
                         (sm["mlo"], I["mlo_in"]), (sm["mhi"], I["mhi_in"])):
            b.dma("pool", "sIP", lambda e, d=dst, s=src: e.dma_start(out=d[:], in_=s))
        b.dwait("dve", "sIN")
        b.dwait("dve", "sIP")
        b.dwait("pe", "sIP")
        dve(lambda e: e.memset(sm["onescol"][:], 1.0))
        dve(lambda e: e.memset(sm["onesrow32"][:], 1.0))
        dve(lambda e: e.memset(sm["cb"][:, 0:1], EPS_RMS))
        dve(lambda e: e.memset(sm["cb"][:, 1:2], 1.0))
        dve(lambda e: e.memset(sm["cb"][:, 2:3], EPS_LN))
        dve(lambda e: e.tensor_scalar_add(sm["row"][0:1, 2048:2060],
                                          sm["tiny"][0:1, 0:12], BASE_LR_INV), sem=DV)
        b.wait("pe", DV, b.c[DV])
        b.dwait("pe", "sIN")
        for j in range(4):
            mm(ps[j % 2][:], sm["onesrow32"][:], sm["row"][0:1, 512 * j:512 * (j + 1)],
               True, True, sem=PE)
            b.wait("dve", PE, b.c[PE])
            dve(lambda e, j=j, p=ps[j % 2]: e.tensor_copy(
                sm["ds_bc"][:, 512 * j:512 * (j + 1)], p[:]), sem=DV)
            b.wait("pe", DV, b.c[DV])
        b.wait("act", DV, b.c[DV])
        act(lambda e: e.activation(sm["s12"][:, 0:1], sm["cb"][:, 1:2], AF.Ln), sem=AC)
        b.wait("act", AC, b.c[AC])
        act(lambda e: e.activation(sm["s12"][:, 0:1], sm["cb"][:, 1:2], AF.Exp), sem=AC)
        b.wait("act", AC, b.c[AC])
        act(lambda e: e.activation(sm["s12"][:, 0:1], sm["cb"][:, 1:2], AF.Square),
            sem=AC)
        b.wait("act", AC, b.c[AC])
        act(lambda e: e.activation(sm["s12"][:, 0:1], sm["cb"][:, 1:2], AF.Identity),
            sem=AC)
        mm(ps[0][:, 0:12], sm["onesrow32"][:], sm["row"][0:1, 2048:2060], True, True,
           sem=PE)
        mm(ps[1][:, 0:3], sm["onesrow32"][:], sm["tiny"][0:1, 12:15], True, True, sem=PE)
        b.wait("dve", PE, b.c[PE])
        dve(lambda e: e.tensor_copy(sm["lrb_bc"][:], ps[0][:, 0:12]))
        dve(lambda e: e.tensor_copy(sm["gcols"][:, 0:3], ps[1][:, 0:3]), sem=DV)
        b.wait("pe", DV, b.c[DV])

    def phase_a():
        lnxT = arv(0, 16)
        scr = ar(16, 4)
        WB = 20
        qres = arv(46, 16, 1024)
        kres = arv(78, 16, 1024)
        vres = arv(110, 8, 2048)
        xt = [A32[:, 0:2048], A32[:, 2048:4096]]
        xsem = ["sX0", "sX1"]
        xgate = [0, 0]
        stt_ = sm["stat"]
        xv = I["x_own"].rearrange("(tt p) f -> p tt f", p=128)
        free_w = b.c[PE]

        for gt0 in range(2):
            b.dma("sync", xsem[gt0 % 2], lambda e, gt0=gt0, d=xt[gt0 % 2]: e.dma_start(
                out=d, in_=xv[:, gt0, :]))
        for tb in range(NTB):
            for tt in range(4):
                gt = tb * 4 + tt
                b.dwait("act", xsem[tt % 2])
                b.dwait("dve", xsem[tt % 2])
                b.wait("dve", AC, b.c[AC])
                dve(lambda e: e.tensor_single_scalar(stt_[:, 1:2], sm["cb"][:, 0:1],
                                                     0.0, ALU.mult))
                dve(lambda e, x=xt[tt % 2]: e.reduce_sum(stt_[:, 0:1], x, axis=AX.X),
                    sem=DV)
                b.wait("act", DV, b.c[DV])
                act(lambda e, x=xt[tt % 2]: e.activation(
                    scr, x, AF.Square, accum_out=stt_[:, 1:2]), sem=AC)
                act(lambda e: e.activation(stt_[:, 2:3], stt_[:, 0:1], AF.Square,
                                           scale=1.0 / H), sem=AC)
                b.wait("act", AC, b.c[AC])
                act(lambda e: e.activation(stt_[:, 6:7], stt_[:, 0:1], AF.Identity,
                                           scale=1.0 / H), sem=AC)
                b.wait("dve", AC, b.c[AC])
                dve(lambda e: e.scalar_tensor_tensor(
                    stt_[:, 3:4], stt_[:, 1:2], 1.0 / H, stt_[:, 2:3],
                    ALU.mult, ALU.subtract), sem=DV)
                b.wait("act", DV, b.c[DV])
                act(lambda e: e.activation(stt_[:, 4:5], stt_[:, 3:4], AF.Ln,
                                           bias=sm["cb"][:, 2:3]), sem=AC)
                b.wait("act", AC, b.c[AC])
                act(lambda e: e.activation(stt_[:, 5:6], stt_[:, 4:5], AF.Exp,
                                           scale=-0.5), sem=AC)
                b.wait("dve", AC, b.c[AC])
                dve(lambda e, x=xt[tt % 2]: e.tensor_scalar(
                    scr, x, stt_[:, 6:7], stt_[:, 5:6], ALU.subtract, ALU.mult), sem=DV)
                if gt + 2 < 4 * NTB:
                    b.wait("sync", DV, b.c[DV])
                    b.dma("sync", xsem[tt % 2], lambda e, gt=gt, d=xt[tt % 2]:
                          e.dma_start(out=d, in_=xv[:, gt + 2, :]))
                if DEBUG and tb == 0 and tt == 0:
                    dve(lambda e: e.tensor_copy(sm["dbgc"][:, 0:6], stt_[:, 0:6]),
                        sem=DV)
                    b.wait("sync", DV, b.c[DV])
                    b.dma("sync", "sKS", lambda e: e.dma_start(
                        out=DBG["dbg_scr"], in_=scr))
                xgate[tt % 2] = b.c[DV]
                b.wait("pe", DV, b.c[DV])
                for ft in range(16):
                    tr(pslot(ft), scr[:, 128 * ft:128 * (ft + 1)], sem=PE)
                    b.wait("dve", PE, b.c[PE])
                    dve(lambda e, ft=ft, tt=tt: e.tensor_copy(
                        lnxT[:, ft, 128 * tt:128 * (tt + 1)], pslot(ft)), sem=DV)
                    b.wait("pe", DV, b.c[DV] - 1)
                b.wait("pe", DV, b.c[DV])

            if DEBUG and tb == 0:
                b.wait("sync", DV, b.c[DV])
                b.dma("sync", "sKS", lambda e: e.dma_start(
                    out=DBG["dbg_lnx"], in_=AR[:, 0:8192]))
            for wc in range(12):
                b.wait("pool", PE, free_w)
                load_w(WB, I["wqkvT"], H, 512 * wc, 512, "sWA")
                b.dwait("pe", "sWA")
                wv = arv(WB, 16)
                if wc < 8:
                    dst = qres if wc < 4 else kres
                    for osub in range(4):
                        head = (wc % 4) * 4 + osub
                        bank = ps[osub % 3]
                        b.wait("pe", DV, b.c[DV])
                        for ft in range(16):
                            mm(bank[:], wv[:, ft, 128 * osub:128 * (osub + 1)],
                               lnxT[:, ft, :], ft == 0, ft == 15,
                               sem=PE if ft == 15 else None)
                        b.wait("dve", PE, b.c[PE])
                        dve(lambda e, d=dst, hd=head, tb=tb, p=bank: e.tensor_copy(
                            d[:, hd, 512 * tb:512 * (tb + 1)], p[:]), sem=DV)
                else:
                    for tt in range(4):
                        bank = ps[tt % 3]
                        b.wait("pe", DV, b.c[DV])
                        for ft in range(16):
                            mm(bank[:], lnxT[:, ft, 128 * tt:128 * (tt + 1)],
                               wv[:, ft, :], ft == 0, ft == 15,
                               sem=PE if ft == 15 else None)
                        b.wait("dve", PE, b.c[PE])
                        kt = tb * 4 + tt
                        dve(lambda e, kt=kt, wc=wc, p=bank: e.tensor_copy(
                            vres[:, kt, 512 * (wc - 8):512 * (wc - 7)], p[:]), sem=DV)
                free_w = b.c[PE]
            b.wait("pe", DV, b.c[DV])

            for (res, gam) in ((qres, sm["qnw"]), (kres, sm["knw"])):
                for ft in range(16):
                    if ft >= 4:
                        b.wait("dve", PE, b.c[PE] - 3)
                    dve(lambda e, ft=ft, r=res, tb=tb: e.tensor_mul(
                        scr[:, 512 * (ft % 4):512 * (ft % 4 + 1)],
                        r[:, ft, 512 * tb:512 * (tb + 1)],
                        r[:, ft, 512 * tb:512 * (tb + 1)]), sem=DV)
                    b.wait("pe", DV, b.c[DV])
                    mm(ps[3][0:1, 0:512], sm["onescol"][:],
                       scr[:, 512 * (ft % 4):512 * (ft % 4 + 1)], ft == 0, ft == 15,
                       sem=PE)
                b.wait("dve", PE, b.c[PE])
                dve(lambda e: e.tensor_copy(sm["row"][0:1, 2048:2560], ps[3][0:1, 0:512]),
                    sem=DV)
                b.wait("act", DV, b.c[DV])
                rsqrt_row(512, 1.0 / H)
                bcast_row(512, ps[3], sm["rbc"])
                for ft in range(16):
                    cs = slice(512 * tb, 512 * (tb + 1))
                    dve(lambda e, r=res, ft=ft, gm=gam, cs=cs: e.scalar_tensor_tensor(
                        scr[:, 0:512], r[:, ft, cs], gm[:, ft:ft + 1], sm["rbc"][:, 0:512],
                        ALU.mult, ALU.mult), sem=DV)
                    b.wait("pe", DV, b.c[DV])
                    mm(ps[4][:], sm["psw"][:], scr[:, 0:512], True, True, sem=PE)
                    b.wait("dve", PE, b.c[PE])
                    dve(lambda e, cs=cs: e.tensor_mul(
                        scr[:, 512:1024], scr[:, 0:512], sm["cosT"][:, cs]))
                    dve(lambda e, cs=cs: e.tensor_mul(
                        scr[:, 1024:1536], ps[4][:], sm["sinT"][:, cs]), sem=DV)
                    b.wait("pe", DV, b.c[DV])
                    dve(lambda e, ft=ft, r=res, cs=cs: e.tensor_add(
                        r[:, ft, cs], scr[:, 512:1024], scr[:, 1024:1536]), sem=DV)
                b.wait("pe", DV, b.c[DV])

        b.wait("sync", DV, b.c[DV])
        b.dma("sync", "sKS", lambda e: e.dma_start(
            out=S["ksend"].rearrange("(ft p) t -> p ft t", p=128), in_=kres[:, :, :]))
        b.dma("sync", "sKS", lambda e: e.dma_start(
            out=S["vsend"].rearrange("(kt p) f -> p kt f", p=128), in_=vres[:, :, :]))
        b.dwait("pool", "sKS")
        b.op("pool", lambda e: e.collective_compute(
            "AllGather", ALU.bypass, replica_groups=GROUPS,
            ins=[S["ksend"].opt()], outs=[S["kgat"].opt()]), sem=CC)
        b.op("pool", lambda e: e.collective_compute(
            "AllGather", ALU.bypass, replica_groups=GROUPS,
            ins=[S["vsend"].opt()], outs=[S["vgat"].opt()]), sem=CC)
        return qres, kres, vres

    def phase_b(qres, kres, vres):
        kh = ar(0, 2)
        vh = arv(2, 8, 128)
        ptil = ar(4, 9)
        sqb = ar(14, 4)
        oTh = ar(18, 2)
        b.wait("sync", CC, b.c[CC])
        khg = S["kgat"][0:H, :].rearrange("(ft p) t -> p ft t", p=128)
        vhg = S["vgat"][0:TOWN, :].rearrange("(kt p) f -> p kt f", p=128)

        pieces = []
        for j in range(12):
            s_lo, s_hi = max(0, j - 7), min(3, j - 1)
            if s_hi >= s_lo:
                pieces.append((j, 128 * s_lo, 128 * (s_hi - s_lo + 1), None))
            if j <= 3:
                pieces.append((j, 128 * j, 128, "lo"))
            if j >= 8:
                pieces.append((j, 128 * (j - 8), 128, "hi"))
        ppos, poff = [], 0
        for (j, c0, n, msk) in pieces:
            ppos.append((j, c0, n, msk, poff))
            poff += n
        assert poff == 4608

        pe_prev_head = b.c[PE]
        dv_prev_head = b.c[DV]
        for h in range(16):
            b.wait("sync", PE, pe_prev_head)
            b.wait("sync", DV, dv_prev_head)
            b.dwait("sync", "sOW")
            b.dma("sync", "sKH", lambda e, h=h: e.dma_start(out=kh, in_=khg[:, h, :]))
            for half in range(2):
                b.dma("sync", "sVH", lambda e, h=h, half=half: e.dma_start(
                    out=vh[:, 4 * half:4 * half + 4, :],
                    in_=vhg[:, 4 * half:4 * half + 4, 128 * h:128 * (h + 1)]))
            b.dwait("dve", "sKH")
            b.dwait("dve", "sVH")
            b.dwait("pe", "sKH")
            b.dwait("pe", "sVH")
            for idx4 in range(4):
                ksrc = kh[:, 512 * idx4:512 * (idx4 + 1)] if idx4 < 2 else \
                    kres[:, h, :][:, 512 * (idx4 - 2):512 * (idx4 - 1)]
                b.wait("dve", PE, b.c[PE])
                dve(lambda e, ksrc=ksrc: e.tensor_mul(sqb[:, 0:512], ksrc, ksrc), sem=DV)
                b.wait("pe", DV, b.c[DV])
                mm(ps[3][0:1, 0:512], sm["onescol"][:], sqb[:, 0:512], True, True, sem=PE)
                b.wait("dve", PE, b.c[PE])
                dve(lambda e, o=512 * idx4: e.tensor_copy(
                    sm["row"][0:1, o:o + 512], ps[3][0:1, 0:512]), sem=DV)

            for qb in range(NTB):
                q0 = 512 * qb
                b.wait("dve", PE, b.c[PE])
                dve(lambda e, h=h, q0=q0: e.tensor_mul(
                    sqb[:, 0:512], qres[:, h, :][:, q0:q0 + 512],
                    qres[:, h, :][:, q0:q0 + 512]), sem=DV)
                b.wait("pe", DV, b.c[DV])
                mm(ps[3][0:1, 0:512], sm["onescol"][:], sqb[:, 0:512], True, True, sem=PE)
                b.wait("dve", PE, b.c[PE])
                dve(lambda e: e.tensor_copy(sm["row"][0:1, 2048:2560], ps[3][0:1, 0:512]),
                    sem=DV)
                dve(lambda e: e.reduce_max(sm["tiny"][0:1, 4:5],
                                           sm["row"][0:1, 2048:2560], axis=AX.X))
                dve(lambda e, q0=q0: e.reduce_max(
                    sm["tiny"][0:1, 5:6], sm["row"][0:1, q0:q0 + 1536], axis=AX.X),
                    sem=DV)
                b.wait("act", DV, b.c[DV])
                act(lambda e: e.activation(sm["tiny"][0:1, 6:7], sm["tiny"][0:1, 4:5],
                                           AF.Identity, scale=sm["tiny"][0:1, 5:6]),
                    sem=AC)
                b.wait("act", AC, b.c[AC])
                act(lambda e: e.activation(sm["tiny"][0:1, 7:8], sm["tiny"][0:1, 6:7],
                                           AF.Ln), sem=AC)
                b.wait("act", AC, b.c[AC])
                act(lambda e: e.activation(sm["tiny"][0:1, 9:10], sm["tiny"][0:1, 7:8],
                                           AF.Exp, scale=0.5), sem=AC)
                b.wait("dve", AC, b.c[AC])
                dve(lambda e: e.tensor_scalar_mul(sm["tiny"][0:1, 10:11],
                                                  sm["tiny"][0:1, 9:10], -ASCALE),
                    sem=DV)
                b.wait("pe", DV, b.c[DV])
                mm(ps[3][:, 0:1], sm["onesrow32"][:], sm["tiny"][0:1, 10:11], True, True,
                   sem=PE)
                b.wait("dve", PE, b.c[PE])
                dve(lambda e: e.tensor_copy(sm["stat"][:, 3:4], ps[3][:, 0:1]), sem=DV)
                b.wait("act", DV, b.c[DV])
                act(lambda e: e.activation(sm["stat"][:, 2:3], sm["stat"][:, 3:4],
                                           AF.Identity, bias=sm["hbcol"][:]), sem=AC)
                b.wait("act", AC, b.c[AC])

                def kT_src(j, qb=qb, h=h):
                    kk0 = 512 * qb + 128 * j
                    if kk0 < 1024:
                        return kh[:, kk0:kk0 + 128]
                    return kres[:, h, :][:, kk0 - 1024:kk0 - 896]

                def v_src(j, qb=qb, h=h):
                    kk0 = 512 * qb + 128 * j
                    if kk0 < 1024:
                        return vh[:, kk0 // 128, :]
                    return vres[:, (kk0 - 1024) // 128, 128 * h:128 * (h + 1)]

                bank_free = [b.c[AC]] * 4
                bi = 0
                i0 = 0
                while i0 < len(ppos):
                    used, grp = 0, []
                    while i0 < len(ppos) and used + ppos[i0][2] <= 512:
                        grp.append(ppos[i0])
                        used += ppos[i0][2]
                        i0 += 1
                    bank = ps[bi]
                    b.wait("pe", AC, bank_free[bi])
                    ro = 0
                    for gi, (j, c0, n, msk, po) in enumerate(grp):
                        mm(bank[:, ro:ro + n], kT_src(j),
                           qres[:, h, :][:, q0 + c0:q0 + c0 + n], True, True,
                           sem=PE if gi == len(grp) - 1 else None)
                        ro += n
                    b.wait("act", PE, b.c[PE])
                    ro = 0
                    for (j, c0, n, msk, po) in grp:
                        halo = (512 * qb + 128 * j) < 1024
                        bias = sm["stat"][:, 2:3] if halo else sm["stat"][:, 3:4]
                        act(lambda e, ro=ro, n=n, po=po, bias=bias, bank=bank:
                            e.activation(ptil[:, po:po + n], bank[:, ro:ro + n], AF.Exp,
                                         scale=ASCALE, bias=bias), sem=AC)
                        ro += n
                    bank_free[bi] = b.c[AC]
                    bi = (bi + 1) % 4
                av = b.c[AC]
                b.wait("dve", AC, av)
                for (j, c0, n, msk, po) in ppos:
                    if msk is None:
                        continue
                    m = sm["mlo"] if msk == "lo" else sm["mhi"]
                    dve(lambda e, po=po, m=m: e.tensor_mul(
                        ptil[:, po:po + 128], ptil[:, po:po + 128], m[:]), sem=DV)
                dvv = b.c[DV]
                b.wait("pe", DV, dvv)
                b.wait("pe", AC, av)
                for pi, (j, c0, n, msk, po) in enumerate(ppos):
                    mm(ps[4][:, c0:c0 + n], v_src(j), ptil[:, po:po + n],
                       pi == 0, pi == len(ppos) - 1)
                for pi, (j, c0, n, msk, po) in enumerate(ppos):
                    mm(ps[5][0:1, c0:c0 + n], sm["onescol"][:], ptil[:, po:po + n],
                       pi == 0, pi == len(ppos) - 1,
                       sem=PE if pi == len(ppos) - 1 else None)
                b.wait("dve", PE, b.c[PE])
                dve(lambda e: e.tensor_scalar_add(sm["row"][0:1, 2048:2560],
                                                  ps[5][0:1, 0:512], 1e-38))
                dve(lambda e: e.reciprocal(sm["row"][0:1, 2048:2560],
                                           sm["row"][0:1, 2048:2560]), sem=DV)
                b.wait("pe", DV, b.c[DV])
                mm(ps[5][:, 0:512], sm["onesrow32"][:], sm["row"][0:1, 2048:2560],
                   True, True, sem=PE)
                b.wait("dve", PE, b.c[PE])
                dve(lambda e: e.tensor_copy(sm["rbc"][:], ps[5][:, 0:512]), sem=DV)
                dve(lambda e, qb=qb: e.tensor_mul(
                    oTh[:, 512 * qb:512 * (qb + 1)], ps[4][:, 0:512], sm["rbc"][:]),
                    sem=DV)
                b.wait("pe", DV, b.c[DV])
            pe_prev_head = b.c[PE]
            dv_prev_head = b.c[DV]
            b.wait("sync", DV, b.c[DV])
            b.dma("sync", "sOW", lambda e, h=h: e.dma_start(
                out=S["oT_s"].rearrange("(ft p) t -> p ft t", p=128)[:, h, :], in_=oTh))

    def phase_c():
        oTst = arv(0, 16)
        attn_sb = arv(16, 4, 2048)
        x2T = arv(32, 16)
        WB0, WB1 = 48, 64
        fst = arv(80, 48)
        scr = ar(128, 4)
        lrw_sb = arv(132, 16, 12)
        b.dma("pool", "sLR", lambda e: e.dma_start(
            out=lrw_sb[:, :, :], in_=I["lrwT"].rearrange("(ft p) c -> p ft c", p=128)))
        free_w = [b.c[PE]] * 2
        pe_prev = b.c[PE]
        wsem = ["sW0", "sW1"]

        for tb in range(NTB):
            b.wait("sync", PE, pe_prev)
            b.dma("sync", "sOT", lambda e, tb=tb: e.dma_start(
                out=oTst[:, :, :], in_=S["oT_s"].rearrange("(ft p) t -> p ft t", p=128)
                [:, :, 512 * tb:512 * (tb + 1)]))
            b.dwait("pe", "sOT")
            b.dwait("dve", "sAW")
            for oc in range(4):
                b.wait("pool", PE, free_w[oc % 2])
                load_w((WB0, WB1)[oc % 2], I["woT"], H, 512 * oc, 512, wsem[oc % 2])
                b.dwait("pe", wsem[oc % 2])
                wv = arv((WB0, WB1)[oc % 2], 16)
                for tt in range(4):
                    bank = ps[tt % 3]
                    b.wait("pe", DV, b.c[DV])
                    for ft in range(16):
                        mm(bank[:], oTst[:, ft, 128 * tt:128 * (tt + 1)], wv[:, ft, :],
                           ft == 0, ft == 15, sem=PE if ft == 15 else None)
                    b.wait("dve", PE, b.c[PE])
                    dve(lambda e, tt=tt, oc=oc, p=bank: e.tensor_copy(
                        attn_sb[:, tt, 512 * oc:512 * (oc + 1)], p[:]), sem=DV)
                free_w[oc % 2] = b.c[PE]
            b.wait("sync", DV, b.c[DV])
            b.dma("sync", "sAW", lambda e, tb=tb: e.dma_start(
                out=S["att_s"].rearrange("(tt p) f -> p tt f", p=128)
                [:, 4 * tb:4 * tb + 4, :], in_=attn_sb[:, :, :]))
            dve(lambda e: e.tensor_single_scalar(sm["stat"][:, 0:4], sm["qnw"][:, 0:4],
                                                 0.0, ALU.mult), sem=DV)
            for tt in range(4):
                dve(lambda e, tt=tt: e.scalar_tensor_tensor(
                    scr[:, 0:2048], attn_sb[:, tt, :], 1.0, attn_sb[:, tt, :],
                    ALU.mult, ALU.mult, accum_out=sm["stat"][:, tt:tt + 1]), sem=DV)
            b.wait("act", DV, b.c[DV])
            act(lambda e: e.activation(sm["stat"][:, 4:8], sm["stat"][:, 0:4], AF.Ln,
                                       scale=1.0 / H, bias=sm["cb"][:, 0:1]), sem=AC)
            b.wait("act", AC, b.c[AC])
            act(lambda e: e.activation(sm["stat"][:, 4:8], sm["stat"][:, 4:8], AF.Exp,
                                       scale=-0.5), sem=AC)
            b.wait("dve", AC, b.c[AC])
            for tt in range(4):
                dve(lambda e, tt=tt: e.tensor_scalar_mul(
                    scr[:, 0:2048], attn_sb[:, tt, :], sm["stat"][:, 4 + tt:5 + tt]),
                    sem=DV)
                b.wait("pe", DV, b.c[DV])
                for ft in range(16):
                    tr(pslot(ft), scr[:, 128 * ft:128 * (ft + 1)], sem=PE)
                    b.wait("dve", PE, b.c[PE])
                    dve(lambda e, ft=ft, tt=tt: e.tensor_scalar_mul(
                        x2T[:, ft, 128 * tt:128 * (tt + 1)], pslot(ft),
                        sm["mlpnw"][:, ft:ft + 1]), sem=DV)
                    b.wait("pe", DV, b.c[DV] - 1)
                b.wait("pe", DV, b.c[DV])

            b.dwait("act", "sFW")
            for wc in range(12):
                b.wait("pool", PE, free_w[wc % 2])
                load_w((WB0, WB1)[wc % 2], I["wmlpT"], H, 512 * wc, 512, wsem[wc % 2])
                b.dwait("pe", wsem[wc % 2])
                wv = arv((WB0, WB1)[wc % 2], 16)
                for osub in range(4):
                    bank = ps[osub % 3]
                    b.wait("pe", AC, b.c[AC])
                    for ft in range(16):
                        mm(bank[:], wv[:, ft, 128 * osub:128 * (osub + 1)], x2T[:, ft, :],
                           ft == 0, ft == 15, sem=PE if ft == 15 else None)
                    b.wait("act", PE, b.c[PE])
                    act(lambda e, i=4 * wc + osub, p=bank: e.activation(
                        fst[:, i, :], p[:], AF.Silu), sem=AC)
                free_w[wc % 2] = b.c[PE]
            ac_silu = b.c[AC]
            b.wait("dve", AC, ac_silu)

            for base in (0, 16):
                for hh in range(NFH):
                    for k in range(4):
                        i = base + 4 * hh + k
                        if k > 0:
                            b.wait("dve", PE, b.c[PE])
                        dve(lambda e, i=i: e.tensor_mul(scr[:, 0:512], fst[:, i, :],
                                                        fst[:, i, :]), sem=DV)
                        b.wait("pe", DV, b.c[DV])
                        mm(ps[3][0:1, 0:512], sm["onescol"][:], scr[:, 0:512],
                           k == 0, k == 3, sem=PE)
                    b.wait("dve", PE, b.c[PE])
                    dve(lambda e: e.tensor_copy(sm["row"][0:1, 2048:2560],
                                                ps[3][0:1, 0:512]), sem=DV)
                    b.wait("act", DV, b.c[DV])
                    act(lambda e: e.activation(sm["row"][0:1, 2048:2560],
                                               sm["row"][0:1, 2048:2560], AF.Ln), sem=AC)
                    b.wait("act", AC, b.c[AC])
                    act(lambda e: e.activation(sm["row"][0:1, 2048:2560],
                                               sm["row"][0:1, 2048:2560], AF.Exp,
                                               scale=0.5), sem=AC)
                    b.wait("dve", AC, b.c[AC])
                    dve(lambda e: e.tensor_scalar_max(sm["row"][0:1, 2048:2560],
                                                      sm["row"][0:1, 2048:2560], 1e-6))
                    dve(lambda e: e.reciprocal(sm["row"][0:1, 2048:2560],
                                               sm["row"][0:1, 2048:2560]), sem=DV)
                    bcast_row(512, ps[3], sm["rbc"])
                    for k in range(4):
                        i = base + 4 * hh + k
                        dve(lambda e, i=i: e.tensor_mul(fst[:, i, :], fst[:, i, :],
                                                        sm["rbc"][:, 0:512]), sem=DV)
            b.wait("sync", DV, b.c[DV])
            b.wait("sync", AC, ac_silu)
            for (dst_s, lo) in ((S["fqT_s"], 0), (S["fkT_s"], 16), (S["fvT_s"], 32)):
                b.dma("sync", "sFW", lambda e, d=dst_s, lo=lo, tb=tb: e.dma_start(
                    out=d.rearrange("(ft p) t -> p ft t", p=128)
                    [:, :, 512 * tb:512 * (tb + 1)], in_=fst[:, lo:lo + 16, :]))

            b.dwait("pe", "sLR")
            for tt in range(4):
                gt = 4 * tb + tt
                b.wait("pe", DV, b.c[DV])
                for ft in range(16):
                    mm(ps[4][:, 0:12], x2T[:, ft, 128 * tt:128 * (tt + 1)],
                       lrw_sb[:, ft, :], ft == 0, ft == 15, sem=PE if ft == 15 else None)
                b.wait("dve", PE, b.c[PE])
                dve(lambda e: e.tensor_add(sm["w12"][:, 0:12], ps[4][:, 0:12],
                                           sm["lrb_bc"][:]), sem=DV)
                b.wait("act", DV, b.c[DV])
                act(lambda e: e.activation(sm["s12"][:, 0:12], sm["w12"][:, 0:12],
                                           AF.Abs), sem=AC)
                b.wait("act", AC, b.c[AC])
                act(lambda e: e.activation(sm["s12"][:, 0:12], sm["s12"][:, 0:12],
                                           AF.Exp, scale=-1.0), sem=AC)
                b.wait("act", AC, b.c[AC])
                act(lambda e: e.activation(sm["s12"][:, 0:12], sm["s12"][:, 0:12],
                                           AF.Ln, bias=sm["cb"][:, 1:2]), sem=AC)
                b.wait("act", AC, b.c[AC])
                act(lambda e: e.activation(sm["w12"][:, 0:12], sm["w12"][:, 0:12],
                                           AF.Relu), sem=AC)
                b.wait("dve", AC, b.c[AC])
                dve(lambda e, gt=gt: e.tensor_add(
                    sm["lr_tok"][:, 12 * gt:12 * (gt + 1)], sm["w12"][:, 0:12],
                    sm["s12"][:, 0:12]), sem=DV)
            pe_prev = b.c[PE]

    def phase_d():
        mats = ("st0", "st2", "st1", "w1o")
        stu = {}
        u0 = 0
        for hh in range(NFH):
            for m in mats:
                stu[(hh, m)] = u0
                u0 += 4
        ST = lambda hh, m: arv(stu[(hh, m)], 4)
        fk_c = arv(64, 16)
        fvh = arv(80, 4)
        fqh = arv(84, 4)
        KC0, KC2, VC1, HID, DGP, DHB, SGK, GKS, HBM = [arv(88 + 4 * i, 4)
                                                       for i in range(9)]
        HQ = arv(124, 4)
        OCS = arv(128, 4)
        scr = ar(132, 4)

        def load_base():
            for hh in range(NFH):
                for m, src in zip(mats, (I["w0T"], I["w2T"], I["w1T"], I["w1om"])):
                    b.dma("pool", "sST", lambda e, hh=hh, m=m, src=src: e.dma_start(
                        out=ST(hh, m)[:, :, :],
                        in_=src[hh].rearrange("(dj p) e -> p dj e", p=128)))

        def kpath(hh, c, upd):
            fkh_t = lambda dj: fk_c[:, 4 * hh + dj, :]
            fvh_t = lambda dj: fvh[:, dj, :]
            for (srcf, outs) in ((fkh_t, ((KC0, 0), (KC2, 2))), (fvh_t, ((VC1, 1),))):
                for dj in range(4):
                    for cj in range(4):
                        tr(pslot(cj), srcf(dj)[:, 128 * cj:128 * (cj + 1)],
                           sem=PE)
                        b.wait("dve", PE, b.c[PE])
                        for (dst, li) in outs:
                            lrc = sm["lr_tok"][:, 12 * (4 * c + cj) + 4 * li + hh:
                                               12 * (4 * c + cj) + 4 * li + hh + 1]
                            dve(lambda e, dst=dst, cj=cj, dj=dj, lrc=lrc:
                                e.tensor_scalar_mul(
                                    dst[:, cj, 128 * dj:128 * (dj + 1)],
                                    pslot(cj), lrc), sem=DV)
                        b.wait("pe", DV, b.c[DV] - len(outs))
                b.wait("pe", DV, b.c[DV])
            for cj in range(4):
                for dj in range(4):
                    mm(ps[0][:], fkh_t(dj)[:, 128 * cj:128 * (cj + 1)],
                       ST(hh, "st0")[:, dj, :], dj == 0, dj == 3)
                for dj in range(4):
                    mm(ps[1][:], fkh_t(dj)[:, 128 * cj:128 * (cj + 1)],
                       ST(hh, "st2")[:, dj, :], dj == 0, dj == 3)
                for oj in range(4):
                    mm(ps[2][:], fvh_t(oj)[:, 128 * cj:128 * (cj + 1)],
                       ST(hh, "w1o")[:, oj, :], oj == 0, oj == 3,
                       sem=PE if oj == 3 else None)
                pv = b.c[PE]
                b.wait("act", PE, pv)
                act(lambda e, cj=cj: e.activation(SGK[:, cj, :], ps[0][:], AF.Sigmoid),
                    sem=AC)
                b.wait("dve", AC, b.c[AC])
                b.wait("dve", PE, pv)
                dve(lambda e, cj=cj: e.tensor_mul(GKS[:, cj, :], ps[0][:], SGK[:, cj, :]))
                dve(lambda e, cj=cj: e.tensor_sub(scr[:, 0:512], ps[0][:], GKS[:, cj, :]))
                dve(lambda e, cj=cj: e.tensor_copy(HBM[:, cj, :], ps[1][:]))
                dve(lambda e, cj=cj: e.tensor_mul(HID[:, cj, :], GKS[:, cj, :],
                                                  HBM[:, cj, :]))
                dve(lambda e, cj=cj: e.tensor_mul(scr[:, 512:1024], ps[2][:],
                                                  HBM[:, cj, :]))
                dve(lambda e, cj=cj: e.tensor_mul(DHB[:, cj, :], ps[2][:], GKS[:, cj, :]))
                dve(lambda e, cj=cj: e.scalar_tensor_tensor(
                    scr[:, 0:512], scr[:, 0:512], 1.0, SGK[:, cj, :], ALU.add, ALU.mult))
                dve(lambda e, cj=cj: e.tensor_mul(DGP[:, cj, :], scr[:, 0:512],
                                                  scr[:, 512:1024]), sem=DV)
                b.wait("pe", DV, b.c[DV])
            if not upd:
                return
            for (lhs, rhs, mat) in ((KC0, DGP, "st0"), (KC2, DHB, "st2"),
                                    (VC1, HID, "w1o"), (HID, VC1, "st1")):
                for dj in range(4):
                    bank = ps[3 + dj % 2]
                    for cj in range(4):
                        mm(bank[:], lhs[:, cj, 128 * dj:128 * (dj + 1)], rhs[:, cj, :],
                           cj == 0, cj == 3, sem=PE if cj == 3 else None)
                    b.wait("dve", PE, b.c[PE])
                    dve(lambda e, mat=mat, dj=dj, hh=hh, p=bank: e.tensor_add(
                        ST(hh, mat)[:, dj, :], ST(hh, mat)[:, dj, :], p[:]), sem=DV)
                    if dj >= 1:
                        b.wait("pe", DV, b.c[DV] - 1)
                b.wait("pe", DV, b.c[DV])

        def qpath(hh, c):
            for ej in range(4):
                for dj in range(4):
                    mm(ps[0][:], ST(hh, "st0")[:, dj, 128 * ej:128 * (ej + 1)],
                       fqh[:, dj, :], dj == 0, dj == 3)
                for dj in range(4):
                    mm(ps[1][:], ST(hh, "st2")[:, dj, 128 * ej:128 * (ej + 1)],
                       fqh[:, dj, :], dj == 0, dj == 3, sem=PE if dj == 3 else None)
                pv = b.c[PE]
                b.wait("act", PE, pv)
                act(lambda e: e.activation(scr[:, 0:512], ps[0][:], AF.Sigmoid), sem=AC)
                b.wait("dve", AC, b.c[AC])
                b.wait("dve", PE, pv)
                dve(lambda e: e.tensor_mul(scr[:, 512:1024], ps[0][:], scr[:, 0:512]))
                dve(lambda e, ej=ej: e.tensor_mul(HQ[:, ej, :], scr[:, 512:1024],
                                                  ps[1][:]), sem=DV)
                b.wait("pe", DV, b.c[DV])
            b.dwait("dve", "sOC")
            b.dwait("dve", "sRS")
            for oj in range(4):
                bank = ps[2 + oj % 2]
                for ej in range(4):
                    mm(bank[:], ST(hh, "st1")[:, ej, 128 * oj:128 * (oj + 1)],
                       HQ[:, ej, :], ej == 0, ej == 3, sem=PE if ej == 3 else None)
                b.wait("dve", PE, b.c[PE])
                dve(lambda e, oj=oj, p=bank: e.tensor_copy(OCS[:, oj, :], p[:]), sem=DV)
                if oj >= 1:
                    b.wait("pe", DV, b.c[DV] - 1)
            b.wait("sync", DV, b.c[DV])
            b.dma("sync", "sOC", lambda e, hh=hh, c=c: e.dma_start(
                out=S["oc_s"].rearrange("(q p) t -> p q t", p=128)
                [:, 4 * hh:4 * hh + 4, 512 * c:512 * (c + 1)], in_=OCS[:, :, :]))

        def stream_chunk(c):
            b.dma("sync", "sFK", lambda e, c=c: e.dma_start(
                out=fk_c[:, :, :], in_=S["fkT_s"].rearrange("(ft p) t -> p ft t", p=128)
                [:, :, 512 * c:512 * (c + 1)]))

        def stream_head(c, hh, src, dst, sem):
            b.dma("sync", sem, lambda e, c=c, hh=hh, src=src, dst=dst: e.dma_start(
                out=dst[:, :, :], in_=src.rearrange("(ft p) t -> p ft t", p=128)
                [:, 4 * hh:4 * hh + 4, 512 * c:512 * (c + 1)]))

        load_base()
        b.dwait("pe", "sST")
        b.dwait("dve", "sST")
        for c in range(2):
            b.wait("sync", PE, b.c[PE])
            b.wait("sync", DV, b.c[DV])
            stream_chunk(c)
            for hh in range(NFH):
                b.wait("sync", PE, b.c[PE])
                stream_head(c, hh, S["fvT_s"], fvh, "sFV")
                b.dwait("pe", "sFK")
                b.dwait("dve", "sFK")
                b.dwait("pe", "sFV")
                b.dwait("dve", "sFV")
                kpath(hh, c, upd=True)
        rsv = S["rsend"].rearrange("(i p) e -> p i e", p=128)
        for hh in range(NFH):
            for mi, m in enumerate(mats):
                b.dwait("dve", "sRS")
                for dj in range(4):
                    dve(lambda e, hh=hh, m=m, dj=dj: e.tensor_scalar_mul(
                        OCS[:, dj, :], ST(hh, m)[:, dj, :], sm["gcols"][:, 2:3]), sem=DV)
                b.wait("sync", DV, b.c[DV])
                idx = (hh * 4 + mi) * 4
                b.dma("sync", "sRS", lambda e, idx=idx: e.dma_start(
                    out=rsv[:, idx:idx + 4, :], in_=OCS[:, :, :]))
        b.dwait("pool", "sRS")
        b.op("pool", lambda e: e.collective_compute(
            "AllReduce", ALU.add, replica_groups=GROUPS,
            ins=[S["rsend"].opt()], outs=[S["rgat"].opt()]), sem=CC)

        load_base()
        b.dwait("dve", "sST")
        b.wait("sync", CC, b.c[CC])
        rgv = S["rgat"].rearrange("(i p) e -> p i e", p=128)
        for hh in range(NFH):
            for mi, m in enumerate(mats):
                idx = (hh * 4 + mi) * 4
                b.wait("sync", DV, b.c[DV])
                b.dma("sync", "sRL", lambda e, idx=idx: e.dma_start(
                    out=fqh[:, :, :], in_=rgv[:, idx:idx + 4, :]))
                b.dwait("dve", "sRL")
                for dj in range(4):
                    dve(lambda e, hh=hh, m=m, dj=dj: e.tensor_scalar_mul(
                        ST(hh, m)[:, dj, :], ST(hh, m)[:, dj, :], sm["gcols"][:, 0:1]))
                    dve(lambda e, hh=hh, m=m, dj=dj: e.scalar_tensor_tensor(
                        ST(hh, m)[:, dj, :], fqh[:, dj, :], sm["gcols"][:, 1:2],
                        ST(hh, m)[:, dj, :], ALU.mult, ALU.add), sem=DV)
        b.wait("pe", DV, b.c[DV])

        for c in range(2):
            b.wait("sync", PE, b.c[PE])
            b.wait("sync", DV, b.c[DV])
            stream_chunk(c)
            for hh in range(NFH):
                b.wait("sync", PE, b.c[PE])
                b.wait("sync", DV, b.c[DV])
                stream_head(c, hh, S["fqT_s"], fqh, "sFQ")
                stream_head(c, hh, S["fvT_s"], fvh, "sFV")
                b.dwait("pe", "sFK")
                b.dwait("dve", "sFK")
                b.dwait("pe", "sFQ")
                b.dwait("pe", "sFV")
                b.dwait("dve", "sFV")
                qpath(hh, c)
                kpath(hh, c, upd=(c == 0))

    def phase_e():
        onst = arv(0, 16)
        ln2T = arv(0, 16)
        hdn = arv(16, 44)
        WB0, WB1 = 60, 76
        WBD = 92
        scr = ar(136, 4)
        att4 = arv(16, 4, 2048)
        xn = [A32[:, 2048 * i:2048 * (i + 1)] for i in range(4)]
        xsem = ["sX0", "sX1", "sX2", "sX3"]
        outt = A32[:, 8192:8704]
        xv = I["x_own"].rearrange("(tt p) f -> p tt f", p=128)
        av = S["att_s"].rearrange("(tt p) f -> p tt f", p=128)
        xnv = S["xnew_s"].rearrange("(tt p) f -> p tt f", p=128)
        yv = y_own.rearrange("(tt p) f -> p tt f", p=128)
        free_w = [b.c[PE]] * 2
        wsem = ["sW0", "sW1"]
        pe_prev = b.c[PE]

        for tb in range(NTB):
            b.wait("sync", PE, pe_prev)
            b.wait("sync", DV, b.c[DV])
            b.dma("sync", "sON", lambda e, tb=tb: e.dma_start(
                out=onst[:, :, :], in_=S["oc_s"].rearrange("(q p) t -> p q t", p=128)
                [:, :, 512 * tb:512 * (tb + 1)]))
            b.dwait("dve", "sON")
            b.dwait("pe", "sON")
            for hh in range(NFH):
                for k in range(4):
                    i = 4 * hh + k
                    if k > 0:
                        b.wait("dve", PE, b.c[PE])
                    dve(lambda e, i=i: e.tensor_mul(scr[:, 0:512], onst[:, i, :],
                                                    onst[:, i, :]), sem=DV)
                    b.wait("pe", DV, b.c[DV])
                    mm(ps[5][0:1, 0:512], sm["onescol"][:], scr[:, 0:512],
                       k == 0, k == 3, sem=PE)
                b.wait("dve", PE, b.c[PE])
                dve(lambda e: e.tensor_copy(sm["row"][0:1, 2048:2560],
                                            ps[5][0:1, 0:512]), sem=DV)
                b.wait("act", DV, b.c[DV])
                rsqrt_row(512, 1.0 / FHD)
                bcast_row(512, ps[5], sm["rbc"])
                for k in range(4):
                    i = 4 * hh + k
                    dve(lambda e, i=i, k=k: e.scalar_tensor_tensor(
                        onst[:, i, :], onst[:, i, :], sm["tttnw"][:, k:k + 1],
                        sm["rbc"][:, 0:512], ALU.mult, ALU.mult), sem=DV)
            b.wait("pe", DV, b.c[DV])
            b.dwait("sync", "sXW")
            for tt in range(4):
                gt = 4 * tb + tt
                b.dma("sync", xsem[tt], lambda e, gt=gt, d=xn[tt]: e.dma_start(
                    out=d, in_=xv[:, gt, :]))
            b.wait("sync", DV, b.c[DV])
            b.dma("sync", "sAT", lambda e, tb=tb: e.dma_start(
                out=att4[:, :, :], in_=av[:, 4 * tb:4 * tb + 4, :]))
            for tt in range(4):
                b.dwait("dve", xsem[tt])
            b.dwait("dve", "sAT")
            for mc in range(4):
                b.wait("pool", PE, free_w[mc % 2])
                load_w((WB0, WB1)[mc % 2], I["wmoT"], H, 512 * mc, 512, wsem[mc % 2])
                b.dwait("pe", wsem[mc % 2])
                wv = arv((WB0, WB1)[mc % 2], 16)
                cs = slice(512 * mc, 512 * (mc + 1))
                for tt in range(4):
                    bank = ps[tt % 3]
                    b.wait("pe", DV, b.c[DV])
                    for ft in range(16):
                        mm(bank[:], onst[:, ft, 128 * tt:128 * (tt + 1)], wv[:, ft, :],
                           ft == 0, ft == 15, sem=PE if ft == 15 else None)
                    b.wait("dve", PE, b.c[PE])
                    dve(lambda e, cs=cs, d=xn[tt], p=bank: e.tensor_add(
                        d[:, cs], d[:, cs], p[:]))
                    dve(lambda e, cs=cs, d=xn[tt], tt=tt: e.tensor_add(
                        d[:, cs], d[:, cs], att4[:, tt, cs]), sem=DV)
                free_w[mc % 2] = b.c[PE]
            for tt in range(4):
                gt = 4 * tb + tt
                dve(lambda e, gt=gt: e.tensor_single_scalar(
                    sm["rscol"][:, gt:gt + 1], sm["cb"][:, 0:1], 0.0, ALU.mult))
                dve(lambda e, d=xn[tt], gt=gt: e.reduce_sum(
                    sm["mucol"][:, gt:gt + 1], d, axis=AX.X), sem=DV)
                dve(lambda e, d=xn[tt], gt=gt: e.scalar_tensor_tensor(
                    scr[:, 0:2048], d, 1.0, d, ALU.mult, ALU.mult,
                    accum_out=sm["rscol"][:, gt:gt + 1]), sem=DV)
                b.wait("sync", DV, b.c[DV])
                b.dma("sync", "sXW", lambda e, gt=gt, d=xn[tt]: e.dma_start(
                    out=xnv[:, gt, :], in_=d))
            pe_prev = b.c[PE]
        act(lambda e: e.activation(sm["rscol"][:, 8:16], sm["mucol"][:, 0:8],
                                   AF.Square, scale=1.0 / H), sem=AC)
        b.wait("act", AC, b.c[AC])
        act(lambda e: e.activation(sm["mucol"][:, 8:16], sm["mucol"][:, 0:8],
                                   AF.Identity, scale=1.0 / H), sem=AC)
        b.wait("dve", AC, b.c[AC])
        dve(lambda e: e.scalar_tensor_tensor(
            sm["rscol"][:, 0:8], sm["rscol"][:, 0:8], 1.0 / H, sm["rscol"][:, 8:16],
            ALU.mult, ALU.subtract), sem=DV)
        b.wait("act", DV, b.c[DV])
        act(lambda e: e.activation(sm["rscol"][:, 0:8], sm["rscol"][:, 0:8], AF.Ln,
                                   bias=sm["cb"][:, 2:3]), sem=AC)
        b.wait("act", AC, b.c[AC])
        act(lambda e: e.activation(sm["rscol"][:, 0:8], sm["rscol"][:, 0:8], AF.Exp,
                                   scale=-0.5), sem=AC)
        act(lambda e: e.copy(sm["mucol"][:, 0:8], sm["mucol"][:, 8:16]), sem=AC)
        b.wait("dve", AC, b.c[AC])

        for tb in range(NTB):
            for tt in range(4):
                gt = 4 * tb + tt
                b.dwait("sync", "sYW")
                b.dma("sync", xsem[tt], lambda e, gt=gt, d=xn[tt]: e.dma_start(
                    out=d, in_=xnv[:, gt, :]))
                b.dwait("dve", xsem[tt])
                b.wait("dve", PE, b.c[PE])
                dve(lambda e, d=xn[tt], gt=gt: e.tensor_scalar(
                    scr[:, 0:2048], d, sm["mucol"][:, gt:gt + 1],
                    sm["rscol"][:, gt:gt + 1], ALU.subtract, ALU.mult), sem=DV)
                b.wait("pe", DV, b.c[DV])
                for ft in range(16):
                    tr(pslot(ft), scr[:, 128 * ft:128 * (ft + 1)], sem=PE)
                    b.wait("dve", PE, b.c[PE])
                    dve(lambda e, ft=ft, tt=tt: e.tensor_copy(
                        ln2T[:, ft, 128 * tt:128 * (tt + 1)], pslot(ft)), sem=DV)
                    b.wait("pe", DV, b.c[DV] - 1)
                b.wait("pe", DV, b.c[DV])
            for wc in range(11):
                b.wait("pool", PE, free_w[0])
                load_w(WB0, I["gT"], H, 512 * wc, 512, "sW0")
                b.wait("pool", PE, free_w[1])
                load_w(WB1, I["uT"], H, 512 * wc, 512, "sW1")
                b.dwait("pe", "sW0")
                b.dwait("pe", "sW1")
                gvw, uvw = arv(WB0, 16), arv(WB1, 16)
                for isub in range(4):
                    icol = 4 * wc + isub
                    b.wait("pe", AC, b.c[AC])
                    for ft in range(16):
                        mm(ps[isub % 2][:], gvw[:, ft, 128 * isub:128 * (isub + 1)],
                           ln2T[:, ft, :], ft == 0, ft == 15, sem=PE if ft == 15 else None)
                    b.wait("act", PE, b.c[PE])
                    act(lambda e, icol=icol, p=ps[isub % 2]: e.activation(
                        sm["rbc2"][:, 0:512], p[:], AF.Silu,
                        scale=sm["gate_s"][:, icol:icol + 1]), sem=AC)
                    acv = b.c[AC]
                    b.wait("pe", DV, b.c[DV])
                    for ft in range(16):
                        mm(ps[2 + isub % 2][:], uvw[:, ft, 128 * isub:128 * (isub + 1)],
                           ln2T[:, ft, :], ft == 0, ft == 15, sem=PE if ft == 15 else None)
                    b.wait("dve", PE, b.c[PE])
                    b.wait("dve", AC, acv)
                    dve(lambda e, icol=icol, p=ps[2 + isub % 2]: e.scalar_tensor_tensor(
                        hdn[:, icol, :], p[:], sm["up_s"][:, icol:icol + 1],
                        sm["rbc2"][:, 0:512], ALU.mult, ALU.mult), sem=DV)
                free_w[0] = free_w[1] = b.c[PE]
            b.wait("pe", DV, b.c[DV])
            for mc in range(4):
                b.wait("pool", PE, free_w[0])
                load_w(WBD, I["dT"], INTER, 512 * mc, 512, "sWD", split=4)
                b.dwait("pe", "sWD")
                dwv = arv(WBD, 44)
                for tt in range(4):
                    bank = ps[tt % 2]
                    b.wait("pe", DV, b.c[DV])
                    for ki in range(44):
                        mm(bank[:], hdn[:, ki, 128 * tt:128 * (tt + 1)], dwv[:, ki, :],
                           ki == 0, ki == 43, sem=PE if ki == 43 else None)
                    b.wait("dve", PE, b.c[PE])
                    cs = slice(512 * mc, 512 * (mc + 1))
                    dve(lambda e, cs=cs, p=bank: e.tensor_mul(
                        outt[:, 0:512], p[:], sm["ds_bc"][:, cs]))
                    dve(lambda e, cs=cs, tt=tt: e.tensor_add(
                        xn[tt][:, cs], xn[tt][:, cs], outt[:, 0:512]), sem=DV)
                free_w[0] = b.c[PE]
            b.wait("sync", DV, b.c[DV])
            for tt in range(4):
                gt = 4 * tb + tt
                b.dma("sync", "sYW", lambda e, gt=gt, tt=tt: e.dma_start(
                    out=yv[:, gt, :], in_=xn[tt]))
        b.dwait("sync", "sYW")
        if DEBUG:
            b.dma("sync", "sYW", lambda e: e.dma_start(out=DBG["dbg_cols"],
                                                       in_=sm["dbgc"][:]))
            for nm, srcs in (("dbg_ks", "ksend"), ("dbg_oT", "oT_s"),
                             ("dbg_att", "att_s"), ("dbg_fq", "fqT_s"),
                             ("dbg_fk", "fkT_s"), ("dbg_fv", "fvT_s"),
                             ("dbg_oc", "oc_s"), ("dbg_xn", "xnew_s"),
                             ("dbg_rg", "rgat")):
                b.dma("sync", "sYW", lambda e, d=DBG[nm], s=S[srcs]: e.dma_start(
                    out=d, in_=s))
            b.dwait("sync", "sYW")

    nums = sorted(h.num for h in b.sems.values())
    lo, hi = nums[0], nums[-1]
    assert nums == list(range(lo, hi + 1)), nums
    b.raw("pool", lambda e, lo=lo, hi=hi: e.sem_clear(range(lo, hi + 1)))
    b.raw("pool", lambda e: e.sem_inc(b.sems["GO"], 1))
    b.c["GO"] = 1
    for _eng in ("pe", "dve", "act", "sync"):
        b.wait(_eng, "GO", 1)

    load_const()
    b.barrier()
    qres, kres, vres = phase_a()
    b.barrier()
    phase_b(qres, kres, vres)
    b.barrier()
    phase_c()
    b.barrier()
    phase_d()
    b.barrier()
    phase_e()



def kernel(**inp):
    x = np.ascontiguousarray(np.asarray(inp["L0_attn_hidden_states"], np.float32))
    Bn, Sq, _ = x.shape

    T = lambda a: np.ascontiguousarray(np.asarray(a, np.float32).T)
    shared = dict(
        wqkvT=T(inp["L0_attn_attn_qkv_w"]), woT=T(inp["L0_attn_attn_o_w"]),
        wmlpT=T(inp["L0_attn_mlp_qkv_w"]), lrwT=T(inp["L0_attn_lr_w"]),
        wmoT=T(inp["L0_attn_mlp_o_w"]),
        w0T=np.ascontiguousarray(np.asarray(inp["L0_attn_w0"], np.float32)
                                 .transpose(0, 2, 1)),
        w1om=np.ascontiguousarray(np.asarray(inp["L0_attn_w1"], np.float32)),
        w1T=np.ascontiguousarray(np.asarray(inp["L0_attn_w1"], np.float32)
                                 .transpose(0, 2, 1)),
        w2T=np.ascontiguousarray(np.asarray(inp["L0_attn_w2"], np.float32)
                                 .transpose(0, 2, 1)),
        gT=T(inp["L0_ffn_gate_w"]), uT=T(inp["L0_ffn_up_w"]), dT=T(inp["L0_ffn_down_w"]),
    )
    col = lambda a, n: np.ascontiguousarray(np.asarray(a, np.float32).reshape(n, 128).T)
    shared["qnw"] = col(inp["L0_attn_q_norm_w"], 16)
    shared["knw"] = col(inp["L0_attn_k_norm_w"], 16)
    shared["mlpnw"] = col(inp["L0_attn_mlp_norm_w"], 16)
    shared["tttnw"] = col(inp["L0_attn_ttt_norm_w"], 4)
    shared["gate_s"] = col(inp["L0_ffn_gate_s"], 44)
    shared["up_s"] = col(inp["L0_ffn_up_s"], 44)
    shared["down_row"] = np.asarray(inp["L0_ffn_down_s"], np.float32).reshape(1, -1)
    shared["lrb_row"] = np.asarray(inp["L0_attn_lr_b"], np.float32).reshape(1, -1)
    shared["eye_in"] = np.eye(128, dtype=np.float32)
    psw = np.zeros((128, 128), np.float32)
    for d in range(64):
        psw[d + 64, d] = -1.0
        psw[d, d + 64] = 1.0
    shared["psw_in"] = psw
    ii = np.arange(128)[:, None]
    cc = np.arange(128)[None, :]
    shared["mlo_in"] = (ii > cc).astype(np.float32)
    shared["mhi_in"] = (ii <= cc).astype(np.float32)

    invf = 1.0 / (500000.0 ** (np.arange(0, 128, 2, dtype=np.float64) / 128.0))
    fr = np.outer(np.arange(Sq, dtype=np.float64), invf)
    cosT = np.concatenate([np.cos(fr).T, np.cos(fr).T], axis=0).astype(np.float32)
    sinT = np.concatenate([np.sin(fr).T, np.sin(fr).T], axis=0).astype(np.float32)

    nc = build_program()

    in_maps = []
    for c in range(8):
        bq, p = c // 2, c % 2
        t0 = TOWN * p
        m = dict(shared)
        m["x_own"] = np.ascontiguousarray(x[bq, t0:t0 + TOWN])
        m["cosT_in"] = np.ascontiguousarray(cosT[:, t0:t0 + TOWN])
        m["sinT_in"] = np.ascontiguousarray(sinT[:, t0:t0 + TOWN])
        m["hb_in"] = np.full((128, 1), 0.0 if p == 1 else -1e30, np.float32)
        m["ga_in"] = np.array([[1.0 if p == 0 else 0.0]], np.float32)
        m["gb_in"] = np.array([[0.0 if p == 0 else 1.0]], np.float32)
        m["gout_in"] = np.array([[1.0 if p == 0 else 0.0]], np.float32)
        in_maps.append(m)

    res = run_bass_kernel_spmd(nc, in_maps, core_ids=list(range(8)))
    LAST_RESULTS[0] = res
    out = np.empty((Bn, Sq, H), np.float32)
    for c in range(8):
        bq, p = c // 2, c % 2
        out[bq, TOWN * p:TOWN * (p + 1)] = np.asarray(res.results[c]["y_own"])
    return out


# version 07
# speedup vs baseline: 1.0712x; 1.0712x over previous
import math
import contextlib
import numpy as np
import concourse.bass as bass
import concourse.mybir as mybir
from concourse.bass_utils import run_bass_kernel_spmd

F32 = mybir.dt.float32
BF16 = mybir.dt.bfloat16
AF = mybir.ActivationFunctionType
ALU = mybir.AluOpType
AX = mybir.AxisListType

H = 2048
NFH = 4
FHD = 512
INTER = 5632
TOWN = 1024
NTB = TOWN // 512
EPS_LN = 1e-5
EPS_RMS = 1e-6
BASE_LR_INV = 0.001 + float(np.log(-np.expm1(-0.001)))
ASCALE = 1.0 / math.sqrt(128.0)
GROUPS = [[0, 1], [2, 3], [4, 5], [6, 7]]

DEBUG = False
LAST_RESULTS = [None]

ENGINES = ("pe", "dve", "act", "sync", "pool")
U = 512

DMA_SEMS = ("sIN", "sX0", "sX1", "sX2", "sX3", "sWA", "sKS", "sKH", "sVH", "sOW",
            "sOT", "sW0", "sW1", "sLR", "sAW", "sFW", "sST", "sFK", "sFV", "sFQ",
            "sRS", "sRL", "sON", "sAT", "sWD", "sXW", "sYW", "sOC", "sIP")
ENG_SEMS = ("PE", "DV", "AC", "CC", "GO")


class B:
    def __init__(self, nc):
        self.nc = nc
        self.th = {e: [] for e in ENGINES}
        self.sems = {}
        self.c = {}
        self.wm = {e: {} for e in ENGINES}
        self.flush_fn = None
        self.last_flush = {}

    def add_sem(self, name, handle):
        self.sems[name] = handle
        self.c[name] = 0

    def raw(self, eng, fn):
        self.th[eng].append(fn)

    def op(self, eng, build, sem=None, n=1):
        if sem is not None:
            self.c[sem] += n
            s, k = self.sems[sem], n

            def th(e, b=build, s=s, k=k):
                b(e).then_inc(s, k)
        else:
            def th(e, b=build):
                b(e)
        self.th[eng].append(th)
        return self.c[sem] if sem is not None else None

    def dma(self, eng, sem, build):
        return self.op(eng, build, sem=sem, n=16)

    def wait(self, eng, sem, val):
        assert val <= self.c[sem], f"wait {sem}>={val} emitted={self.c[sem]}"
        if val <= 0 or self.wm[eng].get(sem, -1) >= val:
            return
        if self.flush_fn is not None and val == self.c[sem] and \
                ((sem == "DV" and eng != "dve") or (sem == "AC" and eng != "act")):
            if self.last_flush.get(sem, -1) < val:
                self.flush_fn[sem]()
                self.last_flush[sem] = self.c[sem]
            val = self.c[sem]
            if self.wm[eng].get(sem, -1) >= val:
                return
        self.wm[eng][sem] = val
        self.th[eng].append(lambda e, s=self.sems[sem], v=val: e.wait_ge(s, v))

    def dwait(self, eng, sem):
        self.wait(eng, sem, self.c[sem])

    def barrier(self):
        snap = dict(self.c)
        for eng in ENGINES:
            for s, v in snap.items():
                self.wait(eng, s, v)


def build_program():
    nc = bass.Bass("TRN2", target_bir_lowering=False,
                   detect_race_conditions=False)

    def din(name, shape, dt=F32):
        return nc.dram_tensor(name, shape, dt, kind="ExternalInput").ap()

    I = {}
    for name, shape in (
        ("x_own", [TOWN, H]), ("wqkvT", [H, 3 * H]), ("woT", [H, H]),
        ("wmlpT", [H, 3 * H]), ("lrwT", [H, 12]), ("wmoT", [H, H]),
        ("w0T", [NFH, FHD, FHD]), ("w1om", [NFH, FHD, FHD]),
        ("w1T", [NFH, FHD, FHD]), ("w2T", [NFH, FHD, FHD]),
        ("gT", [H, INTER]), ("uT", [H, INTER]), ("dT", [INTER, H]),
        ("qnw", [128, 16]), ("knw", [128, 16]), ("mlpnw", [128, 16]),
        ("tttnw", [128, NFH]), ("gate_s", [128, 44]), ("up_s", [128, 44]),
        ("down_row", [1, H]), ("lrb_row", [1, 12]), ("eye_in", [128, 128]),
        ("psw_in", [128, 128]), ("cosT_in", [128, TOWN]), ("sinT_in", [128, TOWN]),
        ("mlo_in", [128, 128]), ("mhi_in", [128, 128]), ("hb_in", [128, 1]),
        ("ga_in", [1, 1]), ("gb_in", [1, 1]), ("gout_in", [1, 1]),
    ):
        I[name] = din(name, shape)

    y_own = nc.dram_tensor("y_own", [TOWN, H], F32, kind="ExternalOutput").ap()
    DBG = {}
    if DEBUG:
        for nm, shape, dt in (("dbg_ks", [H, TOWN], BF16), ("dbg_oT", [H, TOWN], BF16),
                              ("dbg_att", [TOWN, H], BF16), ("dbg_fq", [H, TOWN], BF16),
                              ("dbg_fk", [H, TOWN], BF16), ("dbg_fv", [H, TOWN], BF16),
                              ("dbg_oc", [H, TOWN], BF16), ("dbg_xn", [TOWN, H], F32),
                              ("dbg_rg", [16 * FHD, FHD], BF16)):
            DBG[nm] = nc.dram_tensor(nm, shape, dt, kind="ExternalOutput").ap()
        DBG["dbg_scr"] = nc.dram_tensor("dbg_scr", [128, 2048], BF16,
                                        kind="ExternalOutput").ap()
        DBG["dbg_lnx"] = nc.dram_tensor("dbg_lnx", [128, 8192], BF16,
                                        kind="ExternalOutput").ap()
        DBG["dbg_cols"] = nc.dram_tensor("dbg_cols", [128, 8], F32,
                                         kind="ExternalOutput").ap()

    S = {}
    for name, shape, dt in (
        ("ksend", [H, TOWN], BF16), ("vsend", [TOWN, H], BF16),
        ("kgat", [2 * H, TOWN], BF16), ("vgat", [2 * TOWN, H], BF16),
        ("oT_s", [H, TOWN], BF16), ("att_s", [TOWN, H], BF16),
        ("fqT_s", [H, TOWN], BF16), ("fkT_s", [H, TOWN], BF16),
        ("fvT_s", [H, TOWN], BF16), ("oc_s", [H, TOWN], BF16),
        ("xnew_s", [TOWN, H], F32),
        ("rsend", [16 * FHD, FHD], BF16), ("rgat", [16 * FHD, FHD], BF16),
    ):
        S[name] = nc.dram_tensor(name, shape, dt, kind="Internal").ap()

    st = contextlib.ExitStack()
    with st:
        def sb(name, shape, dt):
            return st.enter_context(nc.sbuf_tensor(name, shape, dt))

        sm = {}
        sm["eye"] = sb("eye", [128, 128], BF16)
        sm["psw"] = sb("psw", [128, 128], BF16)
        sm["mlo"] = sb("mlo", [128, 128], BF16)
        sm["mhi"] = sb("mhi", [128, 128], BF16)
        sm["cosT"] = sb("cosT", [128, TOWN], BF16)
        sm["sinT"] = sb("sinT", [128, TOWN], BF16)
        sm["onescol"] = sb("onescol", [128, 1], BF16)
        sm["onesrow32"] = sb("onesrow32", [1, 128], F32)
        for n_, w_ in (("qnw", 16), ("knw", 16), ("mlpnw", 16), ("tttnw", NFH),
                       ("gate_s", 44), ("up_s", 44)):
            sm[n_] = sb("g_" + n_, [128, w_], F32)
        sm["ds_bc"] = sb("ds_bc", [128, H], F32)
        sm["lrb_bc"] = sb("lrb_bc", [128, 12], F32)
        sm["hbcol"] = sb("hbcol", [128, 1], F32)
        sm["gcols"] = sb("gcols", [128, 4], F32)
        sm["lr_tok"] = sb("lr_tok", [128, 8 * 12], F32)
        sm["mucol"] = sb("mucol", [128, 16], F32)
        sm["rscol"] = sb("rscol", [128, 16], F32)
        sm["row"] = sb("row", [1, 2560], F32)
        sm["tiny"] = sb("tiny", [1, 16], F32)
        sm["w12"] = sb("w12", [128, 16], F32)
        sm["s12"] = sb("s12", [128, 16], F32)
        sm["stat"] = sb("stat", [128, 16], F32)
        sm["rbc"] = sb("rbc", [128, 512], F32)
        sm["rbc2"] = sb("rbc2", [128, 1024], F32)
        sm["cb"] = sb("cb", [128, 3], F32)
        sm["dbgc"] = sb("dbgc", [128, 8], F32)

        AR = sb("AR", [128, 142 * U], BF16)
        A32 = sb("A32", [128, 8704], F32)

        ps = [st.enter_context(nc.psum_tensor(f"ps{i}", [128, 512], F32))
              for i in range(6)]
        pst = [st.enter_context(nc.psum_tensor("pstA", [128, 1024], BF16)),
               st.enter_context(nc.psum_tensor("pstB", [128, 1024], BF16))]

        bld = B(nc)
        for s_ in ENG_SEMS + DMA_SEMS:
            bld.add_sem(s_, st.enter_context(nc.semaphore(s_)))

        emit_all(bld, I, S, sm, AR, A32, ps, pst, y_own, DBG)

        blk = st.enter_context(nc.Block())
        for eng_name, key in (("tensor", "pe"), ("vector", "dve"), ("scalar", "act"),
                              ("sync", "sync"), ("gpsimd", "pool")):
            def body(e, lst=bld.th[key]):
                for t in lst:
                    t(e)
            getattr(blk, eng_name)(body)

    return nc


def emit_all(b, I, S, sm, AR, A32, ps, pst, y_own, DBG):
    PE, DV, AC, CC = "PE", "DV", "AC", "CC"

    def ar(u0, nu):
        assert u0 + nu <= 142
        return AR[:, u0 * U:(u0 + nu) * U]

    def arv(u0, ntiles, w=512):
        nu = (ntiles * w + U - 1) // U
        assert u0 + nu <= 142
        t = AR[:, u0 * U:u0 * U + ntiles * w]
        return t.rearrange("p (a c) -> p a c", c=w)

    def mm(out, lhsT, rhs, start, stop, sem=None):
        return b.op("pe", lambda e, o=out, l=lhsT, r=rhs, s1=start, s2=stop:
                    e.matmul(o, l, r, start=s1, stop=s2), sem=sem)

    def tr(out, in_, sem=None):
        return b.op("pe", lambda e, o=out, i=in_: e.transpose(o, i, sm["eye"][:]),
                    sem=sem)

    def pslot(i):
        bank = pst[i % 2]
        c0 = 128 * ((i // 2) % 4)
        return bank[:, c0:c0 + 128]

    def dve(fn, sem=None):
        return b.op("dve", fn, sem=sem)

    def act(fn, sem=None):
        return b.op("act", fn, sem=sem)

    def _dv_flush():
        b.op("dve", lambda e: e.tensor_copy(sm["dbgc"][:, 7:8], sm["cb"][:, 0:1]),
             sem="DV")

    def _ac_flush():
        b.op("act", lambda e: e.copy(sm["dbgc"][:, 6:7], sm["cb"][:, 0:1]), sem="AC")

    b.flush_fn = {"DV": _dv_flush, "AC": _ac_flush}

    def rsqrt_row(n, scale):
        r = sm["row"][0:1, 2048:2048 + n]
        act(lambda e: e.activation(r, r, AF.Ln, scale=scale, bias=sm["cb"][0:1, 0:1]),
            sem=AC)
        b.wait("act", AC, b.c[AC])
        return act(lambda e: e.activation(r, r, AF.Exp, scale=-0.5), sem=AC)

    def bcast_row(n, psbank, dst):
        b.wait("pe", AC, b.c[AC])
        b.wait("pe", DV, b.c[DV])
        mm(psbank[:, 0:n], sm["onesrow32"][:], sm["row"][0:1, 2048:2048 + n],
           True, True, sem=PE)
        b.wait("dve", PE, b.c[PE])
        dve(lambda e: e.tensor_copy(dst[:, 0:n], psbank[:, 0:n]), sem=DV)
        b.wait("pe", DV, b.c[DV])

    def load_w(dst_u, srcT, rows, col0, w, sem, split=2):
        nft = rows // 128
        dst = arv(dst_u, nft, w)
        view = srcT.rearrange("(ft p) c -> p ft c", p=128)
        step = (nft + split - 1) // split
        for s0 in range(0, nft, step):
            s1 = min(nft, s0 + step)
            b.dma("pool", sem, lambda e, a=s0, c=s1: e.dma_start(
                out=dst[:, a:c, :], in_=view[:, a:c, col0:col0 + w]))

    def load_const():
        for dst, src in ((sm["qnw"], I["qnw"]), (sm["knw"], I["knw"]),
                         (sm["mlpnw"], I["mlpnw"]), (sm["tttnw"], I["tttnw"]),
                         (sm["gate_s"], I["gate_s"]), (sm["up_s"], I["up_s"]),
                         (sm["hbcol"], I["hb_in"])):
            b.dma("sync", "sIN", lambda e, d=dst, s=src: e.dma_start(out=d[:], in_=s))
        b.dma("sync", "sIN", lambda e: e.dma_start(out=sm["row"][0:1, 0:H],
                                                   in_=I["down_row"]))
        b.dma("sync", "sIN", lambda e: e.dma_start(out=sm["tiny"][0:1, 0:12],
                                                   in_=I["lrb_row"]))
        b.dma("sync", "sIN", lambda e: e.dma_start(out=sm["tiny"][0:1, 12:13],
                                                   in_=I["ga_in"]))
        b.dma("sync", "sIN", lambda e: e.dma_start(out=sm["tiny"][0:1, 13:14],
                                                   in_=I["gb_in"]))
        b.dma("sync", "sIN", lambda e: e.dma_start(out=sm["tiny"][0:1, 14:15],
                                                   in_=I["gout_in"]))
        for dst, src in ((sm["eye"], I["eye_in"]), (sm["psw"], I["psw_in"]),
                         (sm["cosT"], I["cosT_in"]), (sm["sinT"], I["sinT_in"]),
                         (sm["mlo"], I["mlo_in"]), (sm["mhi"], I["mhi_in"])):
            b.dma("pool", "sIP", lambda e, d=dst, s=src: e.dma_start(out=d[:], in_=s))
        b.dwait("dve", "sIN")
        b.dwait("dve", "sIP")
        b.dwait("pe", "sIP")
        for nm in ("lr_tok", "mucol", "rscol", "w12", "s12",
                   "stat", "rbc", "rbc2"):
            dve(lambda e, nm=nm: e.memset(sm[nm][:], 0.0))
        dve(lambda e: e.memset(sm["onescol"][:], 1.0))
        dve(lambda e: e.memset(sm["onesrow32"][:], 1.0))
        dve(lambda e: e.memset(sm["cb"][:, 0:1], EPS_RMS))
        dve(lambda e: e.memset(sm["cb"][:, 1:2], 1.0))
        dve(lambda e: e.memset(sm["cb"][:, 2:3], EPS_LN))
        dve(lambda e: e.tensor_scalar_add(sm["row"][0:1, 2048:2060],
                                          sm["tiny"][0:1, 0:12], BASE_LR_INV), sem=DV)
        b.wait("pe", DV, b.c[DV])
        b.dwait("pe", "sIN")
        for j in range(4):
            mm(ps[j % 2][:], sm["onesrow32"][:], sm["row"][0:1, 512 * j:512 * (j + 1)],
               True, True, sem=PE)
            b.wait("dve", PE, b.c[PE])
            dve(lambda e, j=j, p=ps[j % 2]: e.tensor_copy(
                sm["ds_bc"][:, 512 * j:512 * (j + 1)], p[:]), sem=DV)
            b.wait("pe", DV, b.c[DV])
        b.wait("act", DV, b.c[DV])
        act(lambda e: e.activation(sm["s12"][:, 0:1], sm["cb"][:, 1:2], AF.Ln), sem=AC)
        b.wait("act", AC, b.c[AC])
        act(lambda e: e.activation(sm["s12"][:, 0:1], sm["cb"][:, 1:2], AF.Exp), sem=AC)
        b.wait("act", AC, b.c[AC])
        act(lambda e: e.activation(sm["s12"][:, 0:1], sm["cb"][:, 1:2], AF.Square),
            sem=AC)
        b.wait("act", AC, b.c[AC])
        act(lambda e: e.activation(sm["s12"][:, 0:1], sm["cb"][:, 1:2], AF.Identity),
            sem=AC)
        mm(ps[0][:, 0:12], sm["onesrow32"][:], sm["row"][0:1, 2048:2060], True, True,
           sem=PE)
        mm(ps[1][:, 0:3], sm["onesrow32"][:], sm["tiny"][0:1, 12:15], True, True, sem=PE)
        b.wait("dve", PE, b.c[PE])
        dve(lambda e: e.tensor_copy(sm["lrb_bc"][:], ps[0][:, 0:12]))
        dve(lambda e: e.tensor_copy(sm["gcols"][:, 0:3], ps[1][:, 0:3]), sem=DV)
        b.wait("pe", DV, b.c[DV])

    def phase_a():
        lnxT = arv(0, 16)
        scr = ar(16, 4)
        WB = 20
        qres = arv(46, 16, 1024)
        kres = arv(78, 16, 1024)
        vres = arv(110, 8, 2048)
        xt = [A32[:, 0:2048], A32[:, 2048:4096]]
        xsem = ["sX0", "sX1"]
        xgate = [0, 0]
        stt_ = sm["stat"]
        xv = I["x_own"].rearrange("(tt p) f -> p tt f", p=128)
        free_w = b.c[PE]
        bfree = [0, 0, 0]

        for gt0 in range(2):
            b.dma("sync", xsem[gt0 % 2], lambda e, gt0=gt0, d=xt[gt0 % 2]: e.dma_start(
                out=d, in_=xv[:, gt0, :]))
        for tb in range(NTB):
            for tt in range(4):
                gt = tb * 4 + tt
                b.dwait("act", xsem[tt % 2])
                b.dwait("dve", xsem[tt % 2])
                b.wait("dve", AC, b.c[AC])
                dve(lambda e: e.tensor_single_scalar(stt_[:, 1:2], sm["cb"][:, 0:1],
                                                     0.0, ALU.mult))
                dve(lambda e, x=xt[tt % 2]: e.reduce_sum(stt_[:, 0:1], x, axis=AX.X),
                    sem=DV)
                b.wait("act", DV, b.c[DV])
                act(lambda e, x=xt[tt % 2]: e.activation(
                    scr, x, AF.Square, accum_out=stt_[:, 1:2]), sem=AC)
                act(lambda e: e.activation(stt_[:, 2:3], stt_[:, 0:1], AF.Square,
                                           scale=1.0 / H), sem=AC)
                b.wait("act", AC, b.c[AC])
                act(lambda e: e.activation(stt_[:, 6:7], stt_[:, 0:1], AF.Identity,
                                           scale=1.0 / H), sem=AC)
                b.wait("dve", AC, b.c[AC])
                dve(lambda e: e.scalar_tensor_tensor(
                    stt_[:, 3:4], stt_[:, 1:2], 1.0 / H, stt_[:, 2:3],
                    ALU.mult, ALU.subtract), sem=DV)
                b.wait("act", DV, b.c[DV])
                act(lambda e: e.activation(stt_[:, 4:5], stt_[:, 3:4], AF.Ln,
                                           bias=sm["cb"][:, 2:3]), sem=AC)
                b.wait("act", AC, b.c[AC])
                act(lambda e: e.activation(stt_[:, 5:6], stt_[:, 4:5], AF.Exp,
                                           scale=-0.5), sem=AC)
                b.wait("dve", AC, b.c[AC])
                dve(lambda e, x=xt[tt % 2]: e.tensor_scalar(
                    scr, x, stt_[:, 6:7], stt_[:, 5:6], ALU.subtract, ALU.mult), sem=DV)
                if gt + 2 < 4 * NTB:
                    b.wait("sync", DV, b.c[DV])
                    b.dma("sync", xsem[tt % 2], lambda e, gt=gt, d=xt[tt % 2]:
                          e.dma_start(out=d, in_=xv[:, gt + 2, :]))
                if DEBUG and tb == 0 and tt == 0:
                    dve(lambda e: e.tensor_copy(sm["dbgc"][:, 0:6], stt_[:, 0:6]),
                        sem=DV)
                    b.wait("sync", DV, b.c[DV])
                    b.dma("sync", "sKS", lambda e: e.dma_start(
                        out=DBG["dbg_scr"], in_=scr))
                xgate[tt % 2] = b.c[DV]
                b.wait("pe", DV, b.c[DV])
                for ft in range(16):
                    tr(pslot(ft), scr[:, 128 * ft:128 * (ft + 1)], sem=PE)
                    b.wait("dve", PE, b.c[PE])
                    dve(lambda e, ft=ft, tt=tt: e.tensor_copy(
                        lnxT[:, ft, 128 * tt:128 * (tt + 1)], pslot(ft)), sem=DV)
                    b.wait("pe", DV, b.c[DV] - 1)
                b.wait("pe", DV, b.c[DV])

            if DEBUG and tb == 0:
                b.wait("sync", DV, b.c[DV])
                b.dma("sync", "sKS", lambda e: e.dma_start(
                    out=DBG["dbg_lnx"], in_=AR[:, 0:8192]))
            for wc in range(12):
                b.wait("pool", PE, free_w)
                load_w(WB, I["wqkvT"], H, 512 * wc, 512, "sWA")
                b.dwait("pe", "sWA")
                wv = arv(WB, 16)
                if wc < 8:
                    dst = qres if wc < 4 else kres
                    for osub in range(4):
                        head = (wc % 4) * 4 + osub
                        bi = (4 * wc + osub) % 3
                        bank = ps[bi]
                        b.wait("pe", DV, bfree[bi])
                        for ft in range(16):
                            mm(bank[:], wv[:, ft, 128 * osub:128 * (osub + 1)],
                               lnxT[:, ft, :], ft == 0, ft == 15,
                               sem=PE if ft == 15 else None)
                        b.wait("dve", PE, b.c[PE])
                        dve(lambda e, d=dst, hd=head, tb=tb, p=bank: e.tensor_copy(
                            d[:, hd, 512 * tb:512 * (tb + 1)], p[:]), sem=DV)
                        bfree[bi] = b.c[DV]
                else:
                    for tt in range(4):
                        bi = (4 * wc + tt) % 3
                        bank = ps[bi]
                        b.wait("pe", DV, bfree[bi])
                        for ft in range(16):
                            mm(bank[:], lnxT[:, ft, 128 * tt:128 * (tt + 1)],
                               wv[:, ft, :], ft == 0, ft == 15,
                               sem=PE if ft == 15 else None)
                        b.wait("dve", PE, b.c[PE])
                        kt = tb * 4 + tt
                        dve(lambda e, kt=kt, wc=wc, p=bank: e.tensor_copy(
                            vres[:, kt, 512 * (wc - 8):512 * (wc - 7)], p[:]), sem=DV)
                        bfree[bi] = b.c[DV]
                free_w = b.c[PE]
            b.wait("pe", DV, b.c[DV])

            for (res, gam) in ((qres, sm["qnw"]), (kres, sm["knw"])):
                for ft in range(16):
                    if ft >= 4:
                        b.wait("dve", PE, b.c[PE] - 3)
                    dve(lambda e, ft=ft, r=res, tb=tb: e.tensor_mul(
                        scr[:, 512 * (ft % 4):512 * (ft % 4 + 1)],
                        r[:, ft, 512 * tb:512 * (tb + 1)],
                        r[:, ft, 512 * tb:512 * (tb + 1)]), sem=DV)
                    b.wait("pe", DV, b.c[DV])
                    mm(ps[3][0:1, 0:512], sm["onescol"][:],
                       scr[:, 512 * (ft % 4):512 * (ft % 4 + 1)], ft == 0, ft == 15,
                       sem=PE)
                b.wait("dve", PE, b.c[PE])
                dve(lambda e: e.tensor_copy(sm["row"][0:1, 2048:2560], ps[3][0:1, 0:512]),
                    sem=DV)
                b.wait("act", DV, b.c[DV])
                rsqrt_row(512, 1.0 / H)
                bcast_row(512, ps[3], sm["rbc"])
                for ft in range(16):
                    cs = slice(512 * tb, 512 * (tb + 1))
                    dve(lambda e, r=res, ft=ft, gm=gam, cs=cs: e.scalar_tensor_tensor(
                        scr[:, 0:512], r[:, ft, cs], gm[:, ft:ft + 1], sm["rbc"][:, 0:512],
                        ALU.mult, ALU.mult), sem=DV)
                    b.wait("pe", DV, b.c[DV])
                    mm(ps[4][:], sm["psw"][:], scr[:, 0:512], True, True, sem=PE)
                    b.wait("dve", PE, b.c[PE])
                    dve(lambda e, cs=cs: e.tensor_mul(
                        scr[:, 512:1024], scr[:, 0:512], sm["cosT"][:, cs]))
                    dve(lambda e, cs=cs: e.tensor_mul(
                        scr[:, 1024:1536], ps[4][:], sm["sinT"][:, cs]), sem=DV)
                    b.wait("pe", DV, b.c[DV])
                    dve(lambda e, ft=ft, r=res, cs=cs: e.tensor_add(
                        r[:, ft, cs], scr[:, 512:1024], scr[:, 1024:1536]), sem=DV)
                b.wait("pe", DV, b.c[DV])

        b.wait("sync", DV, b.c[DV])
        b.dma("sync", "sKS", lambda e: e.dma_start(
            out=S["ksend"].rearrange("(ft p) t -> p ft t", p=128), in_=kres[:, :, :]))
        b.dma("sync", "sKS", lambda e: e.dma_start(
            out=S["vsend"].rearrange("(kt p) f -> p kt f", p=128), in_=vres[:, :, :]))
        b.dwait("pool", "sKS")
        b.op("pool", lambda e: e.collective_compute(
            "AllGather", ALU.bypass, replica_groups=GROUPS,
            ins=[S["ksend"].opt()], outs=[S["kgat"].opt()]), sem=CC)
        b.op("pool", lambda e: e.collective_compute(
            "AllGather", ALU.bypass, replica_groups=GROUPS,
            ins=[S["vsend"].opt()], outs=[S["vgat"].opt()]), sem=CC)
        return qres, kres, vres

    def phase_b(qres, kres, vres):
        kh = ar(0, 2)
        vh = arv(2, 8, 128)
        ptil = ar(4, 9)
        sqb = ar(14, 4)
        oTh = ar(18, 2)
        b.wait("sync", CC, b.c[CC])
        khg = S["kgat"][0:H, :].rearrange("(ft p) t -> p ft t", p=128)
        vhg = S["vgat"][0:TOWN, :].rearrange("(kt p) f -> p kt f", p=128)

        pieces = []
        for j in range(12):
            s_lo, s_hi = max(0, j - 7), min(3, j - 1)
            if s_hi >= s_lo:
                pieces.append((j, 128 * s_lo, 128 * (s_hi - s_lo + 1), None))
            if j <= 3:
                pieces.append((j, 128 * j, 128, "lo"))
            if j >= 8:
                pieces.append((j, 128 * (j - 8), 128, "hi"))
        ppos, poff = [], 0
        for (j, c0, n, msk) in pieces:
            ppos.append((j, c0, n, msk, poff))
            poff += n
        assert poff == 4608

        pe_prev_head = b.c[PE]
        dv_prev_head = b.c[DV]
        for h in range(16):
            b.wait("sync", PE, pe_prev_head)
            b.wait("sync", DV, dv_prev_head)
            b.dwait("sync", "sOW")
            b.dma("sync", "sKH", lambda e, h=h: e.dma_start(out=kh, in_=khg[:, h, :]))
            for half in range(2):
                b.dma("sync", "sVH", lambda e, h=h, half=half: e.dma_start(
                    out=vh[:, 4 * half:4 * half + 4, :],
                    in_=vhg[:, 4 * half:4 * half + 4, 128 * h:128 * (h + 1)]))
            b.dwait("dve", "sKH")
            b.dwait("dve", "sVH")
            b.dwait("pe", "sKH")
            b.dwait("pe", "sVH")
            for idx4 in range(4):
                ksrc = kh[:, 512 * idx4:512 * (idx4 + 1)] if idx4 < 2 else \
                    kres[:, h, :][:, 512 * (idx4 - 2):512 * (idx4 - 1)]
                b.wait("dve", PE, b.c[PE])
                dve(lambda e, ksrc=ksrc: e.tensor_mul(sqb[:, 0:512], ksrc, ksrc), sem=DV)
                b.wait("pe", DV, b.c[DV])
                mm(ps[3][0:1, 0:512], sm["onescol"][:], sqb[:, 0:512], True, True, sem=PE)
                b.wait("dve", PE, b.c[PE])
                dve(lambda e, o=512 * idx4: e.tensor_copy(
                    sm["row"][0:1, o:o + 512], ps[3][0:1, 0:512]), sem=DV)

            for qb in range(NTB):
                q0 = 512 * qb
                b.wait("dve", PE, b.c[PE])
                dve(lambda e, h=h, q0=q0: e.tensor_mul(
                    sqb[:, 0:512], qres[:, h, :][:, q0:q0 + 512],
                    qres[:, h, :][:, q0:q0 + 512]), sem=DV)
                b.wait("pe", DV, b.c[DV])
                mm(ps[3][0:1, 0:512], sm["onescol"][:], sqb[:, 0:512], True, True, sem=PE)
                b.wait("dve", PE, b.c[PE])
                dve(lambda e: e.tensor_copy(sm["row"][0:1, 2048:2560], ps[3][0:1, 0:512]),
                    sem=DV)
                dve(lambda e: e.reduce_max(sm["tiny"][0:1, 4:5],
                                           sm["row"][0:1, 2048:2560], axis=AX.X))
                dve(lambda e, q0=q0: e.reduce_max(
                    sm["tiny"][0:1, 5:6], sm["row"][0:1, q0:q0 + 1536], axis=AX.X),
                    sem=DV)
                b.wait("act", DV, b.c[DV])
                act(lambda e: e.activation(sm["tiny"][0:1, 6:7], sm["tiny"][0:1, 4:5],
                                           AF.Identity, scale=sm["tiny"][0:1, 5:6]),
                    sem=AC)
                b.wait("act", AC, b.c[AC])
                act(lambda e: e.activation(sm["tiny"][0:1, 7:8], sm["tiny"][0:1, 6:7],
                                           AF.Ln), sem=AC)
                b.wait("act", AC, b.c[AC])
                act(lambda e: e.activation(sm["tiny"][0:1, 9:10], sm["tiny"][0:1, 7:8],
                                           AF.Exp, scale=0.5), sem=AC)
                b.wait("dve", AC, b.c[AC])
                dve(lambda e: e.tensor_scalar_mul(sm["tiny"][0:1, 10:11],
                                                  sm["tiny"][0:1, 9:10], -ASCALE),
                    sem=DV)
                b.wait("pe", DV, b.c[DV])
                mm(ps[3][:, 0:1], sm["onesrow32"][:], sm["tiny"][0:1, 10:11], True, True,
                   sem=PE)
                b.wait("dve", PE, b.c[PE])
                dve(lambda e: e.tensor_copy(sm["stat"][:, 3:4], ps[3][:, 0:1]), sem=DV)
                b.wait("act", DV, b.c[DV])
                act(lambda e: e.activation(sm["stat"][:, 2:3], sm["stat"][:, 3:4],
                                           AF.Identity, bias=sm["hbcol"][:]), sem=AC)
                b.wait("act", AC, b.c[AC])

                def kT_src(j, qb=qb, h=h):
                    kk0 = 512 * qb + 128 * j
                    if kk0 < 1024:
                        return kh[:, kk0:kk0 + 128]
                    return kres[:, h, :][:, kk0 - 1024:kk0 - 896]

                def v_src(j, qb=qb, h=h):
                    kk0 = 512 * qb + 128 * j
                    if kk0 < 1024:
                        return vh[:, kk0 // 128, :]
                    return vres[:, (kk0 - 1024) // 128, 128 * h:128 * (h + 1)]

                bank_free = [b.c[AC]] * 4
                bi = 0
                i0 = 0
                while i0 < len(ppos):
                    used, grp = 0, []
                    while i0 < len(ppos) and used + ppos[i0][2] <= 512:
                        grp.append(ppos[i0])
                        used += ppos[i0][2]
                        i0 += 1
                    bank = ps[bi]
                    b.wait("pe", AC, bank_free[bi])
                    ro = 0
                    for gi, (j, c0, n, msk, po) in enumerate(grp):
                        mm(bank[:, ro:ro + n], kT_src(j),
                           qres[:, h, :][:, q0 + c0:q0 + c0 + n], True, True,
                           sem=PE if gi == len(grp) - 1 else None)
                        ro += n
                    b.wait("act", PE, b.c[PE])
                    ro = 0
                    for (j, c0, n, msk, po) in grp:
                        halo = (512 * qb + 128 * j) < 1024
                        bias = sm["stat"][:, 2:3] if halo else sm["stat"][:, 3:4]
                        act(lambda e, ro=ro, n=n, po=po, bias=bias, bank=bank:
                            e.activation(ptil[:, po:po + n], bank[:, ro:ro + n], AF.Exp,
                                         scale=ASCALE, bias=bias), sem=AC)
                        ro += n
                    bank_free[bi] = b.c[AC]
                    bi = (bi + 1) % 4
                av = b.c[AC]
                b.wait("dve", AC, av)
                for (j, c0, n, msk, po) in ppos:
                    if msk is None:
                        continue
                    m = sm["mlo"] if msk == "lo" else sm["mhi"]
                    dve(lambda e, po=po, m=m: e.tensor_mul(
                        ptil[:, po:po + 128], ptil[:, po:po + 128], m[:]), sem=DV)
                dvv = b.c[DV]
                b.wait("pe", DV, dvv)
                b.wait("pe", AC, av)
                for pi, (j, c0, n, msk, po) in enumerate(ppos):
                    mm(ps[4][:, c0:c0 + n], v_src(j), ptil[:, po:po + n],
                       pi == 0, pi == len(ppos) - 1)
                for pi, (j, c0, n, msk, po) in enumerate(ppos):
                    mm(ps[5][0:1, c0:c0 + n], sm["onescol"][:], ptil[:, po:po + n],
                       pi == 0, pi == len(ppos) - 1,
                       sem=PE if pi == len(ppos) - 1 else None)
                b.wait("dve", PE, b.c[PE])
                dve(lambda e: e.tensor_scalar_add(sm["row"][0:1, 2048:2560],
                                                  ps[5][0:1, 0:512], 1e-38))
                dve(lambda e: e.reciprocal(sm["row"][0:1, 2048:2560],
                                           sm["row"][0:1, 2048:2560]), sem=DV)
                b.wait("pe", DV, b.c[DV])
                mm(ps[5][:, 0:512], sm["onesrow32"][:], sm["row"][0:1, 2048:2560],
                   True, True, sem=PE)
                b.wait("dve", PE, b.c[PE])
                dve(lambda e: e.tensor_copy(sm["rbc"][:], ps[5][:, 0:512]), sem=DV)
                dve(lambda e, qb=qb: e.tensor_mul(
                    oTh[:, 512 * qb:512 * (qb + 1)], ps[4][:, 0:512], sm["rbc"][:]),
                    sem=DV)
                b.wait("pe", DV, b.c[DV])
            pe_prev_head = b.c[PE]
            dv_prev_head = b.c[DV]
            b.wait("sync", DV, b.c[DV])
            b.dma("sync", "sOW", lambda e, h=h: e.dma_start(
                out=S["oT_s"].rearrange("(ft p) t -> p ft t", p=128)[:, h, :], in_=oTh))

    def phase_c():
        oTst = arv(0, 16)
        attn_sb = arv(16, 4, 2048)
        x2T = arv(32, 16)
        WB0, WB1 = 48, 64
        fst = arv(80, 48)
        scr = ar(128, 4)
        lrw_sb = arv(132, 16, 12)
        b.dma("pool", "sLR", lambda e: e.dma_start(
            out=lrw_sb[:, :, :], in_=I["lrwT"].rearrange("(ft p) c -> p ft c", p=128)))
        free_w = [b.c[PE]] * 2
        pe_prev = b.c[PE]
        wsem = ["sW0", "sW1"]
        bfree = [0] * 6
        afree = [0] * 6

        for tb in range(NTB):
            b.wait("sync", PE, pe_prev)
            b.dma("sync", "sOT", lambda e, tb=tb: e.dma_start(
                out=oTst[:, :, :], in_=S["oT_s"].rearrange("(ft p) t -> p ft t", p=128)
                [:, :, 512 * tb:512 * (tb + 1)]))
            b.dwait("pe", "sOT")
            b.dwait("dve", "sAW")
            for oc in range(4):
                b.wait("pool", PE, free_w[oc % 2])
                load_w((WB0, WB1)[oc % 2], I["woT"], H, 512 * oc, 512, wsem[oc % 2])
                b.dwait("pe", wsem[oc % 2])
                wv = arv((WB0, WB1)[oc % 2], 16)
                for tt in range(4):
                    bi = (4 * oc + tt) % 3
                    bank = ps[bi]
                    b.wait("pe", DV, bfree[bi])
                    b.wait("pe", AC, afree[bi])
                    for ft in range(16):
                        mm(bank[:], oTst[:, ft, 128 * tt:128 * (tt + 1)], wv[:, ft, :],
                           ft == 0, ft == 15, sem=PE if ft == 15 else None)
                    b.wait("dve", PE, b.c[PE])
                    dve(lambda e, tt=tt, oc=oc, p=bank: e.tensor_copy(
                        attn_sb[:, tt, 512 * oc:512 * (oc + 1)], p[:]), sem=DV)
                    bfree[bi] = b.c[DV]
                free_w[oc % 2] = b.c[PE]
            b.wait("sync", DV, b.c[DV])
            b.dma("sync", "sAW", lambda e, tb=tb: e.dma_start(
                out=S["att_s"].rearrange("(tt p) f -> p tt f", p=128)
                [:, 4 * tb:4 * tb + 4, :], in_=attn_sb[:, :, :]))
            dve(lambda e: e.tensor_single_scalar(sm["stat"][:, 0:4], sm["qnw"][:, 0:4],
                                                 0.0, ALU.mult), sem=DV)
            for tt in range(4):
                dve(lambda e, tt=tt: e.scalar_tensor_tensor(
                    scr[:, 0:2048], attn_sb[:, tt, :], 1.0, attn_sb[:, tt, :],
                    ALU.mult, ALU.mult, accum_out=sm["stat"][:, tt:tt + 1]), sem=DV)
            b.wait("act", DV, b.c[DV])
            act(lambda e: e.activation(sm["stat"][:, 4:8], sm["stat"][:, 0:4], AF.Ln,
                                       scale=1.0 / H, bias=sm["cb"][:, 0:1]), sem=AC)
            b.wait("act", AC, b.c[AC])
            act(lambda e: e.activation(sm["stat"][:, 4:8], sm["stat"][:, 4:8], AF.Exp,
                                       scale=-0.5), sem=AC)
            b.wait("dve", AC, b.c[AC])
            for tt in range(4):
                dve(lambda e, tt=tt: e.tensor_scalar_mul(
                    scr[:, 0:2048], attn_sb[:, tt, :], sm["stat"][:, 4 + tt:5 + tt]),
                    sem=DV)
                b.wait("pe", DV, b.c[DV])
                for ft in range(16):
                    tr(pslot(ft), scr[:, 128 * ft:128 * (ft + 1)], sem=PE)
                    b.wait("dve", PE, b.c[PE])
                    dve(lambda e, ft=ft, tt=tt: e.tensor_scalar_mul(
                        x2T[:, ft, 128 * tt:128 * (tt + 1)], pslot(ft),
                        sm["mlpnw"][:, ft:ft + 1]), sem=DV)
                    b.wait("pe", DV, b.c[DV] - 1)
                b.wait("pe", DV, b.c[DV])

            b.dwait("act", "sFW")
            for wc in range(12):
                b.wait("pool", PE, free_w[wc % 2])
                load_w((WB0, WB1)[wc % 2], I["wmlpT"], H, 512 * wc, 512, wsem[wc % 2])
                b.dwait("pe", wsem[wc % 2])
                wv = arv((WB0, WB1)[wc % 2], 16)
                for osub in range(4):
                    bi = (4 * wc + osub) % 3
                    bank = ps[bi]
                    b.wait("pe", AC, afree[bi])
                    b.wait("pe", DV, bfree[bi])
                    for ft in range(16):
                        mm(bank[:], wv[:, ft, 128 * osub:128 * (osub + 1)], x2T[:, ft, :],
                           ft == 0, ft == 15, sem=PE if ft == 15 else None)
                    b.wait("act", PE, b.c[PE])
                    act(lambda e, i=4 * wc + osub, p=bank: e.activation(
                        fst[:, i, :], p[:], AF.Silu), sem=AC)
                    afree[bi] = b.c[AC]
                free_w[wc % 2] = b.c[PE]
            ac_silu = b.c[AC]
            b.wait("dve", AC, ac_silu)

            for base in (0, 16):
                for hh in range(NFH):
                    for k in range(4):
                        i = base + 4 * hh + k
                        if k > 0:
                            b.wait("dve", PE, b.c[PE])
                        dve(lambda e, i=i: e.tensor_mul(scr[:, 0:512], fst[:, i, :],
                                                        fst[:, i, :]), sem=DV)
                        b.wait("pe", DV, b.c[DV])
                        mm(ps[3][0:1, 0:512], sm["onescol"][:], scr[:, 0:512],
                           k == 0, k == 3, sem=PE)
                    b.wait("dve", PE, b.c[PE])
                    dve(lambda e: e.tensor_copy(sm["row"][0:1, 2048:2560],
                                                ps[3][0:1, 0:512]), sem=DV)
                    b.wait("act", DV, b.c[DV])
                    act(lambda e: e.activation(sm["row"][0:1, 2048:2560],
                                               sm["row"][0:1, 2048:2560], AF.Ln), sem=AC)
                    b.wait("act", AC, b.c[AC])
                    act(lambda e: e.activation(sm["row"][0:1, 2048:2560],
                                               sm["row"][0:1, 2048:2560], AF.Exp,
                                               scale=0.5), sem=AC)
                    b.wait("dve", AC, b.c[AC])
                    dve(lambda e: e.tensor_scalar_max(sm["row"][0:1, 2048:2560],
                                                      sm["row"][0:1, 2048:2560], 1e-6))
                    dve(lambda e: e.reciprocal(sm["row"][0:1, 2048:2560],
                                               sm["row"][0:1, 2048:2560]), sem=DV)
                    bcast_row(512, ps[3], sm["rbc"])
                    for k in range(4):
                        i = base + 4 * hh + k
                        dve(lambda e, i=i: e.tensor_mul(fst[:, i, :], fst[:, i, :],
                                                        sm["rbc"][:, 0:512]), sem=DV)
            b.wait("sync", DV, b.c[DV])
            b.wait("sync", AC, ac_silu)
            for (dst_s, lo) in ((S["fqT_s"], 0), (S["fkT_s"], 16), (S["fvT_s"], 32)):
                b.dma("sync", "sFW", lambda e, d=dst_s, lo=lo, tb=tb: e.dma_start(
                    out=d.rearrange("(ft p) t -> p ft t", p=128)
                    [:, :, 512 * tb:512 * (tb + 1)], in_=fst[:, lo:lo + 16, :]))

            b.dwait("pe", "sLR")
            for tt in range(4):
                gt = 4 * tb + tt
                b.wait("pe", DV, b.c[DV])
                for ft in range(16):
                    mm(ps[4][:, 0:12], x2T[:, ft, 128 * tt:128 * (tt + 1)],
                       lrw_sb[:, ft, :], ft == 0, ft == 15, sem=PE if ft == 15 else None)
                b.wait("dve", PE, b.c[PE])
                dve(lambda e: e.tensor_add(sm["w12"][:, 0:12], ps[4][:, 0:12],
                                           sm["lrb_bc"][:]), sem=DV)
                b.wait("act", DV, b.c[DV])
                act(lambda e: e.activation(sm["s12"][:, 0:12], sm["w12"][:, 0:12],
                                           AF.Abs), sem=AC)
                b.wait("act", AC, b.c[AC])
                act(lambda e: e.activation(sm["s12"][:, 0:12], sm["s12"][:, 0:12],
                                           AF.Exp, scale=-1.0), sem=AC)
                b.wait("act", AC, b.c[AC])
                act(lambda e: e.activation(sm["s12"][:, 0:12], sm["s12"][:, 0:12],
                                           AF.Ln, bias=sm["cb"][:, 1:2]), sem=AC)
                b.wait("act", AC, b.c[AC])
                act(lambda e: e.activation(sm["w12"][:, 0:12], sm["w12"][:, 0:12],
                                           AF.Relu), sem=AC)
                b.wait("dve", AC, b.c[AC])
                dve(lambda e, gt=gt: e.tensor_add(
                    sm["lr_tok"][:, 12 * gt:12 * (gt + 1)], sm["w12"][:, 0:12],
                    sm["s12"][:, 0:12]), sem=DV)
            pe_prev = b.c[PE]

    def phase_d():
        mats = ("st0", "st2", "st1", "w1o")
        stu = {}
        u0 = 0
        for hh in range(NFH):
            for m in mats:
                stu[(hh, m)] = u0
                u0 += 4
        ST = lambda hh, m: arv(stu[(hh, m)], 4)
        fk_c = arv(64, 16)
        fvh = arv(80, 4)
        fqh = arv(84, 4)
        KC0, KC2, VC1, HID, DGP, DHB, SGK, GKS, HBM = [arv(88 + 4 * i, 4)
                                                       for i in range(9)]
        HQ = arv(124, 4)
        OCS = arv(128, 4)
        scr = ar(132, 4)

        def load_base():
            for hh in range(NFH):
                for m, src in zip(mats, (I["w0T"], I["w2T"], I["w1T"], I["w1om"])):
                    b.dma("pool", "sST", lambda e, hh=hh, m=m, src=src: e.dma_start(
                        out=ST(hh, m)[:, :, :],
                        in_=src[hh].rearrange("(dj p) e -> p dj e", p=128)))

        def kpath(hh, c, upd):
            fkh_t = lambda dj: fk_c[:, 4 * hh + dj, :]
            fvh_t = lambda dj: fvh[:, dj, :]
            for (srcf, outs) in ((fkh_t, ((KC0, 0), (KC2, 2))), (fvh_t, ((VC1, 1),))):
                for dj in range(4):
                    for cj in range(4):
                        tr(pslot(cj), srcf(dj)[:, 128 * cj:128 * (cj + 1)],
                           sem=PE)
                        b.wait("dve", PE, b.c[PE])
                        for (dst, li) in outs:
                            lrc = sm["lr_tok"][:, 12 * (4 * c + cj) + 4 * li + hh:
                                               12 * (4 * c + cj) + 4 * li + hh + 1]
                            dve(lambda e, dst=dst, cj=cj, dj=dj, lrc=lrc:
                                e.tensor_scalar_mul(
                                    dst[:, cj, 128 * dj:128 * (dj + 1)],
                                    pslot(cj), lrc), sem=DV)
                        b.wait("pe", DV, b.c[DV] - len(outs))
                b.wait("pe", DV, b.c[DV])
            for cj in range(4):
                for dj in range(4):
                    mm(ps[0][:], fkh_t(dj)[:, 128 * cj:128 * (cj + 1)],
                       ST(hh, "st0")[:, dj, :], dj == 0, dj == 3)
                for dj in range(4):
                    mm(ps[1][:], fkh_t(dj)[:, 128 * cj:128 * (cj + 1)],
                       ST(hh, "st2")[:, dj, :], dj == 0, dj == 3)
                for oj in range(4):
                    mm(ps[2][:], fvh_t(oj)[:, 128 * cj:128 * (cj + 1)],
                       ST(hh, "w1o")[:, oj, :], oj == 0, oj == 3,
                       sem=PE if oj == 3 else None)
                pv = b.c[PE]
                b.wait("act", PE, pv)
                act(lambda e, cj=cj: e.activation(SGK[:, cj, :], ps[0][:], AF.Sigmoid),
                    sem=AC)
                b.wait("dve", AC, b.c[AC])
                b.wait("dve", PE, pv)
                dve(lambda e, cj=cj: e.tensor_mul(GKS[:, cj, :], ps[0][:], SGK[:, cj, :]))
                dve(lambda e, cj=cj: e.tensor_sub(scr[:, 0:512], ps[0][:], GKS[:, cj, :]))
                dve(lambda e, cj=cj: e.tensor_copy(HBM[:, cj, :], ps[1][:]))
                dve(lambda e, cj=cj: e.tensor_mul(HID[:, cj, :], GKS[:, cj, :],
                                                  HBM[:, cj, :]))
                dve(lambda e, cj=cj: e.tensor_mul(scr[:, 512:1024], ps[2][:],
                                                  HBM[:, cj, :]))
                dve(lambda e, cj=cj: e.tensor_mul(DHB[:, cj, :], ps[2][:], GKS[:, cj, :]))
                dve(lambda e, cj=cj: e.scalar_tensor_tensor(
                    scr[:, 0:512], scr[:, 0:512], 1.0, SGK[:, cj, :], ALU.add, ALU.mult))
                dve(lambda e, cj=cj: e.tensor_mul(DGP[:, cj, :], scr[:, 0:512],
                                                  scr[:, 512:1024]), sem=DV)
                b.wait("pe", DV, b.c[DV])
            if not upd:
                return
            for (lhs, rhs, mat) in ((KC0, DGP, "st0"), (KC2, DHB, "st2"),
                                    (VC1, HID, "w1o"), (HID, VC1, "st1")):
                for dj in range(4):
                    bank = ps[3 + dj % 2]
                    for cj in range(4):
                        mm(bank[:], lhs[:, cj, 128 * dj:128 * (dj + 1)], rhs[:, cj, :],
                           cj == 0, cj == 3, sem=PE if cj == 3 else None)
                    b.wait("dve", PE, b.c[PE])
                    dve(lambda e, mat=mat, dj=dj, hh=hh, p=bank: e.tensor_add(
                        ST(hh, mat)[:, dj, :], ST(hh, mat)[:, dj, :], p[:]), sem=DV)
                    if dj >= 1:
                        b.wait("pe", DV, b.c[DV] - 1)
                b.wait("pe", DV, b.c[DV])

        def qpath(hh, c):
            for ej in range(4):
                for dj in range(4):
                    mm(ps[0][:], ST(hh, "st0")[:, dj, 128 * ej:128 * (ej + 1)],
                       fqh[:, dj, :], dj == 0, dj == 3)
                for dj in range(4):
                    mm(ps[1][:], ST(hh, "st2")[:, dj, 128 * ej:128 * (ej + 1)],
                       fqh[:, dj, :], dj == 0, dj == 3, sem=PE if dj == 3 else None)
                pv = b.c[PE]
                b.wait("act", PE, pv)
                act(lambda e: e.activation(scr[:, 0:512], ps[0][:], AF.Sigmoid), sem=AC)
                b.wait("dve", AC, b.c[AC])
                b.wait("dve", PE, pv)
                dve(lambda e: e.tensor_mul(scr[:, 512:1024], ps[0][:], scr[:, 0:512]))
                dve(lambda e, ej=ej: e.tensor_mul(HQ[:, ej, :], scr[:, 512:1024],
                                                  ps[1][:]), sem=DV)
                b.wait("pe", DV, b.c[DV])
            b.dwait("dve", "sOC")
            b.dwait("dve", "sRS")
            for oj in range(4):
                bank = ps[2 + oj % 2]
                for ej in range(4):
                    mm(bank[:], ST(hh, "st1")[:, ej, 128 * oj:128 * (oj + 1)],
                       HQ[:, ej, :], ej == 0, ej == 3, sem=PE if ej == 3 else None)
                b.wait("dve", PE, b.c[PE])
                dve(lambda e, oj=oj, p=bank: e.tensor_copy(OCS[:, oj, :], p[:]), sem=DV)
                if oj >= 1:
                    b.wait("pe", DV, b.c[DV] - 1)
            b.wait("sync", DV, b.c[DV])
            b.dma("sync", "sOC", lambda e, hh=hh, c=c: e.dma_start(
                out=S["oc_s"].rearrange("(q p) t -> p q t", p=128)
                [:, 4 * hh:4 * hh + 4, 512 * c:512 * (c + 1)], in_=OCS[:, :, :]))

        def stream_chunk(c):
            b.dma("sync", "sFK", lambda e, c=c: e.dma_start(
                out=fk_c[:, :, :], in_=S["fkT_s"].rearrange("(ft p) t -> p ft t", p=128)
                [:, :, 512 * c:512 * (c + 1)]))

        def stream_head(c, hh, src, dst, sem):
            b.dma("sync", sem, lambda e, c=c, hh=hh, src=src, dst=dst: e.dma_start(
                out=dst[:, :, :], in_=src.rearrange("(ft p) t -> p ft t", p=128)
                [:, 4 * hh:4 * hh + 4, 512 * c:512 * (c + 1)]))

        load_base()
        b.dwait("pe", "sST")
        b.dwait("dve", "sST")
        for c in range(2):
            b.wait("sync", PE, b.c[PE])
            b.wait("sync", DV, b.c[DV])
            stream_chunk(c)
            for hh in range(NFH):
                b.wait("sync", PE, b.c[PE])
                stream_head(c, hh, S["fvT_s"], fvh, "sFV")
                b.dwait("pe", "sFK")
                b.dwait("dve", "sFK")
                b.dwait("pe", "sFV")
                b.dwait("dve", "sFV")
                kpath(hh, c, upd=True)
        rsv = S["rsend"].rearrange("(i p) e -> p i e", p=128)
        for hh in range(NFH):
            for mi, m in enumerate(mats):
                b.dwait("dve", "sRS")
                for dj in range(4):
                    dve(lambda e, hh=hh, m=m, dj=dj: e.tensor_scalar_mul(
                        OCS[:, dj, :], ST(hh, m)[:, dj, :], sm["gcols"][:, 2:3]), sem=DV)
                b.wait("sync", DV, b.c[DV])
                idx = (hh * 4 + mi) * 4
                b.dma("sync", "sRS", lambda e, idx=idx: e.dma_start(
                    out=rsv[:, idx:idx + 4, :], in_=OCS[:, :, :]))
        b.dwait("pool", "sRS")
        b.op("pool", lambda e: e.collective_compute(
            "AllReduce", ALU.add, replica_groups=GROUPS,
            ins=[S["rsend"].opt()], outs=[S["rgat"].opt()]), sem=CC)

        load_base()
        b.dwait("dve", "sST")
        b.wait("sync", CC, b.c[CC])
        rgv = S["rgat"].rearrange("(i p) e -> p i e", p=128)
        for hh in range(NFH):
            for mi, m in enumerate(mats):
                idx = (hh * 4 + mi) * 4
                b.wait("sync", DV, b.c[DV])
                b.dma("sync", "sRL", lambda e, idx=idx: e.dma_start(
                    out=fqh[:, :, :], in_=rgv[:, idx:idx + 4, :]))
                b.dwait("dve", "sRL")
                for dj in range(4):
                    dve(lambda e, hh=hh, m=m, dj=dj: e.tensor_scalar_mul(
                        ST(hh, m)[:, dj, :], ST(hh, m)[:, dj, :], sm["gcols"][:, 0:1]))
                    dve(lambda e, hh=hh, m=m, dj=dj: e.scalar_tensor_tensor(
                        ST(hh, m)[:, dj, :], fqh[:, dj, :], sm["gcols"][:, 1:2],
                        ST(hh, m)[:, dj, :], ALU.mult, ALU.add), sem=DV)
        b.wait("pe", DV, b.c[DV])

        for c in range(2):
            b.wait("sync", PE, b.c[PE])
            b.wait("sync", DV, b.c[DV])
            stream_chunk(c)
            for hh in range(NFH):
                b.wait("sync", PE, b.c[PE])
                b.wait("sync", DV, b.c[DV])
                stream_head(c, hh, S["fqT_s"], fqh, "sFQ")
                stream_head(c, hh, S["fvT_s"], fvh, "sFV")
                b.dwait("pe", "sFK")
                b.dwait("dve", "sFK")
                b.dwait("pe", "sFQ")
                b.dwait("pe", "sFV")
                b.dwait("dve", "sFV")
                qpath(hh, c)
                kpath(hh, c, upd=(c == 0))

    def phase_e():
        onst = arv(0, 16)
        ln2T = arv(0, 16)
        hdn = arv(16, 44)
        WB0, WB1 = 60, 76
        WBD = 92
        scr = ar(136, 4)
        att4 = arv(16, 4, 2048)
        xn = [A32[:, 2048 * i:2048 * (i + 1)] for i in range(4)]
        xsem = ["sX0", "sX1", "sX2", "sX3"]
        outt = A32[:, 8192:8704]
        xv = I["x_own"].rearrange("(tt p) f -> p tt f", p=128)
        av = S["att_s"].rearrange("(tt p) f -> p tt f", p=128)
        xnv = S["xnew_s"].rearrange("(tt p) f -> p tt f", p=128)
        yv = y_own.rearrange("(tt p) f -> p tt f", p=128)
        free_w = [b.c[PE]] * 2
        wsem = ["sW0", "sW1"]
        pe_prev = b.c[PE]
        bfree = [0] * 6
        afree = [0] * 6

        for tb in range(NTB):
            b.wait("sync", PE, pe_prev)
            b.wait("sync", DV, b.c[DV])
            b.dma("sync", "sON", lambda e, tb=tb: e.dma_start(
                out=onst[:, :, :], in_=S["oc_s"].rearrange("(q p) t -> p q t", p=128)
                [:, :, 512 * tb:512 * (tb + 1)]))
            b.dwait("dve", "sON")
            b.dwait("pe", "sON")
            for hh in range(NFH):
                for k in range(4):
                    i = 4 * hh + k
                    if k > 0:
                        b.wait("dve", PE, b.c[PE])
                    dve(lambda e, i=i: e.tensor_mul(scr[:, 0:512], onst[:, i, :],
                                                    onst[:, i, :]), sem=DV)
                    b.wait("pe", DV, b.c[DV])
                    mm(ps[5][0:1, 0:512], sm["onescol"][:], scr[:, 0:512],
                       k == 0, k == 3, sem=PE)
                b.wait("dve", PE, b.c[PE])
                dve(lambda e: e.tensor_copy(sm["row"][0:1, 2048:2560],
                                            ps[5][0:1, 0:512]), sem=DV)
                b.wait("act", DV, b.c[DV])
                rsqrt_row(512, 1.0 / FHD)
                bcast_row(512, ps[5], sm["rbc"])
                for k in range(4):
                    i = 4 * hh + k
                    dve(lambda e, i=i, k=k: e.scalar_tensor_tensor(
                        onst[:, i, :], onst[:, i, :], sm["tttnw"][:, k:k + 1],
                        sm["rbc"][:, 0:512], ALU.mult, ALU.mult), sem=DV)
            b.wait("pe", DV, b.c[DV])
            b.dwait("sync", "sXW")
            for tt in range(4):
                gt = 4 * tb + tt
                b.dma("sync", xsem[tt], lambda e, gt=gt, d=xn[tt]: e.dma_start(
                    out=d, in_=xv[:, gt, :]))
            b.wait("sync", DV, b.c[DV])
            b.dma("sync", "sAT", lambda e, tb=tb: e.dma_start(
                out=att4[:, :, :], in_=av[:, 4 * tb:4 * tb + 4, :]))
            for tt in range(4):
                b.dwait("dve", xsem[tt])
            b.dwait("dve", "sAT")
            for mc in range(4):
                b.wait("pool", PE, free_w[mc % 2])
                load_w((WB0, WB1)[mc % 2], I["wmoT"], H, 512 * mc, 512, wsem[mc % 2])
                b.dwait("pe", wsem[mc % 2])
                wv = arv((WB0, WB1)[mc % 2], 16)
                cs = slice(512 * mc, 512 * (mc + 1))
                for tt in range(4):
                    bi = (4 * mc + tt) % 3
                    bank = ps[bi]
                    b.wait("pe", DV, bfree[bi])
                    for ft in range(16):
                        mm(bank[:], onst[:, ft, 128 * tt:128 * (tt + 1)], wv[:, ft, :],
                           ft == 0, ft == 15, sem=PE if ft == 15 else None)
                    b.wait("dve", PE, b.c[PE])
                    dve(lambda e, cs=cs, d=xn[tt], p=bank: e.tensor_add(
                        d[:, cs], d[:, cs], p[:]))
                    dve(lambda e, cs=cs, d=xn[tt], tt=tt: e.tensor_add(
                        d[:, cs], d[:, cs], att4[:, tt, cs]), sem=DV)
                    bfree[bi] = b.c[DV]
                free_w[mc % 2] = b.c[PE]
            for tt in range(4):
                gt = 4 * tb + tt
                dve(lambda e, gt=gt: e.tensor_single_scalar(
                    sm["rscol"][:, gt:gt + 1], sm["cb"][:, 0:1], 0.0, ALU.mult))
                dve(lambda e, d=xn[tt], gt=gt: e.reduce_sum(
                    sm["mucol"][:, gt:gt + 1], d, axis=AX.X), sem=DV)
                dve(lambda e, d=xn[tt], gt=gt: e.scalar_tensor_tensor(
                    scr[:, 0:2048], d, 1.0, d, ALU.mult, ALU.mult,
                    accum_out=sm["rscol"][:, gt:gt + 1]), sem=DV)
                b.wait("sync", DV, b.c[DV])
                b.dma("sync", "sXW", lambda e, gt=gt, d=xn[tt]: e.dma_start(
                    out=xnv[:, gt, :], in_=d))
            pe_prev = b.c[PE]
        b.wait("act", DV, b.c[DV])
        act(lambda e: e.activation(sm["rscol"][:, 8:16], sm["mucol"][:, 0:8],
                                   AF.Square, scale=1.0 / H), sem=AC)
        b.wait("act", AC, b.c[AC])
        act(lambda e: e.activation(sm["mucol"][:, 8:16], sm["mucol"][:, 0:8],
                                   AF.Identity, scale=1.0 / H), sem=AC)
        b.wait("dve", AC, b.c[AC])
        dve(lambda e: e.scalar_tensor_tensor(
            sm["rscol"][:, 0:8], sm["rscol"][:, 0:8], 1.0 / H, sm["rscol"][:, 8:16],
            ALU.mult, ALU.subtract), sem=DV)
        b.wait("act", DV, b.c[DV])
        act(lambda e: e.activation(sm["rscol"][:, 0:8], sm["rscol"][:, 0:8], AF.Ln,
                                   bias=sm["cb"][:, 2:3]), sem=AC)
        b.wait("act", AC, b.c[AC])
        act(lambda e: e.activation(sm["rscol"][:, 0:8], sm["rscol"][:, 0:8], AF.Exp,
                                   scale=-0.5), sem=AC)
        act(lambda e: e.copy(sm["mucol"][:, 0:8], sm["mucol"][:, 8:16]), sem=AC)
        b.wait("dve", AC, b.c[AC])

        for tb in range(NTB):
            for tt in range(4):
                gt = 4 * tb + tt
                b.dwait("sync", "sYW")
                b.dma("sync", xsem[tt], lambda e, gt=gt, d=xn[tt]: e.dma_start(
                    out=d, in_=xnv[:, gt, :]))
                b.dwait("dve", xsem[tt])
                b.wait("dve", PE, b.c[PE])
                dve(lambda e, d=xn[tt], gt=gt: e.tensor_scalar(
                    scr[:, 0:2048], d, sm["mucol"][:, gt:gt + 1],
                    sm["rscol"][:, gt:gt + 1], ALU.subtract, ALU.mult), sem=DV)
                b.wait("pe", DV, b.c[DV])
                for ft in range(16):
                    tr(pslot(ft), scr[:, 128 * ft:128 * (ft + 1)], sem=PE)
                    b.wait("dve", PE, b.c[PE])
                    dve(lambda e, ft=ft, tt=tt: e.tensor_copy(
                        ln2T[:, ft, 128 * tt:128 * (tt + 1)], pslot(ft)), sem=DV)
                    b.wait("pe", DV, b.c[DV] - 1)
                b.wait("pe", DV, b.c[DV])
            for wc in range(11):
                b.wait("pool", PE, free_w[0])
                load_w(WB0, I["gT"], H, 512 * wc, 512, "sW0")
                b.wait("pool", PE, free_w[1])
                load_w(WB1, I["uT"], H, 512 * wc, 512, "sW1")
                b.dwait("pe", "sW0")
                b.dwait("pe", "sW1")
                gvw, uvw = arv(WB0, 16), arv(WB1, 16)
                for isub in range(4):
                    icol = 4 * wc + isub
                    gb = isub % 2
                    sl = sm["rbc2"][:, 512 * gb:512 * (gb + 1)]
                    b.wait("pe", AC, afree[gb])
                    b.wait("pe", DV, bfree[gb])
                    for ft in range(16):
                        mm(ps[gb][:], gvw[:, ft, 128 * isub:128 * (isub + 1)],
                           ln2T[:, ft, :], ft == 0, ft == 15, sem=PE if ft == 15 else None)
                    b.wait("act", PE, b.c[PE])
                    b.wait("act", DV, bfree[gb])
                    act(lambda e, p=ps[gb], icol=icol, sl=sl: e.activation(
                        sl, p[:], AF.Silu, scale=sm["gate_s"][:, icol:icol + 1]), sem=AC)
                    acv = b.c[AC]
                    afree[gb] = acv
                    b.wait("pe", DV, bfree[2 + gb])
                    for ft in range(16):
                        mm(ps[2 + gb][:], uvw[:, ft, 128 * isub:128 * (isub + 1)],
                           ln2T[:, ft, :], ft == 0, ft == 15, sem=PE if ft == 15 else None)
                    b.wait("dve", PE, b.c[PE])
                    b.wait("dve", AC, acv)
                    dve(lambda e, icol=icol, p=ps[2 + gb], sl=sl: e.scalar_tensor_tensor(
                        hdn[:, icol, :], p[:], sm["up_s"][:, icol:icol + 1],
                        sl, ALU.mult, ALU.mult), sem=DV)
                    bfree[gb] = b.c[DV]
                    bfree[2 + gb] = b.c[DV]
                free_w[0] = free_w[1] = b.c[PE]
            b.wait("pe", DV, b.c[DV])
            for mc in range(4):
                b.wait("pool", PE, free_w[0])
                load_w(WBD, I["dT"], INTER, 512 * mc, 512, "sWD", split=4)
                b.dwait("pe", "sWD")
                dwv = arv(WBD, 44)
                for tt in range(4):
                    bi = tt % 2
                    bank = ps[bi]
                    b.wait("pe", DV, bfree[bi])
                    b.wait("pe", AC, afree[bi])
                    for ki in range(44):
                        mm(bank[:], hdn[:, ki, 128 * tt:128 * (tt + 1)], dwv[:, ki, :],
                           ki == 0, ki == 43, sem=PE if ki == 43 else None)
                    b.wait("dve", PE, b.c[PE])
                    cs = slice(512 * mc, 512 * (mc + 1))
                    dve(lambda e, cs=cs, p=bank: e.tensor_mul(
                        outt[:, 0:512], p[:], sm["ds_bc"][:, cs]))
                    dve(lambda e, cs=cs, tt=tt: e.tensor_add(
                        xn[tt][:, cs], xn[tt][:, cs], outt[:, 0:512]), sem=DV)
                    bfree[bi] = b.c[DV]
                free_w[0] = b.c[PE]
            b.wait("sync", DV, b.c[DV])
            for tt in range(4):
                gt = 4 * tb + tt
                b.dma("sync", "sYW", lambda e, gt=gt, tt=tt: e.dma_start(
                    out=yv[:, gt, :], in_=xn[tt]))
        b.dwait("sync", "sYW")
        if DEBUG:
            b.dma("sync", "sYW", lambda e: e.dma_start(out=DBG["dbg_cols"],
                                                       in_=sm["dbgc"][:]))
            for nm, srcs in (("dbg_ks", "ksend"), ("dbg_oT", "oT_s"),
                             ("dbg_att", "att_s"), ("dbg_fq", "fqT_s"),
                             ("dbg_fk", "fkT_s"), ("dbg_fv", "fvT_s"),
                             ("dbg_oc", "oc_s"), ("dbg_xn", "xnew_s"),
                             ("dbg_rg", "rgat")):
                b.dma("sync", "sYW", lambda e, d=DBG[nm], s=S[srcs]: e.dma_start(
                    out=d, in_=s))
            b.dwait("sync", "sYW")

    nums = sorted(h.num for h in b.sems.values())
    lo, hi = nums[0], nums[-1]
    assert nums == list(range(lo, hi + 1)), nums
    b.raw("pool", lambda e, lo=lo, hi=hi: e.sem_clear(range(lo, hi + 1)))
    b.raw("pool", lambda e: e.sem_inc(b.sems["GO"], 1))
    b.c["GO"] = 1
    for _eng in ("pe", "dve", "act", "sync"):
        b.wait(_eng, "GO", 1)

    load_const()
    b.barrier()
    qres, kres, vres = phase_a()
    b.barrier()
    phase_b(qres, kres, vres)
    b.barrier()
    phase_c()
    b.barrier()
    phase_d()
    b.barrier()
    phase_e()



def kernel(**inp):
    x = np.ascontiguousarray(np.asarray(inp["L0_attn_hidden_states"], np.float32))
    Bn, Sq, _ = x.shape

    T = lambda a: np.ascontiguousarray(np.asarray(a, np.float32).T)
    shared = dict(
        wqkvT=T(inp["L0_attn_attn_qkv_w"]), woT=T(inp["L0_attn_attn_o_w"]),
        wmlpT=T(inp["L0_attn_mlp_qkv_w"]), lrwT=T(inp["L0_attn_lr_w"]),
        wmoT=T(inp["L0_attn_mlp_o_w"]),
        w0T=np.ascontiguousarray(np.asarray(inp["L0_attn_w0"], np.float32)
                                 .transpose(0, 2, 1)),
        w1om=np.ascontiguousarray(np.asarray(inp["L0_attn_w1"], np.float32)),
        w1T=np.ascontiguousarray(np.asarray(inp["L0_attn_w1"], np.float32)
                                 .transpose(0, 2, 1)),
        w2T=np.ascontiguousarray(np.asarray(inp["L0_attn_w2"], np.float32)
                                 .transpose(0, 2, 1)),
        gT=T(inp["L0_ffn_gate_w"]), uT=T(inp["L0_ffn_up_w"]), dT=T(inp["L0_ffn_down_w"]),
    )
    col = lambda a, n: np.ascontiguousarray(np.asarray(a, np.float32).reshape(n, 128).T)
    shared["qnw"] = col(inp["L0_attn_q_norm_w"], 16)
    shared["knw"] = col(inp["L0_attn_k_norm_w"], 16)
    shared["mlpnw"] = col(inp["L0_attn_mlp_norm_w"], 16)
    shared["tttnw"] = col(inp["L0_attn_ttt_norm_w"], 4)
    shared["gate_s"] = col(inp["L0_ffn_gate_s"], 44)
    shared["up_s"] = col(inp["L0_ffn_up_s"], 44)
    shared["down_row"] = np.asarray(inp["L0_ffn_down_s"], np.float32).reshape(1, -1)
    shared["lrb_row"] = np.asarray(inp["L0_attn_lr_b"], np.float32).reshape(1, -1)
    shared["eye_in"] = np.eye(128, dtype=np.float32)
    psw = np.zeros((128, 128), np.float32)
    for d in range(64):
        psw[d + 64, d] = -1.0
        psw[d, d + 64] = 1.0
    shared["psw_in"] = psw
    ii = np.arange(128)[:, None]
    cc = np.arange(128)[None, :]
    shared["mlo_in"] = (ii > cc).astype(np.float32)
    shared["mhi_in"] = (ii <= cc).astype(np.float32)

    invf = 1.0 / (500000.0 ** (np.arange(0, 128, 2, dtype=np.float64) / 128.0))
    fr = np.outer(np.arange(Sq, dtype=np.float64), invf)
    cosT = np.concatenate([np.cos(fr).T, np.cos(fr).T], axis=0).astype(np.float32)
    sinT = np.concatenate([np.sin(fr).T, np.sin(fr).T], axis=0).astype(np.float32)

    nc = build_program()

    in_maps = []
    for c in range(8):
        bq, p = c // 2, c % 2
        t0 = TOWN * p
        m = dict(shared)
        m["x_own"] = np.ascontiguousarray(x[bq, t0:t0 + TOWN])
        m["cosT_in"] = np.ascontiguousarray(cosT[:, t0:t0 + TOWN])
        m["sinT_in"] = np.ascontiguousarray(sinT[:, t0:t0 + TOWN])
        m["hb_in"] = np.full((128, 1), 0.0 if p == 1 else -1e30, np.float32)
        m["ga_in"] = np.array([[1.0 if p == 0 else 0.0]], np.float32)
        m["gb_in"] = np.array([[0.0 if p == 0 else 1.0]], np.float32)
        m["gout_in"] = np.array([[1.0 if p == 0 else 0.0]], np.float32)
        in_maps.append(m)

    res = run_bass_kernel_spmd(nc, in_maps, core_ids=list(range(8)))
    LAST_RESULTS[0] = res
    out = np.empty((Bn, Sq, H), np.float32)
    for c in range(8):
        bq, p = c // 2, c % 2
        out[bq, TOWN * p:TOWN * (p + 1)] = np.asarray(res.results[c]["y_own"])
    return out


# version 08
# speedup vs baseline: 1.1488x; 1.0725x over previous
import math
import contextlib
import numpy as np
import concourse.bass as bass
import concourse.mybir as mybir
from concourse.bass_utils import run_bass_kernel_spmd

F32 = mybir.dt.float32
BF16 = mybir.dt.bfloat16
AF = mybir.ActivationFunctionType
ALU = mybir.AluOpType
AX = mybir.AxisListType

H = 2048
NFH = 4
FHD = 512
INTER = 5632
TOWN = 1024
NTB = TOWN // 512
EPS_LN = 1e-5
EPS_RMS = 1e-6
BASE_LR_INV = 0.001 + float(np.log(-np.expm1(-0.001)))
ASCALE = 1.0 / math.sqrt(128.0)
GROUPS = [[0, 1], [2, 3], [4, 5], [6, 7]]

DEBUG = False
LAST_RESULTS = [None]

ENGINES = ("pe", "dve", "act", "sync", "pool")
U = 512

DMA_SEMS = ("sIN", "sX0", "sX1", "sX2", "sX3", "sWA", "sKS", "sKH", "sVH", "sOW",
            "sOT", "sW0", "sW1", "sLR", "sAW", "sFW", "sST", "sFK", "sFV", "sFQ",
            "sRS", "sRL", "sON", "sAT", "sWD", "sXW", "sYW", "sOC", "sIP")
ENG_SEMS = ("PE", "DV", "AC", "CC", "GO")


class B:
    def __init__(self, nc):
        self.nc = nc
        self.th = {e: [] for e in ENGINES}
        self.sems = {}
        self.c = {}
        self.wm = {e: {} for e in ENGINES}
        self.flush_fn = None
        self.last_flush = {}

    def add_sem(self, name, handle):
        self.sems[name] = handle
        self.c[name] = 0

    def raw(self, eng, fn):
        self.th[eng].append(fn)

    def op(self, eng, build, sem=None, n=1):
        if sem is not None:
            self.c[sem] += n
            s, k = self.sems[sem], n

            def th(e, b=build, s=s, k=k):
                b(e).then_inc(s, k)
        else:
            def th(e, b=build):
                b(e)
        self.th[eng].append(th)
        return self.c[sem] if sem is not None else None

    def dma(self, eng, sem, build):
        return self.op(eng, build, sem=sem, n=16)

    def wait(self, eng, sem, val):
        assert val <= self.c[sem], f"wait {sem}>={val} emitted={self.c[sem]}"
        if val <= 0 or self.wm[eng].get(sem, -1) >= val:
            return
        if self.flush_fn is not None and val == self.c[sem] and \
                ((sem == "DV" and eng != "dve") or (sem == "AC" and eng != "act")):
            if self.last_flush.get(sem, -1) < val:
                self.flush_fn[sem]()
                self.last_flush[sem] = self.c[sem]
            val = self.c[sem]
            if self.wm[eng].get(sem, -1) >= val:
                return
        self.wm[eng][sem] = val
        self.th[eng].append(lambda e, s=self.sems[sem], v=val: e.wait_ge(s, v))

    def dwait(self, eng, sem):
        self.wait(eng, sem, self.c[sem])

    def barrier(self):
        snap = dict(self.c)
        for eng in ENGINES:
            for s, v in snap.items():
                self.wait(eng, s, v)


def build_program():
    nc = bass.Bass("TRN2", target_bir_lowering=False,
                   detect_race_conditions=False)

    def din(name, shape, dt=F32):
        return nc.dram_tensor(name, shape, dt, kind="ExternalInput").ap()

    I = {}
    for name, shape in (
        ("x_own", [TOWN, H]), ("wqkvT", [H, 3 * H]), ("woT", [H, H]),
        ("wmlpT", [H, 3 * H]), ("lrwT", [H, 12]), ("wmoT", [H, H]),
        ("w0T", [NFH, FHD, FHD]), ("w1om", [NFH, FHD, FHD]),
        ("w1T", [NFH, FHD, FHD]), ("w2T", [NFH, FHD, FHD]),
        ("gT", [H, INTER]), ("uT", [H, INTER]), ("dT", [INTER, H]),
        ("qnw", [128, 16]), ("knw", [128, 16]), ("mlpnw", [128, 16]),
        ("tttnw", [128, NFH]), ("gate_s", [128, 44]), ("up_s", [128, 44]),
        ("down_row", [1, H]), ("lrb_row", [1, 12]), ("eye_in", [128, 128]),
        ("psw_in", [128, 128]), ("cosT_in", [128, TOWN]), ("sinT_in", [128, TOWN]),
        ("mlo_in", [128, 128]), ("mhi_in", [128, 128]), ("hb_in", [128, 1]),
        ("ga_in", [1, 1]), ("gb_in", [1, 1]), ("gout_in", [1, 1]),
    ):
        I[name] = din(name, shape)

    y_own = nc.dram_tensor("y_own", [TOWN, H], F32, kind="ExternalOutput").ap()
    DBG = {}
    if DEBUG:
        for nm, shape, dt in (("dbg_ks", [H, TOWN], BF16), ("dbg_oT", [H, TOWN], BF16),
                              ("dbg_att", [TOWN, H], BF16), ("dbg_fq", [H, TOWN], BF16),
                              ("dbg_fk", [H, TOWN], BF16), ("dbg_fv", [H, TOWN], BF16),
                              ("dbg_oc", [H, TOWN], BF16), ("dbg_xn", [TOWN, H], F32),
                              ("dbg_rg", [16 * FHD, FHD], BF16)):
            DBG[nm] = nc.dram_tensor(nm, shape, dt, kind="ExternalOutput").ap()
        DBG["dbg_scr"] = nc.dram_tensor("dbg_scr", [128, 2048], BF16,
                                        kind="ExternalOutput").ap()
        DBG["dbg_lnx"] = nc.dram_tensor("dbg_lnx", [128, 8192], BF16,
                                        kind="ExternalOutput").ap()
        DBG["dbg_cols"] = nc.dram_tensor("dbg_cols", [128, 8], F32,
                                         kind="ExternalOutput").ap()

    S = {}
    for name, shape, dt in (
        ("ksend", [H, TOWN], BF16), ("vsend", [TOWN, H], BF16),
        ("kgat", [2 * H, TOWN], BF16), ("vgat", [2 * TOWN, H], BF16),
        ("oT_s", [H, TOWN], BF16), ("att_s", [TOWN, H], BF16),
        ("fqT_s", [H, TOWN], BF16), ("fkT_s", [H, TOWN], BF16),
        ("fvT_s", [H, TOWN], BF16), ("oc_s", [H, TOWN], BF16),
        ("xnew_s", [TOWN, H], F32),
        ("rsend", [16 * FHD, FHD], BF16), ("rgat", [16 * FHD, FHD], BF16),
    ):
        S[name] = nc.dram_tensor(name, shape, dt, kind="Internal").ap()

    st = contextlib.ExitStack()
    with st:
        def sb(name, shape, dt):
            return st.enter_context(nc.sbuf_tensor(name, shape, dt))

        sm = {}
        sm["eye"] = sb("eye", [128, 128], BF16)
        sm["psw"] = sb("psw", [128, 128], BF16)
        sm["mlo"] = sb("mlo", [128, 128], BF16)
        sm["mhi"] = sb("mhi", [128, 128], BF16)
        sm["cosT"] = sb("cosT", [128, TOWN], BF16)
        sm["sinT"] = sb("sinT", [128, TOWN], BF16)
        sm["onescol"] = sb("onescol", [128, 1], BF16)
        sm["onesrow32"] = sb("onesrow32", [1, 128], F32)
        for n_, w_ in (("qnw", 16), ("knw", 16), ("mlpnw", 16), ("tttnw", NFH),
                       ("gate_s", 44), ("up_s", 44)):
            sm[n_] = sb("g_" + n_, [128, w_], F32)
        sm["ds_bc"] = sb("ds_bc", [128, H], F32)
        sm["lrb_bc"] = sb("lrb_bc", [128, 12], F32)
        sm["hbcol"] = sb("hbcol", [128, 1], F32)
        sm["gcols"] = sb("gcols", [128, 4], F32)
        sm["lr_tok"] = sb("lr_tok", [128, 8 * 12], F32)
        sm["mucol"] = sb("mucol", [128, 16], F32)
        sm["rscol"] = sb("rscol", [128, 16], F32)
        sm["row"] = sb("row", [1, 2560], F32)
        sm["tiny"] = sb("tiny", [1, 16], F32)
        sm["w12"] = sb("w12", [128, 16], F32)
        sm["s12"] = sb("s12", [128, 16], F32)
        sm["stat"] = sb("stat", [128, 16], F32)
        sm["rbc"] = sb("rbc", [128, 512], F32)
        sm["rbc2"] = sb("rbc2", [128, 1024], F32)
        sm["cb"] = sb("cb", [128, 3], F32)
        sm["dbgc"] = sb("dbgc", [128, 8], F32)

        AR = sb("AR", [128, 142 * U], BF16)
        A32 = sb("A32", [128, 8704], F32)

        ps = [st.enter_context(nc.psum_tensor(f"ps{i}", [128, 512], F32))
              for i in range(6)]
        pst = [st.enter_context(nc.psum_tensor("pstA", [128, 1024], BF16)),
               st.enter_context(nc.psum_tensor("pstB", [128, 1024], BF16))]

        bld = B(nc)
        for s_ in ENG_SEMS + DMA_SEMS:
            bld.add_sem(s_, st.enter_context(nc.semaphore(s_)))

        emit_all(bld, I, S, sm, AR, A32, ps, pst, y_own, DBG)

        blk = st.enter_context(nc.Block())
        for eng_name, key in (("tensor", "pe"), ("vector", "dve"), ("scalar", "act"),
                              ("sync", "sync"), ("gpsimd", "pool")):
            def body(e, lst=bld.th[key]):
                for t in lst:
                    t(e)
            getattr(blk, eng_name)(body)

    return nc


def emit_all(b, I, S, sm, AR, A32, ps, pst, y_own, DBG):
    PE, DV, AC, CC = "PE", "DV", "AC", "CC"

    def ar(u0, nu):
        assert u0 + nu <= 142
        return AR[:, u0 * U:(u0 + nu) * U]

    def arv(u0, ntiles, w=512):
        nu = (ntiles * w + U - 1) // U
        assert u0 + nu <= 142
        t = AR[:, u0 * U:u0 * U + ntiles * w]
        return t.rearrange("p (a c) -> p a c", c=w)

    def mm(out, lhsT, rhs, start, stop, sem=None):
        return b.op("pe", lambda e, o=out, l=lhsT, r=rhs, s1=start, s2=stop:
                    e.matmul(o, l, r, start=s1, stop=s2), sem=sem)

    def tr(out, in_, sem=None):
        return b.op("pe", lambda e, o=out, i=in_: e.transpose(o, i, sm["eye"][:]),
                    sem=sem)

    def pslot(i):
        bank = pst[i % 2]
        c0 = 128 * ((i // 2) % 4)
        return bank[:, c0:c0 + 128]

    def dve(fn, sem=None):
        return b.op("dve", fn, sem=sem)

    def act(fn, sem=None):
        return b.op("act", fn, sem=sem)

    def _dv_flush():
        b.op("dve", lambda e: e.tensor_copy(sm["dbgc"][:, 7:8], sm["cb"][:, 0:1]),
             sem="DV")

    def _ac_flush():
        b.op("act", lambda e: e.copy(sm["dbgc"][:, 6:7], sm["cb"][:, 0:1]), sem="AC")

    b.flush_fn = {"DV": _dv_flush, "AC": _ac_flush}

    def rsqrt_row(n, scale):
        r = sm["row"][0:1, 2048:2048 + n]
        act(lambda e: e.activation(r, r, AF.Ln, scale=scale, bias=sm["cb"][0:1, 0:1]),
            sem=AC)
        b.wait("act", AC, b.c[AC])
        return act(lambda e: e.activation(r, r, AF.Exp, scale=-0.5), sem=AC)

    def bcast_row(n, psbank, dst):
        b.wait("pe", AC, b.c[AC])
        b.wait("pe", DV, b.c[DV])
        mm(psbank[:, 0:n], sm["onesrow32"][:], sm["row"][0:1, 2048:2048 + n],
           True, True, sem=PE)
        b.wait("dve", PE, b.c[PE])
        dve(lambda e: e.tensor_copy(dst[:, 0:n], psbank[:, 0:n]), sem=DV)
        b.wait("pe", DV, b.c[DV])

    def load_w(dst_u, srcT, rows, col0, w, sem, split=2):
        nft = rows // 128
        dst = arv(dst_u, nft, w)
        view = srcT.rearrange("(ft p) c -> p ft c", p=128)
        step = (nft + split - 1) // split
        for s0 in range(0, nft, step):
            s1 = min(nft, s0 + step)
            b.dma("pool", sem, lambda e, a=s0, c=s1: e.dma_start(
                out=dst[:, a:c, :], in_=view[:, a:c, col0:col0 + w]))

    def load_const():
        for dst, src in ((sm["qnw"], I["qnw"]), (sm["knw"], I["knw"]),
                         (sm["mlpnw"], I["mlpnw"]), (sm["tttnw"], I["tttnw"]),
                         (sm["gate_s"], I["gate_s"]), (sm["up_s"], I["up_s"]),
                         (sm["hbcol"], I["hb_in"])):
            b.dma("sync", "sIN", lambda e, d=dst, s=src: e.dma_start(out=d[:], in_=s))
        b.dma("sync", "sIN", lambda e: e.dma_start(out=sm["row"][0:1, 0:H],
                                                   in_=I["down_row"]))
        b.dma("sync", "sIN", lambda e: e.dma_start(out=sm["tiny"][0:1, 0:12],
                                                   in_=I["lrb_row"]))
        b.dma("sync", "sIN", lambda e: e.dma_start(out=sm["tiny"][0:1, 12:13],
                                                   in_=I["ga_in"]))
        b.dma("sync", "sIN", lambda e: e.dma_start(out=sm["tiny"][0:1, 13:14],
                                                   in_=I["gb_in"]))
        b.dma("sync", "sIN", lambda e: e.dma_start(out=sm["tiny"][0:1, 14:15],
                                                   in_=I["gout_in"]))
        for dst, src in ((sm["eye"], I["eye_in"]), (sm["psw"], I["psw_in"]),
                         (sm["cosT"], I["cosT_in"]), (sm["sinT"], I["sinT_in"]),
                         (sm["mlo"], I["mlo_in"]), (sm["mhi"], I["mhi_in"])):
            b.dma("pool", "sIP", lambda e, d=dst, s=src: e.dma_start(out=d[:], in_=s))
        b.dwait("dve", "sIN")
        b.dwait("dve", "sIP")
        b.dwait("pe", "sIP")
        for nm in ("lr_tok", "mucol", "rscol", "w12", "s12",
                   "stat", "rbc", "rbc2"):
            dve(lambda e, nm=nm: e.memset(sm[nm][:], 0.0))
        dve(lambda e: e.memset(sm["onescol"][:], 1.0))
        dve(lambda e: e.memset(sm["onesrow32"][:], 1.0))
        dve(lambda e: e.memset(sm["cb"][:, 0:1], EPS_RMS))
        dve(lambda e: e.memset(sm["cb"][:, 1:2], 1.0))
        dve(lambda e: e.memset(sm["cb"][:, 2:3], EPS_LN))
        dve(lambda e: e.tensor_scalar_add(sm["row"][0:1, 2048:2060],
                                          sm["tiny"][0:1, 0:12], BASE_LR_INV), sem=DV)
        b.wait("pe", DV, b.c[DV])
        b.dwait("pe", "sIN")
        for j in range(4):
            mm(ps[j % 2][:], sm["onesrow32"][:], sm["row"][0:1, 512 * j:512 * (j + 1)],
               True, True, sem=PE)
            b.wait("dve", PE, b.c[PE])
            dve(lambda e, j=j, p=ps[j % 2]: e.tensor_copy(
                sm["ds_bc"][:, 512 * j:512 * (j + 1)], p[:]), sem=DV)
            b.wait("pe", DV, b.c[DV])
        b.wait("act", DV, b.c[DV])
        act(lambda e: e.activation(sm["s12"][:, 0:1], sm["cb"][:, 1:2], AF.Ln), sem=AC)
        b.wait("act", AC, b.c[AC])
        act(lambda e: e.activation(sm["s12"][:, 0:1], sm["cb"][:, 1:2], AF.Exp), sem=AC)
        b.wait("act", AC, b.c[AC])
        act(lambda e: e.activation(sm["s12"][:, 0:1], sm["cb"][:, 1:2], AF.Square),
            sem=AC)
        b.wait("act", AC, b.c[AC])
        act(lambda e: e.activation(sm["s12"][:, 0:1], sm["cb"][:, 1:2], AF.Identity),
            sem=AC)
        mm(ps[0][:, 0:12], sm["onesrow32"][:], sm["row"][0:1, 2048:2060], True, True,
           sem=PE)
        mm(ps[1][:, 0:3], sm["onesrow32"][:], sm["tiny"][0:1, 12:15], True, True, sem=PE)
        b.wait("dve", PE, b.c[PE])
        dve(lambda e: e.tensor_copy(sm["lrb_bc"][:], ps[0][:, 0:12]))
        dve(lambda e: e.tensor_copy(sm["gcols"][:, 0:3], ps[1][:, 0:3]), sem=DV)
        b.wait("pe", DV, b.c[DV])

    def phase_a():
        lnxT = arv(0, 16)
        scr = ar(16, 4)
        WB = 20
        qres = arv(46, 16, 1024)
        kres = arv(78, 16, 1024)
        vres = arv(110, 8, 2048)
        xt = [A32[:, 0:2048], A32[:, 2048:4096]]
        xsem = ["sX0", "sX1"]
        xgate = [0, 0]
        stt_ = sm["stat"]
        xv = I["x_own"].rearrange("(tt p) f -> p tt f", p=128)
        free_w = b.c[PE]
        bfree = [0, 0, 0]

        for gt0 in range(2):
            b.dma("sync", xsem[gt0 % 2], lambda e, gt0=gt0, d=xt[gt0 % 2]: e.dma_start(
                out=d, in_=xv[:, gt0, :]))
        for tb in range(NTB):
            for tt in range(4):
                gt = tb * 4 + tt
                b.dwait("act", xsem[tt % 2])
                b.dwait("dve", xsem[tt % 2])
                b.wait("dve", AC, b.c[AC])
                dve(lambda e: e.tensor_single_scalar(stt_[:, 1:2], sm["cb"][:, 0:1],
                                                     0.0, ALU.mult))
                dve(lambda e, x=xt[tt % 2]: e.reduce_sum(stt_[:, 0:1], x, axis=AX.X),
                    sem=DV)
                b.wait("act", DV, b.c[DV])
                act(lambda e, x=xt[tt % 2]: e.activation(
                    scr, x, AF.Square, accum_out=stt_[:, 1:2]), sem=AC)
                act(lambda e: e.activation(stt_[:, 2:3], stt_[:, 0:1], AF.Square,
                                           scale=1.0 / H), sem=AC)
                b.wait("act", AC, b.c[AC])
                act(lambda e: e.activation(stt_[:, 6:7], stt_[:, 0:1], AF.Identity,
                                           scale=1.0 / H), sem=AC)
                b.wait("dve", AC, b.c[AC])
                dve(lambda e: e.scalar_tensor_tensor(
                    stt_[:, 3:4], stt_[:, 1:2], 1.0 / H, stt_[:, 2:3],
                    ALU.mult, ALU.subtract), sem=DV)
                b.wait("act", DV, b.c[DV])
                act(lambda e: e.activation(stt_[:, 4:5], stt_[:, 3:4], AF.Ln,
                                           bias=sm["cb"][:, 2:3]), sem=AC)
                b.wait("act", AC, b.c[AC])
                act(lambda e: e.activation(stt_[:, 5:6], stt_[:, 4:5], AF.Exp,
                                           scale=-0.5), sem=AC)
                b.wait("dve", AC, b.c[AC])
                dve(lambda e, x=xt[tt % 2]: e.tensor_scalar(
                    scr, x, stt_[:, 6:7], stt_[:, 5:6], ALU.subtract, ALU.mult), sem=DV)
                if gt + 2 < 4 * NTB:
                    b.wait("sync", DV, b.c[DV])
                    b.dma("sync", xsem[tt % 2], lambda e, gt=gt, d=xt[tt % 2]:
                          e.dma_start(out=d, in_=xv[:, gt + 2, :]))
                if DEBUG and tb == 0 and tt == 0:
                    dve(lambda e: e.tensor_copy(sm["dbgc"][:, 0:6], stt_[:, 0:6]),
                        sem=DV)
                    b.wait("sync", DV, b.c[DV])
                    b.dma("sync", "sKS", lambda e: e.dma_start(
                        out=DBG["dbg_scr"], in_=scr))
                xgate[tt % 2] = b.c[DV]
                b.wait("pe", DV, b.c[DV])
                for ft in range(16):
                    tr(pslot(ft), scr[:, 128 * ft:128 * (ft + 1)], sem=PE)
                    b.wait("dve", PE, b.c[PE])
                    dve(lambda e, ft=ft, tt=tt: e.tensor_copy(
                        lnxT[:, ft, 128 * tt:128 * (tt + 1)], pslot(ft)), sem=DV)
                    b.wait("pe", DV, b.c[DV] - 1)
                b.wait("pe", DV, b.c[DV])

            if DEBUG and tb == 0:
                b.wait("sync", DV, b.c[DV])
                b.dma("sync", "sKS", lambda e: e.dma_start(
                    out=DBG["dbg_lnx"], in_=AR[:, 0:8192]))
            for wc in range(12):
                b.wait("pool", PE, free_w)
                load_w(WB, I["wqkvT"], H, 512 * wc, 512, "sWA")
                b.dwait("pe", "sWA")
                wv = arv(WB, 16)
                if wc < 8:
                    dst = qres if wc < 4 else kres
                    for osub in range(4):
                        head = (wc % 4) * 4 + osub
                        bi = (4 * wc + osub) % 3
                        bank = ps[bi]
                        b.wait("pe", DV, bfree[bi])
                        for ft in range(16):
                            mm(bank[:], wv[:, ft, 128 * osub:128 * (osub + 1)],
                               lnxT[:, ft, :], ft == 0, ft == 15,
                               sem=PE if ft == 15 else None)
                        b.wait("dve", PE, b.c[PE])
                        dve(lambda e, d=dst, hd=head, tb=tb, p=bank: e.tensor_copy(
                            d[:, hd, 512 * tb:512 * (tb + 1)], p[:]), sem=DV)
                        bfree[bi] = b.c[DV]
                else:
                    for tt in range(4):
                        bi = (4 * wc + tt) % 3
                        bank = ps[bi]
                        b.wait("pe", DV, bfree[bi])
                        for ft in range(16):
                            mm(bank[:], lnxT[:, ft, 128 * tt:128 * (tt + 1)],
                               wv[:, ft, :], ft == 0, ft == 15,
                               sem=PE if ft == 15 else None)
                        b.wait("dve", PE, b.c[PE])
                        kt = tb * 4 + tt
                        dve(lambda e, kt=kt, wc=wc, p=bank: e.tensor_copy(
                            vres[:, kt, 512 * (wc - 8):512 * (wc - 7)], p[:]), sem=DV)
                        bfree[bi] = b.c[DV]
                free_w = b.c[PE]
            b.wait("pe", DV, b.c[DV])

            for (res, gam) in ((qres, sm["qnw"]), (kres, sm["knw"])):
                for ft in range(16):
                    if ft >= 4:
                        b.wait("dve", PE, b.c[PE] - 3)
                    dve(lambda e, ft=ft, r=res, tb=tb: e.tensor_mul(
                        scr[:, 512 * (ft % 4):512 * (ft % 4 + 1)],
                        r[:, ft, 512 * tb:512 * (tb + 1)],
                        r[:, ft, 512 * tb:512 * (tb + 1)]), sem=DV)
                    b.wait("pe", DV, b.c[DV])
                    mm(ps[3][0:1, 0:512], sm["onescol"][:],
                       scr[:, 512 * (ft % 4):512 * (ft % 4 + 1)], ft == 0, ft == 15,
                       sem=PE)
                b.wait("dve", PE, b.c[PE])
                dve(lambda e: e.tensor_copy(sm["row"][0:1, 2048:2560], ps[3][0:1, 0:512]),
                    sem=DV)
                b.wait("act", DV, b.c[DV])
                rsqrt_row(512, 1.0 / H)
                bcast_row(512, ps[3], sm["rbc"])
                for ft in range(16):
                    cs = slice(512 * tb, 512 * (tb + 1))
                    dve(lambda e, r=res, ft=ft, gm=gam, cs=cs: e.scalar_tensor_tensor(
                        scr[:, 0:512], r[:, ft, cs], gm[:, ft:ft + 1], sm["rbc"][:, 0:512],
                        ALU.mult, ALU.mult), sem=DV)
                    b.wait("pe", DV, b.c[DV])
                    mm(ps[4][:], sm["psw"][:], scr[:, 0:512], True, True, sem=PE)
                    b.wait("dve", PE, b.c[PE])
                    dve(lambda e, cs=cs: e.tensor_mul(
                        scr[:, 512:1024], scr[:, 0:512], sm["cosT"][:, cs]))
                    dve(lambda e, cs=cs: e.tensor_mul(
                        scr[:, 1024:1536], ps[4][:], sm["sinT"][:, cs]), sem=DV)
                    b.wait("pe", DV, b.c[DV])
                    dve(lambda e, ft=ft, r=res, cs=cs: e.tensor_add(
                        r[:, ft, cs], scr[:, 512:1024], scr[:, 1024:1536]), sem=DV)
                b.wait("pe", DV, b.c[DV])

        b.wait("sync", DV, b.c[DV])
        b.dma("sync", "sKS", lambda e: e.dma_start(
            out=S["ksend"].rearrange("(ft p) t -> p ft t", p=128), in_=kres[:, :, :]))
        b.dma("sync", "sKS", lambda e: e.dma_start(
            out=S["vsend"].rearrange("(kt p) f -> p kt f", p=128), in_=vres[:, :, :]))
        b.dwait("pool", "sKS")
        b.op("pool", lambda e: e.collective_compute(
            "AllGather", ALU.bypass, replica_groups=GROUPS,
            ins=[S["ksend"].opt()], outs=[S["kgat"].opt()]), sem=CC)
        b.op("pool", lambda e: e.collective_compute(
            "AllGather", ALU.bypass, replica_groups=GROUPS,
            ins=[S["vsend"].opt()], outs=[S["vgat"].opt()]), sem=CC)
        return qres, kres, vres

    def phase_b(qres, kres, vres):
        kh = ar(0, 2)
        vh = arv(2, 8, 128)
        ptil = ar(4, 9)
        sqb = ar(14, 4)
        oTh = ar(18, 2)
        b.wait("sync", CC, b.c[CC])
        khg = S["kgat"][0:H, :].rearrange("(ft p) t -> p ft t", p=128)
        vhg = S["vgat"][0:TOWN, :].rearrange("(kt p) f -> p kt f", p=128)

        pieces = []
        for j in range(12):
            s_lo, s_hi = max(0, j - 7), min(3, j - 1)
            if s_hi >= s_lo:
                pieces.append((j, 128 * s_lo, 128 * (s_hi - s_lo + 1), None))
            if j <= 3:
                pieces.append((j, 128 * j, 128, "lo"))
            if j >= 8:
                pieces.append((j, 128 * (j - 8), 128, "hi"))
        ppos, poff = [], 0
        for (j, c0, n, msk) in pieces:
            ppos.append((j, c0, n, msk, poff))
            poff += n
        assert poff == 4608

        pe_prev_head = b.c[PE]
        dv_prev_head = b.c[DV]
        for h in range(16):
            b.wait("sync", PE, pe_prev_head)
            b.wait("sync", DV, dv_prev_head)
            b.dwait("sync", "sOW")
            b.dma("sync", "sKH", lambda e, h=h: e.dma_start(out=kh, in_=khg[:, h, :]))
            for half in range(2):
                b.dma("sync", "sVH", lambda e, h=h, half=half: e.dma_start(
                    out=vh[:, 4 * half:4 * half + 4, :],
                    in_=vhg[:, 4 * half:4 * half + 4, 128 * h:128 * (h + 1)]))
            b.dwait("dve", "sKH")
            b.dwait("dve", "sVH")
            b.dwait("pe", "sKH")
            b.dwait("pe", "sVH")
            for idx4 in range(4):
                ksrc = kh[:, 512 * idx4:512 * (idx4 + 1)] if idx4 < 2 else \
                    kres[:, h, :][:, 512 * (idx4 - 2):512 * (idx4 - 1)]
                b.wait("dve", PE, b.c[PE])
                dve(lambda e, ksrc=ksrc: e.tensor_mul(sqb[:, 0:512], ksrc, ksrc), sem=DV)
                b.wait("pe", DV, b.c[DV])
                mm(ps[3][0:1, 0:512], sm["onescol"][:], sqb[:, 0:512], True, True, sem=PE)
                b.wait("dve", PE, b.c[PE])
                dve(lambda e, o=512 * idx4: e.tensor_copy(
                    sm["row"][0:1, o:o + 512], ps[3][0:1, 0:512]), sem=DV)

            for qb in range(NTB):
                q0 = 512 * qb
                b.wait("dve", PE, b.c[PE])
                dve(lambda e, h=h, q0=q0: e.tensor_mul(
                    sqb[:, 0:512], qres[:, h, :][:, q0:q0 + 512],
                    qres[:, h, :][:, q0:q0 + 512]), sem=DV)
                b.wait("pe", DV, b.c[DV])
                mm(ps[3][0:1, 0:512], sm["onescol"][:], sqb[:, 0:512], True, True, sem=PE)
                b.wait("dve", PE, b.c[PE])
                dve(lambda e: e.tensor_copy(sm["row"][0:1, 2048:2560], ps[3][0:1, 0:512]),
                    sem=DV)
                dve(lambda e: e.reduce_max(sm["tiny"][0:1, 4:5],
                                           sm["row"][0:1, 2048:2560], axis=AX.X))
                dve(lambda e, q0=q0: e.reduce_max(
                    sm["tiny"][0:1, 5:6], sm["row"][0:1, q0:q0 + 1536], axis=AX.X),
                    sem=DV)
                b.wait("act", DV, b.c[DV])
                act(lambda e: e.activation(sm["tiny"][0:1, 6:7], sm["tiny"][0:1, 4:5],
                                           AF.Identity, scale=sm["tiny"][0:1, 5:6]),
                    sem=AC)
                b.wait("act", AC, b.c[AC])
                act(lambda e: e.activation(sm["tiny"][0:1, 7:8], sm["tiny"][0:1, 6:7],
                                           AF.Ln), sem=AC)
                b.wait("act", AC, b.c[AC])
                act(lambda e: e.activation(sm["tiny"][0:1, 9:10], sm["tiny"][0:1, 7:8],
                                           AF.Exp, scale=0.5), sem=AC)
                b.wait("dve", AC, b.c[AC])
                dve(lambda e: e.tensor_scalar_mul(sm["tiny"][0:1, 10:11],
                                                  sm["tiny"][0:1, 9:10], -ASCALE),
                    sem=DV)
                b.wait("pe", DV, b.c[DV])
                mm(ps[3][:, 0:1], sm["onesrow32"][:], sm["tiny"][0:1, 10:11], True, True,
                   sem=PE)
                b.wait("dve", PE, b.c[PE])
                dve(lambda e: e.tensor_copy(sm["stat"][:, 3:4], ps[3][:, 0:1]), sem=DV)
                b.wait("act", DV, b.c[DV])
                act(lambda e: e.activation(sm["stat"][:, 2:3], sm["stat"][:, 3:4],
                                           AF.Identity, bias=sm["hbcol"][:]), sem=AC)
                b.wait("act", AC, b.c[AC])

                def kT_src(j, qb=qb, h=h):
                    kk0 = 512 * qb + 128 * j
                    if kk0 < 1024:
                        return kh[:, kk0:kk0 + 128]
                    return kres[:, h, :][:, kk0 - 1024:kk0 - 896]

                def v_src(j, qb=qb, h=h):
                    kk0 = 512 * qb + 128 * j
                    if kk0 < 1024:
                        return vh[:, kk0 // 128, :]
                    return vres[:, (kk0 - 1024) // 128, 128 * h:128 * (h + 1)]

                bank_free = [b.c[AC]] * 4
                bi = 0
                i0 = 0
                while i0 < len(ppos):
                    used, grp = 0, []
                    while i0 < len(ppos) and used + ppos[i0][2] <= 512:
                        grp.append(ppos[i0])
                        used += ppos[i0][2]
                        i0 += 1
                    bank = ps[bi]
                    b.wait("pe", AC, bank_free[bi])
                    ro = 0
                    for gi, (j, c0, n, msk, po) in enumerate(grp):
                        mm(bank[:, ro:ro + n], kT_src(j),
                           qres[:, h, :][:, q0 + c0:q0 + c0 + n], True, True,
                           sem=PE if gi == len(grp) - 1 else None)
                        ro += n
                    b.wait("act", PE, b.c[PE])
                    ro = 0
                    for (j, c0, n, msk, po) in grp:
                        halo = (512 * qb + 128 * j) < 1024
                        bias = sm["stat"][:, 2:3] if halo else sm["stat"][:, 3:4]
                        act(lambda e, ro=ro, n=n, po=po, bias=bias, bank=bank:
                            e.activation(ptil[:, po:po + n], bank[:, ro:ro + n], AF.Exp,
                                         scale=ASCALE, bias=bias), sem=AC)
                        ro += n
                    bank_free[bi] = b.c[AC]
                    bi = (bi + 1) % 4
                av = b.c[AC]
                b.wait("dve", AC, av)
                for (j, c0, n, msk, po) in ppos:
                    if msk is None:
                        continue
                    m = sm["mlo"] if msk == "lo" else sm["mhi"]
                    dve(lambda e, po=po, m=m: e.tensor_mul(
                        ptil[:, po:po + 128], ptil[:, po:po + 128], m[:]), sem=DV)
                dvv = b.c[DV]
                b.wait("pe", DV, dvv)
                b.wait("pe", AC, av)
                for pi, (j, c0, n, msk, po) in enumerate(ppos):
                    mm(ps[4][:, c0:c0 + n], v_src(j), ptil[:, po:po + n],
                       pi == 0, pi == len(ppos) - 1)
                for pi, (j, c0, n, msk, po) in enumerate(ppos):
                    mm(ps[5][0:1, c0:c0 + n], sm["onescol"][:], ptil[:, po:po + n],
                       pi == 0, pi == len(ppos) - 1,
                       sem=PE if pi == len(ppos) - 1 else None)
                b.wait("dve", PE, b.c[PE])
                dve(lambda e: e.tensor_scalar_add(sm["row"][0:1, 2048:2560],
                                                  ps[5][0:1, 0:512], 1e-38))
                dve(lambda e: e.reciprocal(sm["row"][0:1, 2048:2560],
                                           sm["row"][0:1, 2048:2560]), sem=DV)
                b.wait("pe", DV, b.c[DV])
                mm(ps[5][:, 0:512], sm["onesrow32"][:], sm["row"][0:1, 2048:2560],
                   True, True, sem=PE)
                b.wait("dve", PE, b.c[PE])
                dve(lambda e: e.tensor_copy(sm["rbc"][:], ps[5][:, 0:512]), sem=DV)
                dve(lambda e, qb=qb: e.tensor_mul(
                    oTh[:, 512 * qb:512 * (qb + 1)], ps[4][:, 0:512], sm["rbc"][:]),
                    sem=DV)
                b.wait("pe", DV, b.c[DV])
            pe_prev_head = b.c[PE]
            dv_prev_head = b.c[DV]
            b.wait("sync", DV, b.c[DV])
            b.dma("sync", "sOW", lambda e, h=h: e.dma_start(
                out=S["oT_s"].rearrange("(ft p) t -> p ft t", p=128)[:, h, :], in_=oTh))

    def phase_c():
        oTst = arv(0, 16)
        attn_sb = arv(16, 4, 2048)
        x2T = arv(32, 16)
        WB0, WB1 = 48, 64
        fst = arv(80, 48)
        scr = ar(128, 4)
        lrw_sb = arv(132, 16, 12)
        b.dma("pool", "sLR", lambda e: e.dma_start(
            out=lrw_sb[:, :, :], in_=I["lrwT"].rearrange("(ft p) c -> p ft c", p=128)))
        free_w = [b.c[PE]] * 2
        pe_prev = b.c[PE]
        wsem = ["sW0", "sW1"]
        bfree = [0] * 6
        afree = [0] * 6

        for tb in range(NTB):
            b.wait("sync", PE, pe_prev)
            b.dma("sync", "sOT", lambda e, tb=tb: e.dma_start(
                out=oTst[:, :, :], in_=S["oT_s"].rearrange("(ft p) t -> p ft t", p=128)
                [:, :, 512 * tb:512 * (tb + 1)]))
            b.dwait("pe", "sOT")
            b.dwait("dve", "sAW")
            for oc in range(4):
                b.wait("pool", PE, free_w[oc % 2])
                load_w((WB0, WB1)[oc % 2], I["woT"], H, 512 * oc, 512, wsem[oc % 2])
                b.dwait("pe", wsem[oc % 2])
                wv = arv((WB0, WB1)[oc % 2], 16)
                for tt in range(4):
                    bi = (4 * oc + tt) % 3
                    bank = ps[bi]
                    b.wait("pe", DV, bfree[bi])
                    b.wait("pe", AC, afree[bi])
                    for ft in range(16):
                        mm(bank[:], oTst[:, ft, 128 * tt:128 * (tt + 1)], wv[:, ft, :],
                           ft == 0, ft == 15, sem=PE if ft == 15 else None)
                    b.wait("dve", PE, b.c[PE])
                    dve(lambda e, tt=tt, oc=oc, p=bank: e.tensor_copy(
                        attn_sb[:, tt, 512 * oc:512 * (oc + 1)], p[:]), sem=DV)
                    bfree[bi] = b.c[DV]
                free_w[oc % 2] = b.c[PE]
            b.wait("sync", DV, b.c[DV])
            b.dma("sync", "sAW", lambda e, tb=tb: e.dma_start(
                out=S["att_s"].rearrange("(tt p) f -> p tt f", p=128)
                [:, 4 * tb:4 * tb + 4, :], in_=attn_sb[:, :, :]))
            dve(lambda e: e.tensor_single_scalar(sm["stat"][:, 0:4], sm["qnw"][:, 0:4],
                                                 0.0, ALU.mult), sem=DV)
            for tt in range(4):
                dve(lambda e, tt=tt: e.scalar_tensor_tensor(
                    scr[:, 0:2048], attn_sb[:, tt, :], 1.0, attn_sb[:, tt, :],
                    ALU.mult, ALU.mult, accum_out=sm["stat"][:, tt:tt + 1]), sem=DV)
            b.wait("act", DV, b.c[DV])
            act(lambda e: e.activation(sm["stat"][:, 4:8], sm["stat"][:, 0:4], AF.Ln,
                                       scale=1.0 / H, bias=sm["cb"][:, 0:1]), sem=AC)
            b.wait("act", AC, b.c[AC])
            act(lambda e: e.activation(sm["stat"][:, 4:8], sm["stat"][:, 4:8], AF.Exp,
                                       scale=-0.5), sem=AC)
            b.wait("dve", AC, b.c[AC])
            for tt in range(4):
                dve(lambda e, tt=tt: e.tensor_scalar_mul(
                    scr[:, 0:2048], attn_sb[:, tt, :], sm["stat"][:, 4 + tt:5 + tt]),
                    sem=DV)
                b.wait("pe", DV, b.c[DV])
                for ft in range(16):
                    tr(pslot(ft), scr[:, 128 * ft:128 * (ft + 1)], sem=PE)
                    b.wait("dve", PE, b.c[PE])
                    dve(lambda e, ft=ft, tt=tt: e.tensor_scalar_mul(
                        x2T[:, ft, 128 * tt:128 * (tt + 1)], pslot(ft),
                        sm["mlpnw"][:, ft:ft + 1]), sem=DV)
                    b.wait("pe", DV, b.c[DV] - 1)
                b.wait("pe", DV, b.c[DV])

            b.dwait("act", "sFW")
            for wc in range(12):
                b.wait("pool", PE, free_w[wc % 2])
                load_w((WB0, WB1)[wc % 2], I["wmlpT"], H, 512 * wc, 512, wsem[wc % 2])
                b.dwait("pe", wsem[wc % 2])
                wv = arv((WB0, WB1)[wc % 2], 16)
                for osub in range(4):
                    bi = (4 * wc + osub) % 3
                    bank = ps[bi]
                    b.wait("pe", AC, afree[bi])
                    b.wait("pe", DV, bfree[bi])
                    for ft in range(16):
                        mm(bank[:], wv[:, ft, 128 * osub:128 * (osub + 1)], x2T[:, ft, :],
                           ft == 0, ft == 15, sem=PE if ft == 15 else None)
                    b.wait("act", PE, b.c[PE])
                    act(lambda e, i=4 * wc + osub, p=bank: e.activation(
                        fst[:, i, :], p[:], AF.Silu), sem=AC)
                    afree[bi] = b.c[AC]
                free_w[wc % 2] = b.c[PE]
            ac_silu = b.c[AC]
            b.wait("dve", AC, ac_silu)

            for base in (0, 16):
                for hh in range(NFH):
                    for k in range(4):
                        i = base + 4 * hh + k
                        if k > 0:
                            b.wait("dve", PE, b.c[PE])
                        dve(lambda e, i=i: e.tensor_mul(scr[:, 0:512], fst[:, i, :],
                                                        fst[:, i, :]), sem=DV)
                        b.wait("pe", DV, b.c[DV])
                        mm(ps[3][0:1, 0:512], sm["onescol"][:], scr[:, 0:512],
                           k == 0, k == 3, sem=PE)
                    b.wait("dve", PE, b.c[PE])
                    dve(lambda e: e.tensor_copy(sm["row"][0:1, 2048:2560],
                                                ps[3][0:1, 0:512]), sem=DV)
                    b.wait("act", DV, b.c[DV])
                    act(lambda e: e.activation(sm["row"][0:1, 2048:2560],
                                               sm["row"][0:1, 2048:2560], AF.Ln), sem=AC)
                    b.wait("act", AC, b.c[AC])
                    act(lambda e: e.activation(sm["row"][0:1, 2048:2560],
                                               sm["row"][0:1, 2048:2560], AF.Exp,
                                               scale=0.5), sem=AC)
                    b.wait("dve", AC, b.c[AC])
                    dve(lambda e: e.tensor_scalar_max(sm["row"][0:1, 2048:2560],
                                                      sm["row"][0:1, 2048:2560], 1e-6))
                    dve(lambda e: e.reciprocal(sm["row"][0:1, 2048:2560],
                                               sm["row"][0:1, 2048:2560]), sem=DV)
                    bcast_row(512, ps[3], sm["rbc"])
                    for k in range(4):
                        i = base + 4 * hh + k
                        dve(lambda e, i=i: e.tensor_mul(fst[:, i, :], fst[:, i, :],
                                                        sm["rbc"][:, 0:512]), sem=DV)
            b.wait("sync", DV, b.c[DV])
            b.wait("sync", AC, ac_silu)
            for (dst_s, lo) in ((S["fqT_s"], 0), (S["fkT_s"], 16), (S["fvT_s"], 32)):
                b.dma("sync", "sFW", lambda e, d=dst_s, lo=lo, tb=tb: e.dma_start(
                    out=d.rearrange("(ft p) t -> p ft t", p=128)
                    [:, :, 512 * tb:512 * (tb + 1)], in_=fst[:, lo:lo + 16, :]))

            b.dwait("pe", "sLR")
            for tt in range(4):
                gt = 4 * tb + tt
                b.wait("pe", DV, b.c[DV])
                for ft in range(16):
                    mm(ps[4][:, 0:12], x2T[:, ft, 128 * tt:128 * (tt + 1)],
                       lrw_sb[:, ft, :], ft == 0, ft == 15, sem=PE if ft == 15 else None)
                b.wait("dve", PE, b.c[PE])
                dve(lambda e: e.tensor_add(sm["w12"][:, 0:12], ps[4][:, 0:12],
                                           sm["lrb_bc"][:]), sem=DV)
                b.wait("act", DV, b.c[DV])
                act(lambda e: e.activation(sm["s12"][:, 0:12], sm["w12"][:, 0:12],
                                           AF.Abs), sem=AC)
                b.wait("act", AC, b.c[AC])
                act(lambda e: e.activation(sm["s12"][:, 0:12], sm["s12"][:, 0:12],
                                           AF.Exp, scale=-1.0), sem=AC)
                b.wait("act", AC, b.c[AC])
                act(lambda e: e.activation(sm["s12"][:, 0:12], sm["s12"][:, 0:12],
                                           AF.Ln, bias=sm["cb"][:, 1:2]), sem=AC)
                b.wait("act", AC, b.c[AC])
                act(lambda e: e.activation(sm["w12"][:, 0:12], sm["w12"][:, 0:12],
                                           AF.Relu), sem=AC)
                b.wait("dve", AC, b.c[AC])
                dve(lambda e, gt=gt: e.tensor_add(
                    sm["lr_tok"][:, 12 * gt:12 * (gt + 1)], sm["w12"][:, 0:12],
                    sm["s12"][:, 0:12]), sem=DV)
            pe_prev = b.c[PE]

    def phase_d():
        mats = ("st0", "st2", "st1", "w1o")
        stu = {}
        u0 = 0
        for hh in range(NFH):
            for m in mats:
                stu[(hh, m)] = u0
                u0 += 4
        ST = lambda hh, m: arv(stu[(hh, m)], 4)
        fk_c = arv(64, 16)
        fvh = arv(80, 4)
        fqh = arv(84, 4)
        KC0, KC2, VC1, HID, DGP, DHB, SGK, GKS, HBM = [arv(88 + 4 * i, 4)
                                                       for i in range(9)]
        HQ = arv(124, 4)
        OCS = arv(128, 4)
        scr = ar(132, 4)

        def load_base():
            for hh in range(NFH):
                for m, src in zip(mats, (I["w0T"], I["w2T"], I["w1T"], I["w1om"])):
                    b.dma("pool", "sST", lambda e, hh=hh, m=m, src=src: e.dma_start(
                        out=ST(hh, m)[:, :, :],
                        in_=src[hh].rearrange("(dj p) e -> p dj e", p=128)))

        def kpath(hh, c, upd):
            fkh_t = lambda dj: fk_c[:, 4 * hh + dj, :]
            fvh_t = lambda dj: fvh[:, dj, :]
            for (srcf, outs) in ((fkh_t, ((KC0, 0), (KC2, 2))), (fvh_t, ((VC1, 1),))):
                for dj in range(4):
                    for cj in range(4):
                        tr(pslot(cj), srcf(dj)[:, 128 * cj:128 * (cj + 1)],
                           sem=PE)
                        b.wait("dve", PE, b.c[PE])
                        for (dst, li) in outs:
                            lrc = sm["lr_tok"][:, 12 * (4 * c + cj) + 4 * li + hh:
                                               12 * (4 * c + cj) + 4 * li + hh + 1]
                            dve(lambda e, dst=dst, cj=cj, dj=dj, lrc=lrc:
                                e.tensor_scalar_mul(
                                    dst[:, cj, 128 * dj:128 * (dj + 1)],
                                    pslot(cj), lrc), sem=DV)
                        b.wait("pe", DV, b.c[DV] - len(outs))
                b.wait("pe", DV, b.c[DV])
            for cj in range(4):
                for dj in range(4):
                    mm(ps[0][:], fkh_t(dj)[:, 128 * cj:128 * (cj + 1)],
                       ST(hh, "st0")[:, dj, :], dj == 0, dj == 3)
                for dj in range(4):
                    mm(ps[1][:], fkh_t(dj)[:, 128 * cj:128 * (cj + 1)],
                       ST(hh, "st2")[:, dj, :], dj == 0, dj == 3)
                for oj in range(4):
                    mm(ps[2][:], fvh_t(oj)[:, 128 * cj:128 * (cj + 1)],
                       ST(hh, "w1o")[:, oj, :], oj == 0, oj == 3,
                       sem=PE if oj == 3 else None)
                pv = b.c[PE]
                b.wait("act", PE, pv)
                act(lambda e, cj=cj: e.activation(SGK[:, cj, :], ps[0][:], AF.Sigmoid),
                    sem=AC)
                b.wait("dve", AC, b.c[AC])
                b.wait("dve", PE, pv)
                dve(lambda e, cj=cj: e.tensor_mul(GKS[:, cj, :], ps[0][:], SGK[:, cj, :]))
                dve(lambda e, cj=cj: e.tensor_sub(scr[:, 0:512], ps[0][:], GKS[:, cj, :]))
                dve(lambda e, cj=cj: e.tensor_copy(HBM[:, cj, :], ps[1][:]))
                dve(lambda e, cj=cj: e.tensor_mul(HID[:, cj, :], GKS[:, cj, :],
                                                  HBM[:, cj, :]))
                dve(lambda e, cj=cj: e.tensor_mul(scr[:, 512:1024], ps[2][:],
                                                  HBM[:, cj, :]))
                dve(lambda e, cj=cj: e.tensor_mul(DHB[:, cj, :], ps[2][:], GKS[:, cj, :]))
                dve(lambda e, cj=cj: e.scalar_tensor_tensor(
                    scr[:, 0:512], scr[:, 0:512], 1.0, SGK[:, cj, :], ALU.add, ALU.mult))
                dve(lambda e, cj=cj: e.tensor_mul(DGP[:, cj, :], scr[:, 0:512],
                                                  scr[:, 512:1024]), sem=DV)
                b.wait("pe", DV, b.c[DV])
            if not upd:
                return
            for (lhs, rhs, mat) in ((KC0, DGP, "st0"), (KC2, DHB, "st2"),
                                    (VC1, HID, "w1o"), (HID, VC1, "st1")):
                for dj in range(4):
                    bank = ps[3 + dj % 2]
                    for cj in range(4):
                        mm(bank[:], lhs[:, cj, 128 * dj:128 * (dj + 1)], rhs[:, cj, :],
                           cj == 0, cj == 3, sem=PE if cj == 3 else None)
                    b.wait("dve", PE, b.c[PE])
                    dve(lambda e, mat=mat, dj=dj, hh=hh, p=bank: e.tensor_add(
                        ST(hh, mat)[:, dj, :], ST(hh, mat)[:, dj, :], p[:]), sem=DV)
                    if dj >= 1:
                        b.wait("pe", DV, b.c[DV] - 1)
                b.wait("pe", DV, b.c[DV])

        def qpath(hh, c):
            for ej in range(4):
                for dj in range(4):
                    mm(ps[0][:], ST(hh, "st0")[:, dj, 128 * ej:128 * (ej + 1)],
                       fqh[:, dj, :], dj == 0, dj == 3)
                for dj in range(4):
                    mm(ps[1][:], ST(hh, "st2")[:, dj, 128 * ej:128 * (ej + 1)],
                       fqh[:, dj, :], dj == 0, dj == 3, sem=PE if dj == 3 else None)
                pv = b.c[PE]
                b.wait("act", PE, pv)
                act(lambda e: e.activation(scr[:, 0:512], ps[0][:], AF.Sigmoid), sem=AC)
                b.wait("dve", AC, b.c[AC])
                b.wait("dve", PE, pv)
                dve(lambda e: e.tensor_mul(scr[:, 512:1024], ps[0][:], scr[:, 0:512]))
                dve(lambda e, ej=ej: e.tensor_mul(HQ[:, ej, :], scr[:, 512:1024],
                                                  ps[1][:]), sem=DV)
                b.wait("pe", DV, b.c[DV])
            b.dwait("dve", "sOC")
            b.dwait("dve", "sRS")
            for oj in range(4):
                bank = ps[2 + oj % 2]
                for ej in range(4):
                    mm(bank[:], ST(hh, "st1")[:, ej, 128 * oj:128 * (oj + 1)],
                       HQ[:, ej, :], ej == 0, ej == 3, sem=PE if ej == 3 else None)
                b.wait("dve", PE, b.c[PE])
                dve(lambda e, oj=oj, p=bank: e.tensor_copy(OCS[:, oj, :], p[:]), sem=DV)
                if oj >= 1:
                    b.wait("pe", DV, b.c[DV] - 1)
            b.wait("sync", DV, b.c[DV])
            b.dma("sync", "sOC", lambda e, hh=hh, c=c: e.dma_start(
                out=S["oc_s"].rearrange("(q p) t -> p q t", p=128)
                [:, 4 * hh:4 * hh + 4, 512 * c:512 * (c + 1)], in_=OCS[:, :, :]))

        def stream_chunk(c):
            b.dma("sync", "sFK", lambda e, c=c: e.dma_start(
                out=fk_c[:, :, :], in_=S["fkT_s"].rearrange("(ft p) t -> p ft t", p=128)
                [:, :, 512 * c:512 * (c + 1)]))

        def stream_head(c, hh, src, dst, sem):
            b.dma("sync", sem, lambda e, c=c, hh=hh, src=src, dst=dst: e.dma_start(
                out=dst[:, :, :], in_=src.rearrange("(ft p) t -> p ft t", p=128)
                [:, 4 * hh:4 * hh + 4, 512 * c:512 * (c + 1)]))

        load_base()
        b.dwait("pe", "sST")
        b.dwait("dve", "sST")
        rsv = S["rsend"].rearrange("(i p) e -> p i e", p=128)
        ccv = {}
        for c in range(2):
            b.wait("sync", PE, b.c[PE])
            b.wait("sync", DV, b.c[DV])
            stream_chunk(c)
            for hh in range(NFH):
                b.wait("sync", PE, b.c[PE])
                stream_head(c, hh, S["fvT_s"], fvh, "sFV")
                b.dwait("pe", "sFK")
                b.dwait("dve", "sFK")
                b.dwait("pe", "sFV")
                b.dwait("dve", "sFV")
                kpath(hh, c, upd=True)
                if c == 1:
                    for mi, m in enumerate(mats):
                        b.dwait("dve", "sRS")
                        for dj in range(4):
                            dve(lambda e, hh=hh, m=m, dj=dj: e.tensor_scalar_mul(
                                OCS[:, dj, :], ST(hh, m)[:, dj, :],
                                sm["gcols"][:, 2:3]), sem=DV)
                        b.wait("sync", DV, b.c[DV])
                        idx = (hh * 4 + mi) * 4
                        b.dma("sync", "sRS", lambda e, idx=idx: e.dma_start(
                            out=rsv[:, idx:idx + 4, :], in_=OCS[:, :, :]))
                    b.dwait("pool", "sRS")
                    r0 = hh * 4 * FHD
                    b.op("pool", lambda e, r0=r0: e.collective_compute(
                        "AllReduce", ALU.add, replica_groups=GROUPS,
                        ins=[S["rsend"][r0:r0 + 4 * FHD, :].opt()],
                        outs=[S["rgat"][r0:r0 + 4 * FHD, :].opt()]), sem=CC)
                    ccv[hh] = b.c[CC]

        load_base()
        b.dwait("dve", "sST")
        rgv = S["rgat"].rearrange("(i p) e -> p i e", p=128)
        for c in range(2):
            b.wait("sync", PE, b.c[PE])
            b.wait("sync", DV, b.c[DV])
            if c == 0:
                stream_chunk(c)
            for hh in range(NFH):
                if c == 0:
                    b.wait("sync", CC, ccv[hh])
                    for mi, m in enumerate(mats):
                        idx = (hh * 4 + mi) * 4
                        b.wait("sync", DV, b.c[DV])
                        b.dma("sync", "sRL", lambda e, idx=idx: e.dma_start(
                            out=fqh[:, :, :], in_=rgv[:, idx:idx + 4, :]))
                        b.dwait("dve", "sRL")
                        for dj in range(4):
                            dve(lambda e, hh=hh, m=m, dj=dj: e.tensor_scalar_mul(
                                ST(hh, m)[:, dj, :], ST(hh, m)[:, dj, :],
                                sm["gcols"][:, 0:1]))
                            dve(lambda e, hh=hh, m=m, dj=dj: e.scalar_tensor_tensor(
                                ST(hh, m)[:, dj, :], fqh[:, dj, :], sm["gcols"][:, 1:2],
                                ST(hh, m)[:, dj, :], ALU.mult, ALU.add), sem=DV)
                    b.wait("pe", DV, b.c[DV])
                b.wait("sync", PE, b.c[PE])
                b.wait("sync", DV, b.c[DV])
                stream_head(c, hh, S["fqT_s"], fqh, "sFQ")
                if c == 0:
                    stream_head(c, hh, S["fvT_s"], fvh, "sFV")
                b.dwait("pe", "sFK")
                b.dwait("dve", "sFK")
                b.dwait("pe", "sFQ")
                b.dwait("pe", "sFV")
                b.dwait("dve", "sFV")
                qpath(hh, c)
                if c == 0:
                    kpath(hh, c, upd=True)

    def phase_e():
        onst = arv(0, 16)
        ln2T = arv(0, 16)
        hdn = arv(16, 44)
        WB0, WB1 = 60, 76
        WBD = 92
        scr = ar(136, 4)
        att4 = arv(16, 4, 2048)
        xn = [A32[:, 2048 * i:2048 * (i + 1)] for i in range(4)]
        xsem = ["sX0", "sX1", "sX2", "sX3"]
        outt = A32[:, 8192:8704]
        xv = I["x_own"].rearrange("(tt p) f -> p tt f", p=128)
        av = S["att_s"].rearrange("(tt p) f -> p tt f", p=128)
        xnv = S["xnew_s"].rearrange("(tt p) f -> p tt f", p=128)
        yv = y_own.rearrange("(tt p) f -> p tt f", p=128)
        free_w = [b.c[PE]] * 2
        wsem = ["sW0", "sW1"]
        pe_prev = b.c[PE]
        bfree = [0] * 6
        afree = [0] * 6

        for tb in range(NTB):
            b.wait("sync", PE, pe_prev)
            b.wait("sync", DV, b.c[DV])
            b.dma("sync", "sON", lambda e, tb=tb: e.dma_start(
                out=onst[:, :, :], in_=S["oc_s"].rearrange("(q p) t -> p q t", p=128)
                [:, :, 512 * tb:512 * (tb + 1)]))
            b.dwait("dve", "sON")
            b.dwait("pe", "sON")
            for hh in range(NFH):
                for k in range(4):
                    i = 4 * hh + k
                    if k > 0:
                        b.wait("dve", PE, b.c[PE])
                    dve(lambda e, i=i: e.tensor_mul(scr[:, 0:512], onst[:, i, :],
                                                    onst[:, i, :]), sem=DV)
                    b.wait("pe", DV, b.c[DV])
                    mm(ps[5][0:1, 0:512], sm["onescol"][:], scr[:, 0:512],
                       k == 0, k == 3, sem=PE)
                b.wait("dve", PE, b.c[PE])
                dve(lambda e: e.tensor_copy(sm["row"][0:1, 2048:2560],
                                            ps[5][0:1, 0:512]), sem=DV)
                b.wait("act", DV, b.c[DV])
                rsqrt_row(512, 1.0 / FHD)
                bcast_row(512, ps[5], sm["rbc"])
                for k in range(4):
                    i = 4 * hh + k
                    dve(lambda e, i=i, k=k: e.scalar_tensor_tensor(
                        onst[:, i, :], onst[:, i, :], sm["tttnw"][:, k:k + 1],
                        sm["rbc"][:, 0:512], ALU.mult, ALU.mult), sem=DV)
            b.wait("pe", DV, b.c[DV])
            b.dwait("sync", "sXW")
            for tt in range(4):
                gt = 4 * tb + tt
                b.dma("sync", xsem[tt], lambda e, gt=gt, d=xn[tt]: e.dma_start(
                    out=d, in_=xv[:, gt, :]))
            b.wait("sync", DV, b.c[DV])
            b.dma("sync", "sAT", lambda e, tb=tb: e.dma_start(
                out=att4[:, :, :], in_=av[:, 4 * tb:4 * tb + 4, :]))
            for tt in range(4):
                b.dwait("dve", xsem[tt])
            b.dwait("dve", "sAT")
            for mc in range(4):
                b.wait("pool", PE, free_w[mc % 2])
                load_w((WB0, WB1)[mc % 2], I["wmoT"], H, 512 * mc, 512, wsem[mc % 2])
                b.dwait("pe", wsem[mc % 2])
                wv = arv((WB0, WB1)[mc % 2], 16)
                cs = slice(512 * mc, 512 * (mc + 1))
                for tt in range(4):
                    bi = (4 * mc + tt) % 3
                    bank = ps[bi]
                    b.wait("pe", DV, bfree[bi])
                    for ft in range(16):
                        mm(bank[:], onst[:, ft, 128 * tt:128 * (tt + 1)], wv[:, ft, :],
                           ft == 0, ft == 15, sem=PE if ft == 15 else None)
                    b.wait("dve", PE, b.c[PE])
                    dve(lambda e, cs=cs, d=xn[tt], p=bank: e.tensor_add(
                        d[:, cs], d[:, cs], p[:]))
                    dve(lambda e, cs=cs, d=xn[tt], tt=tt: e.tensor_add(
                        d[:, cs], d[:, cs], att4[:, tt, cs]), sem=DV)
                    bfree[bi] = b.c[DV]
                free_w[mc % 2] = b.c[PE]
            for tt in range(4):
                gt = 4 * tb + tt
                dve(lambda e, gt=gt: e.tensor_single_scalar(
                    sm["rscol"][:, gt:gt + 1], sm["cb"][:, 0:1], 0.0, ALU.mult))
                dve(lambda e, d=xn[tt], gt=gt: e.reduce_sum(
                    sm["mucol"][:, gt:gt + 1], d, axis=AX.X), sem=DV)
                dve(lambda e, d=xn[tt], gt=gt: e.scalar_tensor_tensor(
                    scr[:, 0:2048], d, 1.0, d, ALU.mult, ALU.mult,
                    accum_out=sm["rscol"][:, gt:gt + 1]), sem=DV)
                b.wait("sync", DV, b.c[DV])
                b.dma("sync", "sXW", lambda e, gt=gt, d=xn[tt]: e.dma_start(
                    out=xnv[:, gt, :], in_=d))
            pe_prev = b.c[PE]
        b.wait("act", DV, b.c[DV])
        act(lambda e: e.activation(sm["rscol"][:, 8:16], sm["mucol"][:, 0:8],
                                   AF.Square, scale=1.0 / H), sem=AC)
        b.wait("act", AC, b.c[AC])
        act(lambda e: e.activation(sm["mucol"][:, 8:16], sm["mucol"][:, 0:8],
                                   AF.Identity, scale=1.0 / H), sem=AC)
        b.wait("dve", AC, b.c[AC])
        dve(lambda e: e.scalar_tensor_tensor(
            sm["rscol"][:, 0:8], sm["rscol"][:, 0:8], 1.0 / H, sm["rscol"][:, 8:16],
            ALU.mult, ALU.subtract), sem=DV)
        b.wait("act", DV, b.c[DV])
        act(lambda e: e.activation(sm["rscol"][:, 0:8], sm["rscol"][:, 0:8], AF.Ln,
                                   bias=sm["cb"][:, 2:3]), sem=AC)
        b.wait("act", AC, b.c[AC])
        act(lambda e: e.activation(sm["rscol"][:, 0:8], sm["rscol"][:, 0:8], AF.Exp,
                                   scale=-0.5), sem=AC)
        act(lambda e: e.copy(sm["mucol"][:, 0:8], sm["mucol"][:, 8:16]), sem=AC)
        b.wait("dve", AC, b.c[AC])

        for tb in range(NTB):
            for tt in range(4):
                gt = 4 * tb + tt
                b.dwait("sync", "sYW")
                b.dma("sync", xsem[tt], lambda e, gt=gt, d=xn[tt]: e.dma_start(
                    out=d, in_=xnv[:, gt, :]))
                b.dwait("dve", xsem[tt])
                b.wait("dve", PE, b.c[PE])
                dve(lambda e, d=xn[tt], gt=gt: e.tensor_scalar(
                    scr[:, 0:2048], d, sm["mucol"][:, gt:gt + 1],
                    sm["rscol"][:, gt:gt + 1], ALU.subtract, ALU.mult), sem=DV)
                b.wait("pe", DV, b.c[DV])
                for ft in range(16):
                    tr(pslot(ft), scr[:, 128 * ft:128 * (ft + 1)], sem=PE)
                    b.wait("dve", PE, b.c[PE])
                    dve(lambda e, ft=ft, tt=tt: e.tensor_copy(
                        ln2T[:, ft, 128 * tt:128 * (tt + 1)], pslot(ft)), sem=DV)
                    b.wait("pe", DV, b.c[DV] - 1)
                b.wait("pe", DV, b.c[DV])
            for wc in range(11):
                b.wait("pool", PE, free_w[0])
                load_w(WB0, I["gT"], H, 512 * wc, 512, "sW0")
                b.wait("pool", PE, free_w[1])
                load_w(WB1, I["uT"], H, 512 * wc, 512, "sW1")
                b.dwait("pe", "sW0")
                b.dwait("pe", "sW1")
                gvw, uvw = arv(WB0, 16), arv(WB1, 16)
                for isub in range(4):
                    icol = 4 * wc + isub
                    gb = isub % 2
                    sl = sm["rbc2"][:, 512 * gb:512 * (gb + 1)]
                    b.wait("pe", AC, afree[gb])
                    b.wait("pe", DV, bfree[gb])
                    for ft in range(16):
                        mm(ps[gb][:], gvw[:, ft, 128 * isub:128 * (isub + 1)],
                           ln2T[:, ft, :], ft == 0, ft == 15, sem=PE if ft == 15 else None)
                    b.wait("act", PE, b.c[PE])
                    b.wait("act", DV, bfree[gb])
                    act(lambda e, p=ps[gb], icol=icol, sl=sl: e.activation(
                        sl, p[:], AF.Silu, scale=sm["gate_s"][:, icol:icol + 1]), sem=AC)
                    acv = b.c[AC]
                    afree[gb] = acv
                    b.wait("pe", DV, bfree[2 + gb])
                    for ft in range(16):
                        mm(ps[2 + gb][:], uvw[:, ft, 128 * isub:128 * (isub + 1)],
                           ln2T[:, ft, :], ft == 0, ft == 15, sem=PE if ft == 15 else None)
                    b.wait("dve", PE, b.c[PE])
                    b.wait("dve", AC, acv)
                    dve(lambda e, icol=icol, p=ps[2 + gb], sl=sl: e.scalar_tensor_tensor(
                        hdn[:, icol, :], p[:], sm["up_s"][:, icol:icol + 1],
                        sl, ALU.mult, ALU.mult), sem=DV)
                    bfree[gb] = b.c[DV]
                    bfree[2 + gb] = b.c[DV]
                free_w[0] = free_w[1] = b.c[PE]
            b.wait("pe", DV, b.c[DV])
            for mc in range(4):
                b.wait("pool", PE, free_w[0])
                load_w(WBD, I["dT"], INTER, 512 * mc, 512, "sWD", split=4)
                b.dwait("pe", "sWD")
                dwv = arv(WBD, 44)
                for tt in range(4):
                    bi = tt % 2
                    bank = ps[bi]
                    b.wait("pe", DV, bfree[bi])
                    b.wait("pe", AC, afree[bi])
                    for ki in range(44):
                        mm(bank[:], hdn[:, ki, 128 * tt:128 * (tt + 1)], dwv[:, ki, :],
                           ki == 0, ki == 43, sem=PE if ki == 43 else None)
                    b.wait("dve", PE, b.c[PE])
                    cs = slice(512 * mc, 512 * (mc + 1))
                    dve(lambda e, cs=cs, p=bank: e.tensor_mul(
                        outt[:, 0:512], p[:], sm["ds_bc"][:, cs]))
                    dve(lambda e, cs=cs, tt=tt: e.tensor_add(
                        xn[tt][:, cs], xn[tt][:, cs], outt[:, 0:512]), sem=DV)
                    bfree[bi] = b.c[DV]
                free_w[0] = b.c[PE]
            b.wait("sync", DV, b.c[DV])
            for tt in range(4):
                gt = 4 * tb + tt
                b.dma("sync", "sYW", lambda e, gt=gt, tt=tt: e.dma_start(
                    out=yv[:, gt, :], in_=xn[tt]))
        b.dwait("sync", "sYW")
        if DEBUG:
            b.dma("sync", "sYW", lambda e: e.dma_start(out=DBG["dbg_cols"],
                                                       in_=sm["dbgc"][:]))
            for nm, srcs in (("dbg_ks", "ksend"), ("dbg_oT", "oT_s"),
                             ("dbg_att", "att_s"), ("dbg_fq", "fqT_s"),
                             ("dbg_fk", "fkT_s"), ("dbg_fv", "fvT_s"),
                             ("dbg_oc", "oc_s"), ("dbg_xn", "xnew_s"),
                             ("dbg_rg", "rgat")):
                b.dma("sync", "sYW", lambda e, d=DBG[nm], s=S[srcs]: e.dma_start(
                    out=d, in_=s))
            b.dwait("sync", "sYW")

    nums = sorted(h.num for h in b.sems.values())
    lo, hi = nums[0], nums[-1]
    assert nums == list(range(lo, hi + 1)), nums
    b.raw("pool", lambda e, lo=lo, hi=hi: e.sem_clear(range(lo, hi + 1)))
    b.raw("pool", lambda e: e.sem_inc(b.sems["GO"], 1))
    b.c["GO"] = 1
    for _eng in ("pe", "dve", "act", "sync"):
        b.wait(_eng, "GO", 1)

    load_const()
    b.barrier()
    qres, kres, vres = phase_a()
    b.barrier()
    phase_b(qres, kres, vres)
    b.barrier()
    phase_c()
    b.barrier()
    phase_d()
    b.barrier()
    phase_e()



def kernel(**inp):
    x = np.ascontiguousarray(np.asarray(inp["L0_attn_hidden_states"], np.float32))
    Bn, Sq, _ = x.shape

    T = lambda a: np.ascontiguousarray(np.asarray(a, np.float32).T)
    shared = dict(
        wqkvT=T(inp["L0_attn_attn_qkv_w"]), woT=T(inp["L0_attn_attn_o_w"]),
        wmlpT=T(inp["L0_attn_mlp_qkv_w"]), lrwT=T(inp["L0_attn_lr_w"]),
        wmoT=T(inp["L0_attn_mlp_o_w"]),
        w0T=np.ascontiguousarray(np.asarray(inp["L0_attn_w0"], np.float32)
                                 .transpose(0, 2, 1)),
        w1om=np.ascontiguousarray(np.asarray(inp["L0_attn_w1"], np.float32)),
        w1T=np.ascontiguousarray(np.asarray(inp["L0_attn_w1"], np.float32)
                                 .transpose(0, 2, 1)),
        w2T=np.ascontiguousarray(np.asarray(inp["L0_attn_w2"], np.float32)
                                 .transpose(0, 2, 1)),
        gT=T(inp["L0_ffn_gate_w"]), uT=T(inp["L0_ffn_up_w"]), dT=T(inp["L0_ffn_down_w"]),
    )
    col = lambda a, n: np.ascontiguousarray(np.asarray(a, np.float32).reshape(n, 128).T)
    shared["qnw"] = col(inp["L0_attn_q_norm_w"], 16)
    shared["knw"] = col(inp["L0_attn_k_norm_w"], 16)
    shared["mlpnw"] = col(inp["L0_attn_mlp_norm_w"], 16)
    shared["tttnw"] = col(inp["L0_attn_ttt_norm_w"], 4)
    shared["gate_s"] = col(inp["L0_ffn_gate_s"], 44)
    shared["up_s"] = col(inp["L0_ffn_up_s"], 44)
    shared["down_row"] = np.asarray(inp["L0_ffn_down_s"], np.float32).reshape(1, -1)
    shared["lrb_row"] = np.asarray(inp["L0_attn_lr_b"], np.float32).reshape(1, -1)
    shared["eye_in"] = np.eye(128, dtype=np.float32)
    psw = np.zeros((128, 128), np.float32)
    for d in range(64):
        psw[d + 64, d] = -1.0
        psw[d, d + 64] = 1.0
    shared["psw_in"] = psw
    ii = np.arange(128)[:, None]
    cc = np.arange(128)[None, :]
    shared["mlo_in"] = (ii > cc).astype(np.float32)
    shared["mhi_in"] = (ii <= cc).astype(np.float32)

    invf = 1.0 / (500000.0 ** (np.arange(0, 128, 2, dtype=np.float64) / 128.0))
    fr = np.outer(np.arange(Sq, dtype=np.float64), invf)
    cosT = np.concatenate([np.cos(fr).T, np.cos(fr).T], axis=0).astype(np.float32)
    sinT = np.concatenate([np.sin(fr).T, np.sin(fr).T], axis=0).astype(np.float32)

    nc = build_program()

    in_maps = []
    for c in range(8):
        bq, p = c // 2, c % 2
        t0 = TOWN * p
        m = dict(shared)
        m["x_own"] = np.ascontiguousarray(x[bq, t0:t0 + TOWN])
        m["cosT_in"] = np.ascontiguousarray(cosT[:, t0:t0 + TOWN])
        m["sinT_in"] = np.ascontiguousarray(sinT[:, t0:t0 + TOWN])
        m["hb_in"] = np.full((128, 1), 0.0 if p == 1 else -1e30, np.float32)
        m["ga_in"] = np.array([[1.0 if p == 0 else 0.0]], np.float32)
        m["gb_in"] = np.array([[0.0 if p == 0 else 1.0]], np.float32)
        m["gout_in"] = np.array([[1.0 if p == 0 else 0.0]], np.float32)
        in_maps.append(m)

    res = run_bass_kernel_spmd(nc, in_maps, core_ids=list(range(8)))
    LAST_RESULTS[0] = res
    out = np.empty((Bn, Sq, H), np.float32)
    for c in range(8):
        bq, p = c // 2, c % 2
        out[bq, TOWN * p:TOWN * (p + 1)] = np.asarray(res.results[c]["y_own"])
    return out


# version 09
# speedup vs baseline: 1.1681x; 1.0168x over previous
import math
import contextlib
import numpy as np
import concourse.bass as bass
import concourse.mybir as mybir
from concourse.bass_utils import run_bass_kernel_spmd

F32 = mybir.dt.float32
BF16 = mybir.dt.bfloat16
AF = mybir.ActivationFunctionType
ALU = mybir.AluOpType
AX = mybir.AxisListType

H = 2048
NFH = 4
FHD = 512
INTER = 5632
TOWN = 1024
NTB = TOWN // 512
EPS_LN = 1e-5
EPS_RMS = 1e-6
BASE_LR_INV = 0.001 + float(np.log(-np.expm1(-0.001)))
ASCALE = 1.0 / math.sqrt(128.0)
GROUPS = [[0, 1], [2, 3], [4, 5], [6, 7]]

DEBUG = False
LAST_RESULTS = [None]

ENGINES = ("pe", "dve", "act", "sync", "pool")
U = 512

DMA_SEMS = ("sIN", "sX0", "sX1", "sX2", "sX3", "sWA", "sKS", "sKH", "sVH", "sOW",
            "sOT", "sW0", "sW1", "sLR", "sAW", "sFW", "sST", "sFK", "sFV", "sFQ",
            "sRS", "sRL", "sON", "sAT", "sWD", "sXW", "sYW", "sOC", "sIP")
ENG_SEMS = ("PE", "DV", "AC", "CC", "GO")


class B:
    def __init__(self, nc):
        self.nc = nc
        self.th = {e: [] for e in ENGINES}
        self.sems = {}
        self.c = {}
        self.wm = {e: {} for e in ENGINES}
        self.flush_fn = None
        self.last_flush = {}

    def add_sem(self, name, handle):
        self.sems[name] = handle
        self.c[name] = 0

    def raw(self, eng, fn):
        self.th[eng].append(fn)

    def op(self, eng, build, sem=None, n=1):
        if sem is not None:
            self.c[sem] += n
            s, k = self.sems[sem], n

            def th(e, b=build, s=s, k=k):
                b(e).then_inc(s, k)
        else:
            def th(e, b=build):
                b(e)
        self.th[eng].append(th)
        return self.c[sem] if sem is not None else None

    def dma(self, eng, sem, build):
        return self.op(eng, build, sem=sem, n=16)

    def wait(self, eng, sem, val):
        assert val <= self.c[sem], f"wait {sem}>={val} emitted={self.c[sem]}"
        if val <= 0 or self.wm[eng].get(sem, -1) >= val:
            return
        if self.flush_fn is not None and val == self.c[sem] and \
                ((sem == "DV" and eng != "dve") or (sem == "AC" and eng != "act")):
            if self.last_flush.get(sem, -1) < val:
                self.flush_fn[sem]()
                self.last_flush[sem] = self.c[sem]
            val = self.c[sem]
            if self.wm[eng].get(sem, -1) >= val:
                return
        self.wm[eng][sem] = val
        self.th[eng].append(lambda e, s=self.sems[sem], v=val: e.wait_ge(s, v))

    def dwait(self, eng, sem):
        self.wait(eng, sem, self.c[sem])

    def barrier(self):
        snap = dict(self.c)
        for eng in ENGINES:
            for s, v in snap.items():
                self.wait(eng, s, v)


def build_program():
    nc = bass.Bass("TRN2", target_bir_lowering=False,
                   detect_race_conditions=False)

    def din(name, shape, dt=F32):
        return nc.dram_tensor(name, shape, dt, kind="ExternalInput").ap()

    I = {}
    for name, shape in (
        ("x_own", [TOWN, H]), ("wqkvT", [H, 3 * H]), ("woT", [H, H]),
        ("wmlpT", [H, 3 * H]), ("lrwT", [H, 12]), ("wmoT", [H, H]),
        ("w0T", [NFH, FHD, FHD]), ("w1om", [NFH, FHD, FHD]),
        ("w1T", [NFH, FHD, FHD]), ("w2T", [NFH, FHD, FHD]),
        ("gT", [H, INTER]), ("uT", [H, INTER]), ("dT", [INTER, H]),
        ("qnw", [128, 16]), ("knw", [128, 16]), ("mlpnw", [128, 16]),
        ("tttnw", [128, NFH]), ("gate_s", [128, 44]), ("up_s", [128, 44]),
        ("down_row", [1, H]), ("lrb_row", [1, 12]), ("eye_in", [128, 128]),
        ("psw_in", [128, 128]), ("cosT_in", [128, TOWN]), ("sinT_in", [128, TOWN]),
        ("mlo_in", [128, 128]), ("mhi_in", [128, 128]), ("hb_in", [128, 1]),
        ("ga_in", [1, 1]), ("gb_in", [1, 1]), ("gout_in", [1, 1]),
    ):
        I[name] = din(name, shape)

    y_own = nc.dram_tensor("y_own", [TOWN, H], F32, kind="ExternalOutput").ap()
    DBG = {}
    if DEBUG:
        for nm, shape, dt in (("dbg_ks", [H, TOWN], BF16), ("dbg_oT", [H, TOWN], BF16),
                              ("dbg_att", [TOWN, H], BF16), ("dbg_fq", [H, TOWN], BF16),
                              ("dbg_fk", [H, TOWN], BF16), ("dbg_fv", [H, TOWN], BF16),
                              ("dbg_oc", [H, TOWN], BF16), ("dbg_xn", [TOWN, H], F32),
                              ("dbg_rg", [16 * FHD, FHD], BF16)):
            DBG[nm] = nc.dram_tensor(nm, shape, dt, kind="ExternalOutput").ap()
        DBG["dbg_scr"] = nc.dram_tensor("dbg_scr", [128, 2048], BF16,
                                        kind="ExternalOutput").ap()
        DBG["dbg_lnx"] = nc.dram_tensor("dbg_lnx", [128, 8192], BF16,
                                        kind="ExternalOutput").ap()
        DBG["dbg_cols"] = nc.dram_tensor("dbg_cols", [128, 8], F32,
                                         kind="ExternalOutput").ap()

    S = {}
    for name, shape, dt in (
        ("ksend0", [H, 512], BF16), ("ksend1", [H, 512], BF16),
        ("vsend0", [512, H], BF16), ("vsend1", [512, H], BF16),
        ("kgat0", [2 * H, 512], BF16), ("kgat1", [2 * H, 512], BF16),
        ("vgat0", [2 * 512, H], BF16), ("vgat1", [2 * 512, H], BF16),
        ("oT_s", [H, TOWN], BF16), ("att_s", [TOWN, H], BF16),
        ("fqT_s", [H, TOWN], BF16), ("fkT_s", [H, TOWN], BF16),
        ("fvT_s", [H, TOWN], BF16), ("oc_s", [H, TOWN], BF16),
        ("xnew_s", [TOWN, H], F32),
        ("rsend", [16 * FHD, FHD], BF16), ("rgat", [16 * FHD, FHD], BF16),
    ):
        S[name] = nc.dram_tensor(name, shape, dt, kind="Internal").ap()

    st = contextlib.ExitStack()
    with st:
        def sb(name, shape, dt):
            return st.enter_context(nc.sbuf_tensor(name, shape, dt))

        sm = {}
        sm["eye"] = sb("eye", [128, 128], BF16)
        sm["psw"] = sb("psw", [128, 128], BF16)
        sm["mlo"] = sb("mlo", [128, 128], BF16)
        sm["mhi"] = sb("mhi", [128, 128], BF16)
        sm["cosT"] = sb("cosT", [128, TOWN], BF16)
        sm["sinT"] = sb("sinT", [128, TOWN], BF16)
        sm["onescol"] = sb("onescol", [128, 1], BF16)
        sm["onesrow32"] = sb("onesrow32", [1, 128], F32)
        for n_, w_ in (("qnw", 16), ("knw", 16), ("mlpnw", 16), ("tttnw", NFH),
                       ("gate_s", 44), ("up_s", 44)):
            sm[n_] = sb("g_" + n_, [128, w_], F32)
        sm["ds_bc"] = sb("ds_bc", [128, H], F32)
        sm["lrb_bc"] = sb("lrb_bc", [128, 12], F32)
        sm["hbcol"] = sb("hbcol", [128, 1], F32)
        sm["gcols"] = sb("gcols", [128, 4], F32)
        sm["lr_tok"] = sb("lr_tok", [128, 8 * 12], F32)
        sm["mucol"] = sb("mucol", [128, 16], F32)
        sm["rscol"] = sb("rscol", [128, 16], F32)
        sm["row"] = sb("row", [1, 2560], F32)
        sm["tiny"] = sb("tiny", [1, 16], F32)
        sm["w12"] = sb("w12", [128, 16], F32)
        sm["s12"] = sb("s12", [128, 16], F32)
        sm["stat"] = sb("stat", [128, 16], F32)
        sm["rbc"] = sb("rbc", [128, 512], F32)
        sm["rbc2"] = sb("rbc2", [128, 1024], F32)
        sm["cb"] = sb("cb", [128, 3], F32)
        sm["dbgc"] = sb("dbgc", [128, 8], F32)

        AR = sb("AR", [128, 142 * U], BF16)
        A32 = sb("A32", [128, 8704], F32)

        ps = [st.enter_context(nc.psum_tensor(f"ps{i}", [128, 512], F32))
              for i in range(6)]
        pst = [st.enter_context(nc.psum_tensor("pstA", [128, 1024], BF16)),
               st.enter_context(nc.psum_tensor("pstB", [128, 1024], BF16))]

        bld = B(nc)
        for s_ in ENG_SEMS + DMA_SEMS:
            bld.add_sem(s_, st.enter_context(nc.semaphore(s_)))

        emit_all(bld, I, S, sm, AR, A32, ps, pst, y_own, DBG)

        blk = st.enter_context(nc.Block())
        for eng_name, key in (("tensor", "pe"), ("vector", "dve"), ("scalar", "act"),
                              ("sync", "sync"), ("gpsimd", "pool")):
            def body(e, lst=bld.th[key]):
                for t in lst:
                    t(e)
            getattr(blk, eng_name)(body)

    return nc


def emit_all(b, I, S, sm, AR, A32, ps, pst, y_own, DBG):
    PE, DV, AC, CC = "PE", "DV", "AC", "CC"

    def ar(u0, nu):
        assert u0 + nu <= 142
        return AR[:, u0 * U:(u0 + nu) * U]

    def arv(u0, ntiles, w=512):
        nu = (ntiles * w + U - 1) // U
        assert u0 + nu <= 142
        t = AR[:, u0 * U:u0 * U + ntiles * w]
        return t.rearrange("p (a c) -> p a c", c=w)

    def mm(out, lhsT, rhs, start, stop, sem=None):
        return b.op("pe", lambda e, o=out, l=lhsT, r=rhs, s1=start, s2=stop:
                    e.matmul(o, l, r, start=s1, stop=s2), sem=sem)

    def tr(out, in_, sem=None):
        return b.op("pe", lambda e, o=out, i=in_: e.transpose(o, i, sm["eye"][:]),
                    sem=sem)

    def pslot(i):
        bank = pst[i % 2]
        c0 = 128 * ((i // 2) % 4)
        return bank[:, c0:c0 + 128]

    def dve(fn, sem=None):
        return b.op("dve", fn, sem=sem)

    def act(fn, sem=None):
        return b.op("act", fn, sem=sem)

    def _dv_flush():
        b.op("dve", lambda e: e.tensor_copy(sm["dbgc"][:, 7:8], sm["cb"][:, 0:1]),
             sem="DV")

    def _ac_flush():
        b.op("act", lambda e: e.copy(sm["dbgc"][:, 6:7], sm["cb"][:, 0:1]), sem="AC")

    b.flush_fn = {"DV": _dv_flush, "AC": _ac_flush}

    def rsqrt_row(n, scale):
        r = sm["row"][0:1, 2048:2048 + n]
        act(lambda e: e.activation(r, r, AF.Ln, scale=scale, bias=sm["cb"][0:1, 0:1]),
            sem=AC)
        b.wait("act", AC, b.c[AC])
        return act(lambda e: e.activation(r, r, AF.Exp, scale=-0.5), sem=AC)

    def bcast_row(n, psbank, dst):
        b.wait("pe", AC, b.c[AC])
        b.wait("pe", DV, b.c[DV])
        mm(psbank[:, 0:n], sm["onesrow32"][:], sm["row"][0:1, 2048:2048 + n],
           True, True, sem=PE)
        b.wait("dve", PE, b.c[PE])
        dve(lambda e: e.tensor_copy(dst[:, 0:n], psbank[:, 0:n]), sem=DV)
        b.wait("pe", DV, b.c[DV])

    def load_w(dst_u, srcT, rows, col0, w, sem, split=2):
        nft = rows // 128
        dst = arv(dst_u, nft, w)
        view = srcT.rearrange("(ft p) c -> p ft c", p=128)
        step = (nft + split - 1) // split
        for s0 in range(0, nft, step):
            s1 = min(nft, s0 + step)
            b.dma("pool", sem, lambda e, a=s0, c=s1: e.dma_start(
                out=dst[:, a:c, :], in_=view[:, a:c, col0:col0 + w]))

    def load_const():
        for dst, src in ((sm["qnw"], I["qnw"]), (sm["knw"], I["knw"]),
                         (sm["mlpnw"], I["mlpnw"]), (sm["tttnw"], I["tttnw"]),
                         (sm["gate_s"], I["gate_s"]), (sm["up_s"], I["up_s"]),
                         (sm["hbcol"], I["hb_in"])):
            b.dma("sync", "sIN", lambda e, d=dst, s=src: e.dma_start(out=d[:], in_=s))
        b.dma("sync", "sIN", lambda e: e.dma_start(out=sm["row"][0:1, 0:H],
                                                   in_=I["down_row"]))
        b.dma("sync", "sIN", lambda e: e.dma_start(out=sm["tiny"][0:1, 0:12],
                                                   in_=I["lrb_row"]))
        b.dma("sync", "sIN", lambda e: e.dma_start(out=sm["tiny"][0:1, 12:13],
                                                   in_=I["ga_in"]))
        b.dma("sync", "sIN", lambda e: e.dma_start(out=sm["tiny"][0:1, 13:14],
                                                   in_=I["gb_in"]))
        b.dma("sync", "sIN", lambda e: e.dma_start(out=sm["tiny"][0:1, 14:15],
                                                   in_=I["gout_in"]))
        for dst, src in ((sm["eye"], I["eye_in"]), (sm["psw"], I["psw_in"]),
                         (sm["cosT"], I["cosT_in"]), (sm["sinT"], I["sinT_in"]),
                         (sm["mlo"], I["mlo_in"]), (sm["mhi"], I["mhi_in"])):
            b.dma("pool", "sIP", lambda e, d=dst, s=src: e.dma_start(out=d[:], in_=s))
        b.dwait("dve", "sIN")
        b.dwait("dve", "sIP")
        b.dwait("pe", "sIP")
        for nm in ("lr_tok", "mucol", "rscol", "w12", "s12",
                   "stat", "rbc", "rbc2"):
            dve(lambda e, nm=nm: e.memset(sm[nm][:], 0.0))
        dve(lambda e: e.memset(sm["onescol"][:], 1.0))
        dve(lambda e: e.memset(sm["onesrow32"][:], 1.0))
        dve(lambda e: e.memset(sm["cb"][:, 0:1], EPS_RMS))
        dve(lambda e: e.memset(sm["cb"][:, 1:2], 1.0))
        dve(lambda e: e.memset(sm["cb"][:, 2:3], EPS_LN))
        dve(lambda e: e.tensor_scalar_add(sm["row"][0:1, 2048:2060],
                                          sm["tiny"][0:1, 0:12], BASE_LR_INV), sem=DV)
        b.wait("pe", DV, b.c[DV])
        b.dwait("pe", "sIN")
        for j in range(4):
            mm(ps[j % 2][:], sm["onesrow32"][:], sm["row"][0:1, 512 * j:512 * (j + 1)],
               True, True, sem=PE)
            b.wait("dve", PE, b.c[PE])
            dve(lambda e, j=j, p=ps[j % 2]: e.tensor_copy(
                sm["ds_bc"][:, 512 * j:512 * (j + 1)], p[:]), sem=DV)
            b.wait("pe", DV, b.c[DV])
        b.wait("act", DV, b.c[DV])
        act(lambda e: e.activation(sm["s12"][:, 0:1], sm["cb"][:, 1:2], AF.Ln), sem=AC)
        b.wait("act", AC, b.c[AC])
        act(lambda e: e.activation(sm["s12"][:, 0:1], sm["cb"][:, 1:2], AF.Exp), sem=AC)
        b.wait("act", AC, b.c[AC])
        act(lambda e: e.activation(sm["s12"][:, 0:1], sm["cb"][:, 1:2], AF.Square),
            sem=AC)
        b.wait("act", AC, b.c[AC])
        act(lambda e: e.activation(sm["s12"][:, 0:1], sm["cb"][:, 1:2], AF.Identity),
            sem=AC)
        mm(ps[0][:, 0:12], sm["onesrow32"][:], sm["row"][0:1, 2048:2060], True, True,
           sem=PE)
        mm(ps[1][:, 0:3], sm["onesrow32"][:], sm["tiny"][0:1, 12:15], True, True, sem=PE)
        b.wait("dve", PE, b.c[PE])
        dve(lambda e: e.tensor_copy(sm["lrb_bc"][:], ps[0][:, 0:12]))
        dve(lambda e: e.tensor_copy(sm["gcols"][:, 0:3], ps[1][:, 0:3]), sem=DV)
        b.wait("pe", DV, b.c[DV])

    def phase_a():
        lnxT = arv(0, 16)
        scr = ar(16, 4)
        WB = 20
        qres = arv(46, 16, 1024)
        kres = arv(78, 16, 1024)
        vres = arv(110, 8, 2048)
        xt = [A32[:, 0:2048], A32[:, 2048:4096]]
        xsem = ["sX0", "sX1"]
        xgate = [0, 0]
        stt_ = sm["stat"]
        xv = I["x_own"].rearrange("(tt p) f -> p tt f", p=128)
        free_w = b.c[PE]
        bfree = [0, 0, 0]

        for gt0 in range(2):
            b.dma("sync", xsem[gt0 % 2], lambda e, gt0=gt0, d=xt[gt0 % 2]: e.dma_start(
                out=d, in_=xv[:, gt0, :]))
        for tb in range(NTB):
            for tt in range(4):
                gt = tb * 4 + tt
                b.dwait("act", xsem[tt % 2])
                b.dwait("dve", xsem[tt % 2])
                b.wait("dve", AC, b.c[AC])
                dve(lambda e: e.tensor_single_scalar(stt_[:, 1:2], sm["cb"][:, 0:1],
                                                     0.0, ALU.mult))
                dve(lambda e, x=xt[tt % 2]: e.reduce_sum(stt_[:, 0:1], x, axis=AX.X),
                    sem=DV)
                b.wait("act", DV, b.c[DV])
                act(lambda e, x=xt[tt % 2]: e.activation(
                    scr, x, AF.Square, accum_out=stt_[:, 1:2]), sem=AC)
                act(lambda e: e.activation(stt_[:, 2:3], stt_[:, 0:1], AF.Square,
                                           scale=1.0 / H), sem=AC)
                b.wait("act", AC, b.c[AC])
                act(lambda e: e.activation(stt_[:, 6:7], stt_[:, 0:1], AF.Identity,
                                           scale=1.0 / H), sem=AC)
                b.wait("dve", AC, b.c[AC])
                dve(lambda e: e.scalar_tensor_tensor(
                    stt_[:, 3:4], stt_[:, 1:2], 1.0 / H, stt_[:, 2:3],
                    ALU.mult, ALU.subtract), sem=DV)
                b.wait("act", DV, b.c[DV])
                act(lambda e: e.activation(stt_[:, 4:5], stt_[:, 3:4], AF.Ln,
                                           bias=sm["cb"][:, 2:3]), sem=AC)
                b.wait("act", AC, b.c[AC])
                act(lambda e: e.activation(stt_[:, 5:6], stt_[:, 4:5], AF.Exp,
                                           scale=-0.5), sem=AC)
                b.wait("dve", AC, b.c[AC])
                dve(lambda e, x=xt[tt % 2]: e.tensor_scalar(
                    scr, x, stt_[:, 6:7], stt_[:, 5:6], ALU.subtract, ALU.mult), sem=DV)
                if gt + 2 < 4 * NTB:
                    b.wait("sync", DV, b.c[DV])
                    b.dma("sync", xsem[tt % 2], lambda e, gt=gt, d=xt[tt % 2]:
                          e.dma_start(out=d, in_=xv[:, gt + 2, :]))
                if DEBUG and tb == 0 and tt == 0:
                    dve(lambda e: e.tensor_copy(sm["dbgc"][:, 0:6], stt_[:, 0:6]),
                        sem=DV)
                    b.wait("sync", DV, b.c[DV])
                    b.dma("sync", "sKS", lambda e: e.dma_start(
                        out=DBG["dbg_scr"], in_=scr))
                xgate[tt % 2] = b.c[DV]
                b.wait("pe", DV, b.c[DV])
                for ft in range(16):
                    tr(pslot(ft), scr[:, 128 * ft:128 * (ft + 1)], sem=PE)
                    b.wait("dve", PE, b.c[PE])
                    dve(lambda e, ft=ft, tt=tt: e.tensor_copy(
                        lnxT[:, ft, 128 * tt:128 * (tt + 1)], pslot(ft)), sem=DV)
                    b.wait("pe", DV, b.c[DV] - 1)
                b.wait("pe", DV, b.c[DV])

            if DEBUG and tb == 0:
                b.wait("sync", DV, b.c[DV])
                b.dma("sync", "sKS", lambda e: e.dma_start(
                    out=DBG["dbg_lnx"], in_=AR[:, 0:8192]))
            def do_wc(wc):
                nonlocal free_w
                b.wait("pool", PE, free_w)
                load_w(WB, I["wqkvT"], H, 512 * wc, 512, "sWA")
                b.dwait("pe", "sWA")
                wv = arv(WB, 16)
                if wc < 8:
                    dst = qres if wc < 4 else kres
                    for osub in range(4):
                        head = (wc % 4) * 4 + osub
                        bi = (4 * wc + osub) % 3
                        bank = ps[bi]
                        b.wait("pe", DV, bfree[bi])
                        for ft in range(16):
                            mm(bank[:], wv[:, ft, 128 * osub:128 * (osub + 1)],
                               lnxT[:, ft, :], ft == 0, ft == 15,
                               sem=PE if ft == 15 else None)
                        b.wait("dve", PE, b.c[PE])
                        dve(lambda e, d=dst, hd=head, tb=tb, p=bank: e.tensor_copy(
                            d[:, hd, 512 * tb:512 * (tb + 1)], p[:]), sem=DV)
                        bfree[bi] = b.c[DV]
                else:
                    for tt in range(4):
                        bi = (4 * wc + tt) % 3
                        bank = ps[bi]
                        b.wait("pe", DV, bfree[bi])
                        for ft in range(16):
                            mm(bank[:], lnxT[:, ft, 128 * tt:128 * (tt + 1)],
                               wv[:, ft, :], ft == 0, ft == 15,
                               sem=PE if ft == 15 else None)
                        b.wait("dve", PE, b.c[PE])
                        kt = tb * 4 + tt
                        dve(lambda e, kt=kt, wc=wc, p=bank: e.tensor_copy(
                            vres[:, kt, 512 * (wc - 8):512 * (wc - 7)], p[:]), sem=DV)
                        bfree[bi] = b.c[DV]
                free_w = b.c[PE]

            for wc in range(4, 12):
                do_wc(wc)
            b.wait("pe", DV, b.c[DV])

            def rope_body(res, gam):
                for ft in range(16):
                    if ft >= 4:
                        b.wait("dve", PE, b.c[PE] - 3)
                    dve(lambda e, ft=ft, r=res, tb=tb: e.tensor_mul(
                        scr[:, 512 * (ft % 4):512 * (ft % 4 + 1)],
                        r[:, ft, 512 * tb:512 * (tb + 1)],
                        r[:, ft, 512 * tb:512 * (tb + 1)]), sem=DV)
                    b.wait("pe", DV, b.c[DV])
                    mm(ps[3][0:1, 0:512], sm["onescol"][:],
                       scr[:, 512 * (ft % 4):512 * (ft % 4 + 1)], ft == 0, ft == 15,
                       sem=PE)
                b.wait("dve", PE, b.c[PE])
                dve(lambda e: e.tensor_copy(sm["row"][0:1, 2048:2560], ps[3][0:1, 0:512]),
                    sem=DV)
                b.wait("act", DV, b.c[DV])
                rsqrt_row(512, 1.0 / H)
                bcast_row(512, ps[3], sm["rbc"])
                for ft in range(16):
                    cs = slice(512 * tb, 512 * (tb + 1))
                    dve(lambda e, r=res, ft=ft, gm=gam, cs=cs: e.scalar_tensor_tensor(
                        scr[:, 0:512], r[:, ft, cs], gm[:, ft:ft + 1], sm["rbc"][:, 0:512],
                        ALU.mult, ALU.mult), sem=DV)
                    b.wait("pe", DV, b.c[DV])
                    mm(ps[4][:], sm["psw"][:], scr[:, 0:512], True, True, sem=PE)
                    b.wait("dve", PE, b.c[PE])
                    dve(lambda e, cs=cs: e.tensor_mul(
                        scr[:, 512:1024], scr[:, 0:512], sm["cosT"][:, cs]))
                    dve(lambda e, cs=cs: e.tensor_mul(
                        scr[:, 1024:1536], ps[4][:], sm["sinT"][:, cs]), sem=DV)
                    b.wait("pe", DV, b.c[DV])
                    dve(lambda e, ft=ft, r=res, cs=cs: e.tensor_add(
                        r[:, ft, cs], scr[:, 512:1024], scr[:, 1024:1536]), sem=DV)
                b.wait("pe", DV, b.c[DV])

            rope_body(kres, sm["knw"])
            b.wait("sync", DV, b.c[DV])
            b.dma("sync", "sKS", lambda e, tb=tb: e.dma_start(
                out=S[f"ksend{tb}"].rearrange("(ft p) t -> p ft t", p=128),
                in_=kres[:, :, 512 * tb:512 * (tb + 1)]))
            b.dma("sync", "sKS", lambda e, tb=tb: e.dma_start(
                out=S[f"vsend{tb}"].rearrange("(kt p) f -> p kt f", p=128),
                in_=vres[:, 4 * tb:4 * tb + 4, :]))
            b.dwait("pool", "sKS")
            b.op("pool", lambda e, tb=tb: e.collective_compute(
                "AllGather", ALU.bypass, replica_groups=GROUPS,
                ins=[S[f"ksend{tb}"].opt()], outs=[S[f"kgat{tb}"].opt()]), sem=CC)
            b.op("pool", lambda e, tb=tb: e.collective_compute(
                "AllGather", ALU.bypass, replica_groups=GROUPS,
                ins=[S[f"vsend{tb}"].opt()], outs=[S[f"vgat{tb}"].opt()]), sem=CC)

            for wc in range(4):
                do_wc(wc)
            b.wait("pe", DV, b.c[DV])
            rope_body(qres, sm["qnw"])

        return qres, kres, vres

    def phase_b(qres, kres, vres):
        kh = ar(0, 2)
        vh = arv(2, 8, 128)
        ptil = ar(4, 9)
        sqb = ar(14, 4)
        oTh = ar(18, 2)
        b.wait("sync", CC, b.c[CC])
        khg = [S[f"kgat{i}"][0:H, :].rearrange("(ft p) t -> p ft t", p=128)
               for i in range(2)]
        vhg = [S[f"vgat{i}"][0:512, :].rearrange("(kt p) f -> p kt f", p=128)
               for i in range(2)]

        pieces = []
        for j in range(12):
            s_lo, s_hi = max(0, j - 7), min(3, j - 1)
            if s_hi >= s_lo:
                pieces.append((j, 128 * s_lo, 128 * (s_hi - s_lo + 1), None))
            if j <= 3:
                pieces.append((j, 128 * j, 128, "lo"))
            if j >= 8:
                pieces.append((j, 128 * (j - 8), 128, "hi"))
        ppos, poff = [], 0
        for (j, c0, n, msk) in pieces:
            ppos.append((j, c0, n, msk, poff))
            poff += n
        assert poff == 4608

        pe_prev_head = b.c[PE]
        dv_prev_head = b.c[DV]
        for h in range(16):
            b.wait("sync", PE, pe_prev_head)
            b.wait("sync", DV, dv_prev_head)
            b.dwait("sync", "sOW")
            for half in range(2):
                b.dma("sync", "sKH", lambda e, h=h, half=half: e.dma_start(
                    out=kh[:, 512 * half:512 * (half + 1)], in_=khg[half][:, h, :]))
                b.dma("sync", "sVH", lambda e, h=h, half=half: e.dma_start(
                    out=vh[:, 4 * half:4 * half + 4, :],
                    in_=vhg[half][:, 0:4, 128 * h:128 * (h + 1)]))
            b.dwait("dve", "sKH")
            b.dwait("dve", "sVH")
            b.dwait("pe", "sKH")
            b.dwait("pe", "sVH")
            for idx4 in range(4):
                ksrc = kh[:, 512 * idx4:512 * (idx4 + 1)] if idx4 < 2 else \
                    kres[:, h, :][:, 512 * (idx4 - 2):512 * (idx4 - 1)]
                b.wait("dve", PE, b.c[PE])
                dve(lambda e, ksrc=ksrc: e.tensor_mul(sqb[:, 0:512], ksrc, ksrc), sem=DV)
                b.wait("pe", DV, b.c[DV])
                mm(ps[3][0:1, 0:512], sm["onescol"][:], sqb[:, 0:512], True, True, sem=PE)
                b.wait("dve", PE, b.c[PE])
                dve(lambda e, o=512 * idx4: e.tensor_copy(
                    sm["row"][0:1, o:o + 512], ps[3][0:1, 0:512]), sem=DV)

            for qb in range(NTB):
                q0 = 512 * qb
                b.wait("dve", PE, b.c[PE])
                dve(lambda e, h=h, q0=q0: e.tensor_mul(
                    sqb[:, 0:512], qres[:, h, :][:, q0:q0 + 512],
                    qres[:, h, :][:, q0:q0 + 512]), sem=DV)
                b.wait("pe", DV, b.c[DV])
                mm(ps[3][0:1, 0:512], sm["onescol"][:], sqb[:, 0:512], True, True, sem=PE)
                b.wait("dve", PE, b.c[PE])
                dve(lambda e: e.tensor_copy(sm["row"][0:1, 2048:2560], ps[3][0:1, 0:512]),
                    sem=DV)
                dve(lambda e: e.reduce_max(sm["tiny"][0:1, 4:5],
                                           sm["row"][0:1, 2048:2560], axis=AX.X))
                dve(lambda e, q0=q0: e.reduce_max(
                    sm["tiny"][0:1, 5:6], sm["row"][0:1, q0:q0 + 1536], axis=AX.X),
                    sem=DV)
                b.wait("act", DV, b.c[DV])
                act(lambda e: e.activation(sm["tiny"][0:1, 6:7], sm["tiny"][0:1, 4:5],
                                           AF.Identity, scale=sm["tiny"][0:1, 5:6]),
                    sem=AC)
                b.wait("act", AC, b.c[AC])
                act(lambda e: e.activation(sm["tiny"][0:1, 7:8], sm["tiny"][0:1, 6:7],
                                           AF.Ln), sem=AC)
                b.wait("act", AC, b.c[AC])
                act(lambda e: e.activation(sm["tiny"][0:1, 9:10], sm["tiny"][0:1, 7:8],
                                           AF.Exp, scale=0.5), sem=AC)
                b.wait("dve", AC, b.c[AC])
                dve(lambda e: e.tensor_scalar_mul(sm["tiny"][0:1, 10:11],
                                                  sm["tiny"][0:1, 9:10], -ASCALE),
                    sem=DV)
                b.wait("pe", DV, b.c[DV])
                mm(ps[3][:, 0:1], sm["onesrow32"][:], sm["tiny"][0:1, 10:11], True, True,
                   sem=PE)
                b.wait("dve", PE, b.c[PE])
                dve(lambda e: e.tensor_copy(sm["stat"][:, 3:4], ps[3][:, 0:1]), sem=DV)
                b.wait("act", DV, b.c[DV])
                act(lambda e: e.activation(sm["stat"][:, 2:3], sm["stat"][:, 3:4],
                                           AF.Identity, bias=sm["hbcol"][:]), sem=AC)
                b.wait("act", AC, b.c[AC])

                def kT_src(j, qb=qb, h=h):
                    kk0 = 512 * qb + 128 * j
                    if kk0 < 1024:
                        return kh[:, kk0:kk0 + 128]
                    return kres[:, h, :][:, kk0 - 1024:kk0 - 896]

                def v_src(j, qb=qb, h=h):
                    kk0 = 512 * qb + 128 * j
                    if kk0 < 1024:
                        return vh[:, kk0 // 128, :]
                    return vres[:, (kk0 - 1024) // 128, 128 * h:128 * (h + 1)]

                bank_free = [b.c[AC]] * 4
                bi = 0
                i0 = 0
                while i0 < len(ppos):
                    used, grp = 0, []
                    while i0 < len(ppos) and used + ppos[i0][2] <= 512:
                        grp.append(ppos[i0])
                        used += ppos[i0][2]
                        i0 += 1
                    bank = ps[bi]
                    b.wait("pe", AC, bank_free[bi])
                    ro = 0
                    for gi, (j, c0, n, msk, po) in enumerate(grp):
                        mm(bank[:, ro:ro + n], kT_src(j),
                           qres[:, h, :][:, q0 + c0:q0 + c0 + n], True, True,
                           sem=PE if gi == len(grp) - 1 else None)
                        ro += n
                    b.wait("act", PE, b.c[PE])
                    ro = 0
                    for (j, c0, n, msk, po) in grp:
                        halo = (512 * qb + 128 * j) < 1024
                        bias = sm["stat"][:, 2:3] if halo else sm["stat"][:, 3:4]
                        act(lambda e, ro=ro, n=n, po=po, bias=bias, bank=bank:
                            e.activation(ptil[:, po:po + n], bank[:, ro:ro + n], AF.Exp,
                                         scale=ASCALE, bias=bias), sem=AC)
                        ro += n
                    bank_free[bi] = b.c[AC]
                    bi = (bi + 1) % 4
                av = b.c[AC]
                b.wait("dve", AC, av)
                for (j, c0, n, msk, po) in ppos:
                    if msk is None:
                        continue
                    m = sm["mlo"] if msk == "lo" else sm["mhi"]
                    dve(lambda e, po=po, m=m: e.tensor_mul(
                        ptil[:, po:po + 128], ptil[:, po:po + 128], m[:]), sem=DV)
                dvv = b.c[DV]
                b.wait("pe", DV, dvv)
                b.wait("pe", AC, av)
                for pi, (j, c0, n, msk, po) in enumerate(ppos):
                    mm(ps[4][:, c0:c0 + n], v_src(j), ptil[:, po:po + n],
                       pi == 0, pi == len(ppos) - 1)
                for pi, (j, c0, n, msk, po) in enumerate(ppos):
                    mm(ps[5][0:1, c0:c0 + n], sm["onescol"][:], ptil[:, po:po + n],
                       pi == 0, pi == len(ppos) - 1,
                       sem=PE if pi == len(ppos) - 1 else None)
                b.wait("dve", PE, b.c[PE])
                dve(lambda e: e.tensor_scalar_add(sm["row"][0:1, 2048:2560],
                                                  ps[5][0:1, 0:512], 1e-38))
                dve(lambda e: e.reciprocal(sm["row"][0:1, 2048:2560],
                                           sm["row"][0:1, 2048:2560]), sem=DV)
                b.wait("pe", DV, b.c[DV])
                mm(ps[5][:, 0:512], sm["onesrow32"][:], sm["row"][0:1, 2048:2560],
                   True, True, sem=PE)
                b.wait("dve", PE, b.c[PE])
                dve(lambda e: e.tensor_copy(sm["rbc"][:], ps[5][:, 0:512]), sem=DV)
                dve(lambda e, qb=qb: e.tensor_mul(
                    oTh[:, 512 * qb:512 * (qb + 1)], ps[4][:, 0:512], sm["rbc"][:]),
                    sem=DV)
                b.wait("pe", DV, b.c[DV])
            pe_prev_head = b.c[PE]
            dv_prev_head = b.c[DV]
            b.wait("sync", DV, b.c[DV])
            b.dma("sync", "sOW", lambda e, h=h: e.dma_start(
                out=S["oT_s"].rearrange("(ft p) t -> p ft t", p=128)[:, h, :], in_=oTh))

    def phase_c():
        oTst = arv(0, 16)
        attn_sb = arv(16, 4, 2048)
        x2T = arv(32, 16)
        WB0, WB1 = 48, 64
        fst = arv(80, 48)
        scr = ar(128, 4)
        lrw_sb = arv(132, 16, 12)
        b.dma("pool", "sLR", lambda e: e.dma_start(
            out=lrw_sb[:, :, :], in_=I["lrwT"].rearrange("(ft p) c -> p ft c", p=128)))
        free_w = [b.c[PE]] * 2
        pe_prev = b.c[PE]
        wsem = ["sW0", "sW1"]
        bfree = [0] * 6
        afree = [0] * 6

        for tb in range(NTB):
            b.wait("sync", PE, pe_prev)
            b.dma("sync", "sOT", lambda e, tb=tb: e.dma_start(
                out=oTst[:, :, :], in_=S["oT_s"].rearrange("(ft p) t -> p ft t", p=128)
                [:, :, 512 * tb:512 * (tb + 1)]))
            b.dwait("pe", "sOT")
            b.dwait("dve", "sAW")
            for oc in range(4):
                b.wait("pool", PE, free_w[oc % 2])
                load_w((WB0, WB1)[oc % 2], I["woT"], H, 512 * oc, 512, wsem[oc % 2])
                b.dwait("pe", wsem[oc % 2])
                wv = arv((WB0, WB1)[oc % 2], 16)
                for tt in range(4):
                    bi = (4 * oc + tt) % 3
                    bank = ps[bi]
                    b.wait("pe", DV, bfree[bi])
                    b.wait("pe", AC, afree[bi])
                    for ft in range(16):
                        mm(bank[:], oTst[:, ft, 128 * tt:128 * (tt + 1)], wv[:, ft, :],
                           ft == 0, ft == 15, sem=PE if ft == 15 else None)
                    b.wait("dve", PE, b.c[PE])
                    dve(lambda e, tt=tt, oc=oc, p=bank: e.tensor_copy(
                        attn_sb[:, tt, 512 * oc:512 * (oc + 1)], p[:]), sem=DV)
                    bfree[bi] = b.c[DV]
                free_w[oc % 2] = b.c[PE]
            b.wait("sync", DV, b.c[DV])
            b.dma("sync", "sAW", lambda e, tb=tb: e.dma_start(
                out=S["att_s"].rearrange("(tt p) f -> p tt f", p=128)
                [:, 4 * tb:4 * tb + 4, :], in_=attn_sb[:, :, :]))
            dve(lambda e: e.tensor_single_scalar(sm["stat"][:, 0:4], sm["qnw"][:, 0:4],
                                                 0.0, ALU.mult), sem=DV)
            for tt in range(4):
                dve(lambda e, tt=tt: e.scalar_tensor_tensor(
                    scr[:, 0:2048], attn_sb[:, tt, :], 1.0, attn_sb[:, tt, :],
                    ALU.mult, ALU.mult, accum_out=sm["stat"][:, tt:tt + 1]), sem=DV)
            b.wait("act", DV, b.c[DV])
            act(lambda e: e.activation(sm["stat"][:, 4:8], sm["stat"][:, 0:4], AF.Ln,
                                       scale=1.0 / H, bias=sm["cb"][:, 0:1]), sem=AC)
            b.wait("act", AC, b.c[AC])
            act(lambda e: e.activation(sm["stat"][:, 4:8], sm["stat"][:, 4:8], AF.Exp,
                                       scale=-0.5), sem=AC)
            b.wait("dve", AC, b.c[AC])
            for tt in range(4):
                dve(lambda e, tt=tt: e.tensor_scalar_mul(
                    scr[:, 0:2048], attn_sb[:, tt, :], sm["stat"][:, 4 + tt:5 + tt]),
                    sem=DV)
                b.wait("pe", DV, b.c[DV])
                for ft in range(16):
                    tr(pslot(ft), scr[:, 128 * ft:128 * (ft + 1)], sem=PE)
                    b.wait("dve", PE, b.c[PE])
                    dve(lambda e, ft=ft, tt=tt: e.tensor_scalar_mul(
                        x2T[:, ft, 128 * tt:128 * (tt + 1)], pslot(ft),
                        sm["mlpnw"][:, ft:ft + 1]), sem=DV)
                    b.wait("pe", DV, b.c[DV] - 1)
                b.wait("pe", DV, b.c[DV])

            b.dwait("act", "sFW")
            for wc in range(12):
                b.wait("pool", PE, free_w[wc % 2])
                load_w((WB0, WB1)[wc % 2], I["wmlpT"], H, 512 * wc, 512, wsem[wc % 2])
                b.dwait("pe", wsem[wc % 2])
                wv = arv((WB0, WB1)[wc % 2], 16)
                for osub in range(4):
                    bi = (4 * wc + osub) % 3
                    bank = ps[bi]
                    b.wait("pe", AC, afree[bi])
                    b.wait("pe", DV, bfree[bi])
                    for ft in range(16):
                        mm(bank[:], wv[:, ft, 128 * osub:128 * (osub + 1)], x2T[:, ft, :],
                           ft == 0, ft == 15, sem=PE if ft == 15 else None)
                    b.wait("act", PE, b.c[PE])
                    act(lambda e, i=4 * wc + osub, p=bank: e.activation(
                        fst[:, i, :], p[:], AF.Silu), sem=AC)
                    afree[bi] = b.c[AC]
                free_w[wc % 2] = b.c[PE]
            ac_silu = b.c[AC]
            b.wait("dve", AC, ac_silu)

            for base in (0, 16):
                for hh in range(NFH):
                    for k in range(4):
                        i = base + 4 * hh + k
                        if k > 0:
                            b.wait("dve", PE, b.c[PE])
                        dve(lambda e, i=i: e.tensor_mul(scr[:, 0:512], fst[:, i, :],
                                                        fst[:, i, :]), sem=DV)
                        b.wait("pe", DV, b.c[DV])
                        mm(ps[3][0:1, 0:512], sm["onescol"][:], scr[:, 0:512],
                           k == 0, k == 3, sem=PE)
                    b.wait("dve", PE, b.c[PE])
                    dve(lambda e: e.tensor_copy(sm["row"][0:1, 2048:2560],
                                                ps[3][0:1, 0:512]), sem=DV)
                    b.wait("act", DV, b.c[DV])
                    act(lambda e: e.activation(sm["row"][0:1, 2048:2560],
                                               sm["row"][0:1, 2048:2560], AF.Ln), sem=AC)
                    b.wait("act", AC, b.c[AC])
                    act(lambda e: e.activation(sm["row"][0:1, 2048:2560],
                                               sm["row"][0:1, 2048:2560], AF.Exp,
                                               scale=0.5), sem=AC)
                    b.wait("dve", AC, b.c[AC])
                    dve(lambda e: e.tensor_scalar_max(sm["row"][0:1, 2048:2560],
                                                      sm["row"][0:1, 2048:2560], 1e-6))
                    dve(lambda e: e.reciprocal(sm["row"][0:1, 2048:2560],
                                               sm["row"][0:1, 2048:2560]), sem=DV)
                    bcast_row(512, ps[3], sm["rbc"])
                    for k in range(4):
                        i = base + 4 * hh + k
                        dve(lambda e, i=i: e.tensor_mul(fst[:, i, :], fst[:, i, :],
                                                        sm["rbc"][:, 0:512]), sem=DV)
            b.wait("sync", DV, b.c[DV])
            b.wait("sync", AC, ac_silu)
            for (dst_s, lo) in ((S["fqT_s"], 0), (S["fkT_s"], 16), (S["fvT_s"], 32)):
                b.dma("sync", "sFW", lambda e, d=dst_s, lo=lo, tb=tb: e.dma_start(
                    out=d.rearrange("(ft p) t -> p ft t", p=128)
                    [:, :, 512 * tb:512 * (tb + 1)], in_=fst[:, lo:lo + 16, :]))

            b.dwait("pe", "sLR")
            for tt in range(4):
                gt = 4 * tb + tt
                b.wait("pe", DV, b.c[DV])
                for ft in range(16):
                    mm(ps[4][:, 0:12], x2T[:, ft, 128 * tt:128 * (tt + 1)],
                       lrw_sb[:, ft, :], ft == 0, ft == 15, sem=PE if ft == 15 else None)
                b.wait("dve", PE, b.c[PE])
                dve(lambda e: e.tensor_add(sm["w12"][:, 0:12], ps[4][:, 0:12],
                                           sm["lrb_bc"][:]), sem=DV)
                b.wait("act", DV, b.c[DV])
                act(lambda e: e.activation(sm["s12"][:, 0:12], sm["w12"][:, 0:12],
                                           AF.Abs), sem=AC)
                b.wait("act", AC, b.c[AC])
                act(lambda e: e.activation(sm["s12"][:, 0:12], sm["s12"][:, 0:12],
                                           AF.Exp, scale=-1.0), sem=AC)
                b.wait("act", AC, b.c[AC])
                act(lambda e: e.activation(sm["s12"][:, 0:12], sm["s12"][:, 0:12],
                                           AF.Ln, bias=sm["cb"][:, 1:2]), sem=AC)
                b.wait("act", AC, b.c[AC])
                act(lambda e: e.activation(sm["w12"][:, 0:12], sm["w12"][:, 0:12],
                                           AF.Relu), sem=AC)
                b.wait("dve", AC, b.c[AC])
                dve(lambda e, gt=gt: e.tensor_add(
                    sm["lr_tok"][:, 12 * gt:12 * (gt + 1)], sm["w12"][:, 0:12],
                    sm["s12"][:, 0:12]), sem=DV)
            pe_prev = b.c[PE]

    def phase_d():
        mats = ("st0", "st2", "st1", "w1o")
        stu = {}
        u0 = 0
        for hh in range(NFH):
            for m in mats:
                stu[(hh, m)] = u0
                u0 += 4
        ST = lambda hh, m: arv(stu[(hh, m)], 4)
        fk_c = arv(64, 16)
        fvh = arv(80, 4)
        fqh = arv(84, 4)
        KC0, KC2, VC1, HID, DGP, DHB, SGK, GKS, HBM = [arv(88 + 4 * i, 4)
                                                       for i in range(9)]
        HQ = arv(124, 4)
        OCS = arv(128, 4)
        scr = ar(132, 4)

        def load_base():
            for hh in range(NFH):
                for m, src in zip(mats, (I["w0T"], I["w2T"], I["w1T"], I["w1om"])):
                    b.dma("pool", "sST", lambda e, hh=hh, m=m, src=src: e.dma_start(
                        out=ST(hh, m)[:, :, :],
                        in_=src[hh].rearrange("(dj p) e -> p dj e", p=128)))

        def kpath(hh, c, upd):
            fkh_t = lambda dj: fk_c[:, 4 * hh + dj, :]
            fvh_t = lambda dj: fvh[:, dj, :]
            for (srcf, outs) in ((fkh_t, ((KC0, 0), (KC2, 2))), (fvh_t, ((VC1, 1),))):
                for dj in range(4):
                    for cj in range(4):
                        tr(pslot(cj), srcf(dj)[:, 128 * cj:128 * (cj + 1)],
                           sem=PE)
                        b.wait("dve", PE, b.c[PE])
                        for (dst, li) in outs:
                            lrc = sm["lr_tok"][:, 12 * (4 * c + cj) + 4 * li + hh:
                                               12 * (4 * c + cj) + 4 * li + hh + 1]
                            dve(lambda e, dst=dst, cj=cj, dj=dj, lrc=lrc:
                                e.tensor_scalar_mul(
                                    dst[:, cj, 128 * dj:128 * (dj + 1)],
                                    pslot(cj), lrc), sem=DV)
                        b.wait("pe", DV, b.c[DV] - len(outs))
                b.wait("pe", DV, b.c[DV])
            for cj in range(4):
                for dj in range(4):
                    mm(ps[0][:], fkh_t(dj)[:, 128 * cj:128 * (cj + 1)],
                       ST(hh, "st0")[:, dj, :], dj == 0, dj == 3)
                for dj in range(4):
                    mm(ps[1][:], fkh_t(dj)[:, 128 * cj:128 * (cj + 1)],
                       ST(hh, "st2")[:, dj, :], dj == 0, dj == 3)
                for oj in range(4):
                    mm(ps[2][:], fvh_t(oj)[:, 128 * cj:128 * (cj + 1)],
                       ST(hh, "w1o")[:, oj, :], oj == 0, oj == 3,
                       sem=PE if oj == 3 else None)
                pv = b.c[PE]
                b.wait("act", PE, pv)
                act(lambda e, cj=cj: e.activation(SGK[:, cj, :], ps[0][:], AF.Sigmoid),
                    sem=AC)
                b.wait("dve", AC, b.c[AC])
                b.wait("dve", PE, pv)
                dve(lambda e, cj=cj: e.tensor_mul(GKS[:, cj, :], ps[0][:], SGK[:, cj, :]))
                dve(lambda e, cj=cj: e.tensor_sub(scr[:, 0:512], ps[0][:], GKS[:, cj, :]))
                dve(lambda e, cj=cj: e.tensor_copy(HBM[:, cj, :], ps[1][:]))
                dve(lambda e, cj=cj: e.tensor_mul(HID[:, cj, :], GKS[:, cj, :],
                                                  HBM[:, cj, :]))
                dve(lambda e, cj=cj: e.tensor_mul(scr[:, 512:1024], ps[2][:],
                                                  HBM[:, cj, :]))
                dve(lambda e, cj=cj: e.tensor_mul(DHB[:, cj, :], ps[2][:], GKS[:, cj, :]))
                dve(lambda e, cj=cj: e.scalar_tensor_tensor(
                    scr[:, 0:512], scr[:, 0:512], 1.0, SGK[:, cj, :], ALU.add, ALU.mult))
                dve(lambda e, cj=cj: e.tensor_mul(DGP[:, cj, :], scr[:, 0:512],
                                                  scr[:, 512:1024]), sem=DV)
                b.wait("pe", DV, b.c[DV])
            if not upd:
                return
            for (lhs, rhs, mat) in ((KC0, DGP, "st0"), (KC2, DHB, "st2"),
                                    (VC1, HID, "w1o"), (HID, VC1, "st1")):
                for dj in range(4):
                    bank = ps[3 + dj % 2]
                    for cj in range(4):
                        mm(bank[:], lhs[:, cj, 128 * dj:128 * (dj + 1)], rhs[:, cj, :],
                           cj == 0, cj == 3, sem=PE if cj == 3 else None)
                    b.wait("dve", PE, b.c[PE])
                    dve(lambda e, mat=mat, dj=dj, hh=hh, p=bank: e.tensor_add(
                        ST(hh, mat)[:, dj, :], ST(hh, mat)[:, dj, :], p[:]), sem=DV)
                    if dj >= 1:
                        b.wait("pe", DV, b.c[DV] - 1)
                b.wait("pe", DV, b.c[DV])

        def qpath(hh, c):
            for ej in range(4):
                for dj in range(4):
                    mm(ps[0][:], ST(hh, "st0")[:, dj, 128 * ej:128 * (ej + 1)],
                       fqh[:, dj, :], dj == 0, dj == 3)
                for dj in range(4):
                    mm(ps[1][:], ST(hh, "st2")[:, dj, 128 * ej:128 * (ej + 1)],
                       fqh[:, dj, :], dj == 0, dj == 3, sem=PE if dj == 3 else None)
                pv = b.c[PE]
                b.wait("act", PE, pv)
                act(lambda e: e.activation(scr[:, 0:512], ps[0][:], AF.Sigmoid), sem=AC)
                b.wait("dve", AC, b.c[AC])
                b.wait("dve", PE, pv)
                dve(lambda e: e.tensor_mul(scr[:, 512:1024], ps[0][:], scr[:, 0:512]))
                dve(lambda e, ej=ej: e.tensor_mul(HQ[:, ej, :], scr[:, 512:1024],
                                                  ps[1][:]), sem=DV)
                b.wait("pe", DV, b.c[DV])
            b.dwait("dve", "sOC")
            b.dwait("dve", "sRS")
            for oj in range(4):
                bank = ps[2 + oj % 2]
                for ej in range(4):
                    mm(bank[:], ST(hh, "st1")[:, ej, 128 * oj:128 * (oj + 1)],
                       HQ[:, ej, :], ej == 0, ej == 3, sem=PE if ej == 3 else None)
                b.wait("dve", PE, b.c[PE])
                dve(lambda e, oj=oj, p=bank: e.tensor_copy(OCS[:, oj, :], p[:]), sem=DV)
                if oj >= 1:
                    b.wait("pe", DV, b.c[DV] - 1)
            b.wait("sync", DV, b.c[DV])
            b.dma("sync", "sOC", lambda e, hh=hh, c=c: e.dma_start(
                out=S["oc_s"].rearrange("(q p) t -> p q t", p=128)
                [:, 4 * hh:4 * hh + 4, 512 * c:512 * (c + 1)], in_=OCS[:, :, :]))

        def stream_chunk(c):
            b.dma("sync", "sFK", lambda e, c=c: e.dma_start(
                out=fk_c[:, :, :], in_=S["fkT_s"].rearrange("(ft p) t -> p ft t", p=128)
                [:, :, 512 * c:512 * (c + 1)]))

        def stream_head(c, hh, src, dst, sem):
            b.dma("sync", sem, lambda e, c=c, hh=hh, src=src, dst=dst: e.dma_start(
                out=dst[:, :, :], in_=src.rearrange("(ft p) t -> p ft t", p=128)
                [:, 4 * hh:4 * hh + 4, 512 * c:512 * (c + 1)]))

        load_base()
        b.dwait("pe", "sST")
        b.dwait("dve", "sST")
        rsv = S["rsend"].rearrange("(i p) e -> p i e", p=128)
        ccv = {}
        for c in range(2):
            b.wait("sync", PE, b.c[PE])
            b.wait("sync", DV, b.c[DV])
            stream_chunk(c)
            for hh in range(NFH):
                b.wait("sync", PE, b.c[PE])
                stream_head(c, hh, S["fvT_s"], fvh, "sFV")
                b.dwait("pe", "sFK")
                b.dwait("dve", "sFK")
                b.dwait("pe", "sFV")
                b.dwait("dve", "sFV")
                kpath(hh, c, upd=True)
                if c == 1:
                    for mi, m in enumerate(mats):
                        b.dwait("dve", "sRS")
                        for dj in range(4):
                            dve(lambda e, hh=hh, m=m, dj=dj: e.tensor_scalar_mul(
                                OCS[:, dj, :], ST(hh, m)[:, dj, :],
                                sm["gcols"][:, 2:3]), sem=DV)
                        b.wait("sync", DV, b.c[DV])
                        idx = (hh * 4 + mi) * 4
                        b.dma("sync", "sRS", lambda e, idx=idx: e.dma_start(
                            out=rsv[:, idx:idx + 4, :], in_=OCS[:, :, :]))
                    b.dwait("pool", "sRS")
                    r0 = hh * 4 * FHD
                    b.op("pool", lambda e, r0=r0: e.collective_compute(
                        "AllReduce", ALU.add, replica_groups=GROUPS,
                        ins=[S["rsend"][r0:r0 + 4 * FHD, :].opt()],
                        outs=[S["rgat"][r0:r0 + 4 * FHD, :].opt()]), sem=CC)
                    ccv[hh] = b.c[CC]

        load_base()
        b.dwait("dve", "sST")
        rgv = S["rgat"].rearrange("(i p) e -> p i e", p=128)
        for c in range(2):
            b.wait("sync", PE, b.c[PE])
            b.wait("sync", DV, b.c[DV])
            if c == 0:
                stream_chunk(c)
            for hh in range(NFH):
                if c == 0:
                    b.wait("sync", CC, ccv[hh])
                    for mi, m in enumerate(mats):
                        idx = (hh * 4 + mi) * 4
                        b.wait("sync", DV, b.c[DV])
                        b.dma("sync", "sRL", lambda e, idx=idx: e.dma_start(
                            out=fqh[:, :, :], in_=rgv[:, idx:idx + 4, :]))
                        b.dwait("dve", "sRL")
                        for dj in range(4):
                            dve(lambda e, hh=hh, m=m, dj=dj: e.tensor_scalar_mul(
                                ST(hh, m)[:, dj, :], ST(hh, m)[:, dj, :],
                                sm["gcols"][:, 0:1]))
                            dve(lambda e, hh=hh, m=m, dj=dj: e.scalar_tensor_tensor(
                                ST(hh, m)[:, dj, :], fqh[:, dj, :], sm["gcols"][:, 1:2],
                                ST(hh, m)[:, dj, :], ALU.mult, ALU.add), sem=DV)
                    b.wait("pe", DV, b.c[DV])
                b.wait("sync", PE, b.c[PE])
                b.wait("sync", DV, b.c[DV])
                stream_head(c, hh, S["fqT_s"], fqh, "sFQ")
                if c == 0:
                    stream_head(c, hh, S["fvT_s"], fvh, "sFV")
                b.dwait("pe", "sFK")
                b.dwait("dve", "sFK")
                b.dwait("pe", "sFQ")
                b.dwait("pe", "sFV")
                b.dwait("dve", "sFV")
                qpath(hh, c)
                if c == 0:
                    kpath(hh, c, upd=True)

    def phase_e():
        onst = arv(0, 16)
        ln2T = arv(0, 16)
        hdn = arv(16, 44)
        WB0, WB1 = 60, 76
        WBD = 92
        scr = ar(136, 4)
        att4 = arv(16, 4, 2048)
        xn = [A32[:, 2048 * i:2048 * (i + 1)] for i in range(4)]
        xsem = ["sX0", "sX1", "sX2", "sX3"]
        outt = A32[:, 8192:8704]
        xv = I["x_own"].rearrange("(tt p) f -> p tt f", p=128)
        av = S["att_s"].rearrange("(tt p) f -> p tt f", p=128)
        xnv = S["xnew_s"].rearrange("(tt p) f -> p tt f", p=128)
        yv = y_own.rearrange("(tt p) f -> p tt f", p=128)
        free_w = [b.c[PE]] * 2
        wsem = ["sW0", "sW1"]
        pe_prev = b.c[PE]
        bfree = [0] * 6
        afree = [0] * 6

        for tb in range(NTB):
            b.wait("sync", PE, pe_prev)
            b.wait("sync", DV, b.c[DV])
            b.dma("sync", "sON", lambda e, tb=tb: e.dma_start(
                out=onst[:, :, :], in_=S["oc_s"].rearrange("(q p) t -> p q t", p=128)
                [:, :, 512 * tb:512 * (tb + 1)]))
            b.dwait("dve", "sON")
            b.dwait("pe", "sON")
            for hh in range(NFH):
                for k in range(4):
                    i = 4 * hh + k
                    if k > 0:
                        b.wait("dve", PE, b.c[PE])
                    dve(lambda e, i=i: e.tensor_mul(scr[:, 0:512], onst[:, i, :],
                                                    onst[:, i, :]), sem=DV)
                    b.wait("pe", DV, b.c[DV])
                    mm(ps[5][0:1, 0:512], sm["onescol"][:], scr[:, 0:512],
                       k == 0, k == 3, sem=PE)
                b.wait("dve", PE, b.c[PE])
                dve(lambda e: e.tensor_copy(sm["row"][0:1, 2048:2560],
                                            ps[5][0:1, 0:512]), sem=DV)
                b.wait("act", DV, b.c[DV])
                rsqrt_row(512, 1.0 / FHD)
                bcast_row(512, ps[5], sm["rbc"])
                for k in range(4):
                    i = 4 * hh + k
                    dve(lambda e, i=i, k=k: e.scalar_tensor_tensor(
                        onst[:, i, :], onst[:, i, :], sm["tttnw"][:, k:k + 1],
                        sm["rbc"][:, 0:512], ALU.mult, ALU.mult), sem=DV)
            b.wait("pe", DV, b.c[DV])
            b.dwait("sync", "sXW")
            for tt in range(4):
                gt = 4 * tb + tt
                b.dma("sync", xsem[tt], lambda e, gt=gt, d=xn[tt]: e.dma_start(
                    out=d, in_=xv[:, gt, :]))
            b.wait("sync", DV, b.c[DV])
            b.dma("sync", "sAT", lambda e, tb=tb: e.dma_start(
                out=att4[:, :, :], in_=av[:, 4 * tb:4 * tb + 4, :]))
            for tt in range(4):
                b.dwait("dve", xsem[tt])
            b.dwait("dve", "sAT")
            for mc in range(4):
                b.wait("pool", PE, free_w[mc % 2])
                load_w((WB0, WB1)[mc % 2], I["wmoT"], H, 512 * mc, 512, wsem[mc % 2])
                b.dwait("pe", wsem[mc % 2])
                wv = arv((WB0, WB1)[mc % 2], 16)
                cs = slice(512 * mc, 512 * (mc + 1))
                for tt in range(4):
                    bi = (4 * mc + tt) % 3
                    bank = ps[bi]
                    b.wait("pe", DV, bfree[bi])
                    for ft in range(16):
                        mm(bank[:], onst[:, ft, 128 * tt:128 * (tt + 1)], wv[:, ft, :],
                           ft == 0, ft == 15, sem=PE if ft == 15 else None)
                    b.wait("dve", PE, b.c[PE])
                    dve(lambda e, cs=cs, d=xn[tt], p=bank: e.tensor_add(
                        d[:, cs], d[:, cs], p[:]))
                    dve(lambda e, cs=cs, d=xn[tt], tt=tt: e.tensor_add(
                        d[:, cs], d[:, cs], att4[:, tt, cs]), sem=DV)
                    bfree[bi] = b.c[DV]
                free_w[mc % 2] = b.c[PE]
            for tt in range(4):
                gt = 4 * tb + tt
                dve(lambda e, gt=gt: e.tensor_single_scalar(
                    sm["rscol"][:, gt:gt + 1], sm["cb"][:, 0:1], 0.0, ALU.mult))
                dve(lambda e, d=xn[tt], gt=gt: e.reduce_sum(
                    sm["mucol"][:, gt:gt + 1], d, axis=AX.X), sem=DV)
                dve(lambda e, d=xn[tt], gt=gt: e.scalar_tensor_tensor(
                    scr[:, 0:2048], d, 1.0, d, ALU.mult, ALU.mult,
                    accum_out=sm["rscol"][:, gt:gt + 1]), sem=DV)
                b.wait("sync", DV, b.c[DV])
                b.dma("sync", "sXW", lambda e, gt=gt, d=xn[tt]: e.dma_start(
                    out=xnv[:, gt, :], in_=d))
            pe_prev = b.c[PE]
        b.wait("act", DV, b.c[DV])
        act(lambda e: e.activation(sm["rscol"][:, 8:16], sm["mucol"][:, 0:8],
                                   AF.Square, scale=1.0 / H), sem=AC)
        b.wait("act", AC, b.c[AC])
        act(lambda e: e.activation(sm["mucol"][:, 8:16], sm["mucol"][:, 0:8],
                                   AF.Identity, scale=1.0 / H), sem=AC)
        b.wait("dve", AC, b.c[AC])
        dve(lambda e: e.scalar_tensor_tensor(
            sm["rscol"][:, 0:8], sm["rscol"][:, 0:8], 1.0 / H, sm["rscol"][:, 8:16],
            ALU.mult, ALU.subtract), sem=DV)
        b.wait("act", DV, b.c[DV])
        act(lambda e: e.activation(sm["rscol"][:, 0:8], sm["rscol"][:, 0:8], AF.Ln,
                                   bias=sm["cb"][:, 2:3]), sem=AC)
        b.wait("act", AC, b.c[AC])
        act(lambda e: e.activation(sm["rscol"][:, 0:8], sm["rscol"][:, 0:8], AF.Exp,
                                   scale=-0.5), sem=AC)
        act(lambda e: e.copy(sm["mucol"][:, 0:8], sm["mucol"][:, 8:16]), sem=AC)
        b.wait("dve", AC, b.c[AC])

        for tb in range(NTB):
            for tt in range(4):
                gt = 4 * tb + tt
                b.dwait("sync", "sYW")
                b.dma("sync", xsem[tt], lambda e, gt=gt, d=xn[tt]: e.dma_start(
                    out=d, in_=xnv[:, gt, :]))
                b.dwait("dve", xsem[tt])
                b.wait("dve", PE, b.c[PE])
                dve(lambda e, d=xn[tt], gt=gt: e.tensor_scalar(
                    scr[:, 0:2048], d, sm["mucol"][:, gt:gt + 1],
                    sm["rscol"][:, gt:gt + 1], ALU.subtract, ALU.mult), sem=DV)
                b.wait("pe", DV, b.c[DV])
                for ft in range(16):
                    tr(pslot(ft), scr[:, 128 * ft:128 * (ft + 1)], sem=PE)
                    b.wait("dve", PE, b.c[PE])
                    dve(lambda e, ft=ft, tt=tt: e.tensor_copy(
                        ln2T[:, ft, 128 * tt:128 * (tt + 1)], pslot(ft)), sem=DV)
                    b.wait("pe", DV, b.c[DV] - 1)
                b.wait("pe", DV, b.c[DV])
            for wc in range(11):
                b.wait("pool", PE, free_w[0])
                load_w(WB0, I["gT"], H, 512 * wc, 512, "sW0")
                b.wait("pool", PE, free_w[1])
                load_w(WB1, I["uT"], H, 512 * wc, 512, "sW1")
                b.dwait("pe", "sW0")
                b.dwait("pe", "sW1")
                gvw, uvw = arv(WB0, 16), arv(WB1, 16)
                for isub in range(4):
                    icol = 4 * wc + isub
                    gb = isub % 2
                    sl = sm["rbc2"][:, 512 * gb:512 * (gb + 1)]
                    b.wait("pe", AC, afree[gb])
                    b.wait("pe", DV, bfree[gb])
                    for ft in range(16):
                        mm(ps[gb][:], gvw[:, ft, 128 * isub:128 * (isub + 1)],
                           ln2T[:, ft, :], ft == 0, ft == 15, sem=PE if ft == 15 else None)
                    b.wait("act", PE, b.c[PE])
                    b.wait("act", DV, bfree[gb])
                    act(lambda e, p=ps[gb], icol=icol, sl=sl: e.activation(
                        sl, p[:], AF.Silu, scale=sm["gate_s"][:, icol:icol + 1]), sem=AC)
                    acv = b.c[AC]
                    afree[gb] = acv
                    b.wait("pe", DV, bfree[2 + gb])
                    for ft in range(16):
                        mm(ps[2 + gb][:], uvw[:, ft, 128 * isub:128 * (isub + 1)],
                           ln2T[:, ft, :], ft == 0, ft == 15, sem=PE if ft == 15 else None)
                    b.wait("dve", PE, b.c[PE])
                    b.wait("dve", AC, acv)
                    dve(lambda e, icol=icol, p=ps[2 + gb], sl=sl: e.scalar_tensor_tensor(
                        hdn[:, icol, :], p[:], sm["up_s"][:, icol:icol + 1],
                        sl, ALU.mult, ALU.mult), sem=DV)
                    bfree[gb] = b.c[DV]
                    bfree[2 + gb] = b.c[DV]
                free_w[0] = free_w[1] = b.c[PE]
            b.wait("pe", DV, b.c[DV])
            for mc in range(4):
                b.wait("pool", PE, free_w[0])
                load_w(WBD, I["dT"], INTER, 512 * mc, 512, "sWD", split=4)
                b.dwait("pe", "sWD")
                dwv = arv(WBD, 44)
                for tt in range(4):
                    bi = tt % 2
                    bank = ps[bi]
                    b.wait("pe", DV, bfree[bi])
                    b.wait("pe", AC, afree[bi])
                    for ki in range(44):
                        mm(bank[:], hdn[:, ki, 128 * tt:128 * (tt + 1)], dwv[:, ki, :],
                           ki == 0, ki == 43, sem=PE if ki == 43 else None)
                    b.wait("dve", PE, b.c[PE])
                    cs = slice(512 * mc, 512 * (mc + 1))
                    dve(lambda e, cs=cs, p=bank: e.tensor_mul(
                        outt[:, 0:512], p[:], sm["ds_bc"][:, cs]))
                    dve(lambda e, cs=cs, tt=tt: e.tensor_add(
                        xn[tt][:, cs], xn[tt][:, cs], outt[:, 0:512]), sem=DV)
                    bfree[bi] = b.c[DV]
                free_w[0] = b.c[PE]
            b.wait("sync", DV, b.c[DV])
            for tt in range(4):
                gt = 4 * tb + tt
                b.dma("sync", "sYW", lambda e, gt=gt, tt=tt: e.dma_start(
                    out=yv[:, gt, :], in_=xn[tt]))
        b.dwait("sync", "sYW")
        if DEBUG:
            b.dma("sync", "sYW", lambda e: e.dma_start(out=DBG["dbg_cols"],
                                                       in_=sm["dbgc"][:]))
            for nm, srcs in (("dbg_ks", "ksend"), ("dbg_oT", "oT_s"),
                             ("dbg_att", "att_s"), ("dbg_fq", "fqT_s"),
                             ("dbg_fk", "fkT_s"), ("dbg_fv", "fvT_s"),
                             ("dbg_oc", "oc_s"), ("dbg_xn", "xnew_s"),
                             ("dbg_rg", "rgat")):
                b.dma("sync", "sYW", lambda e, d=DBG[nm], s=S[srcs]: e.dma_start(
                    out=d, in_=s))
            b.dwait("sync", "sYW")

    nums = sorted(h.num for h in b.sems.values())
    lo, hi = nums[0], nums[-1]
    assert nums == list(range(lo, hi + 1)), nums
    b.raw("pool", lambda e, lo=lo, hi=hi: e.sem_clear(range(lo, hi + 1)))
    b.raw("pool", lambda e: e.sem_inc(b.sems["GO"], 1))
    b.c["GO"] = 1
    for _eng in ("pe", "dve", "act", "sync"):
        b.wait(_eng, "GO", 1)

    load_const()
    b.barrier()
    qres, kres, vres = phase_a()
    b.barrier()
    phase_b(qres, kres, vres)
    b.barrier()
    phase_c()
    b.barrier()
    phase_d()
    b.barrier()
    phase_e()



def kernel(**inp):
    x = np.ascontiguousarray(np.asarray(inp["L0_attn_hidden_states"], np.float32))
    Bn, Sq, _ = x.shape

    T = lambda a: np.ascontiguousarray(np.asarray(a, np.float32).T)
    shared = dict(
        wqkvT=T(inp["L0_attn_attn_qkv_w"]), woT=T(inp["L0_attn_attn_o_w"]),
        wmlpT=T(inp["L0_attn_mlp_qkv_w"]), lrwT=T(inp["L0_attn_lr_w"]),
        wmoT=T(inp["L0_attn_mlp_o_w"]),
        w0T=np.ascontiguousarray(np.asarray(inp["L0_attn_w0"], np.float32)
                                 .transpose(0, 2, 1)),
        w1om=np.ascontiguousarray(np.asarray(inp["L0_attn_w1"], np.float32)),
        w1T=np.ascontiguousarray(np.asarray(inp["L0_attn_w1"], np.float32)
                                 .transpose(0, 2, 1)),
        w2T=np.ascontiguousarray(np.asarray(inp["L0_attn_w2"], np.float32)
                                 .transpose(0, 2, 1)),
        gT=T(inp["L0_ffn_gate_w"]), uT=T(inp["L0_ffn_up_w"]), dT=T(inp["L0_ffn_down_w"]),
    )
    col = lambda a, n: np.ascontiguousarray(np.asarray(a, np.float32).reshape(n, 128).T)
    shared["qnw"] = col(inp["L0_attn_q_norm_w"], 16)
    shared["knw"] = col(inp["L0_attn_k_norm_w"], 16)
    shared["mlpnw"] = col(inp["L0_attn_mlp_norm_w"], 16)
    shared["tttnw"] = col(inp["L0_attn_ttt_norm_w"], 4)
    shared["gate_s"] = col(inp["L0_ffn_gate_s"], 44)
    shared["up_s"] = col(inp["L0_ffn_up_s"], 44)
    shared["down_row"] = np.asarray(inp["L0_ffn_down_s"], np.float32).reshape(1, -1)
    shared["lrb_row"] = np.asarray(inp["L0_attn_lr_b"], np.float32).reshape(1, -1)
    shared["eye_in"] = np.eye(128, dtype=np.float32)
    psw = np.zeros((128, 128), np.float32)
    for d in range(64):
        psw[d + 64, d] = -1.0
        psw[d, d + 64] = 1.0
    shared["psw_in"] = psw
    ii = np.arange(128)[:, None]
    cc = np.arange(128)[None, :]
    shared["mlo_in"] = (ii > cc).astype(np.float32)
    shared["mhi_in"] = (ii <= cc).astype(np.float32)

    invf = 1.0 / (500000.0 ** (np.arange(0, 128, 2, dtype=np.float64) / 128.0))
    fr = np.outer(np.arange(Sq, dtype=np.float64), invf)
    cosT = np.concatenate([np.cos(fr).T, np.cos(fr).T], axis=0).astype(np.float32)
    sinT = np.concatenate([np.sin(fr).T, np.sin(fr).T], axis=0).astype(np.float32)

    nc = build_program()

    in_maps = []
    for c in range(8):
        bq, p = c // 2, c % 2
        t0 = TOWN * p
        m = dict(shared)
        m["x_own"] = np.ascontiguousarray(x[bq, t0:t0 + TOWN])
        m["cosT_in"] = np.ascontiguousarray(cosT[:, t0:t0 + TOWN])
        m["sinT_in"] = np.ascontiguousarray(sinT[:, t0:t0 + TOWN])
        m["hb_in"] = np.full((128, 1), 0.0 if p == 1 else -1e30, np.float32)
        m["ga_in"] = np.array([[1.0 if p == 0 else 0.0]], np.float32)
        m["gb_in"] = np.array([[0.0 if p == 0 else 1.0]], np.float32)
        m["gout_in"] = np.array([[1.0 if p == 0 else 0.0]], np.float32)
        in_maps.append(m)

    res = run_bass_kernel_spmd(nc, in_maps, core_ids=list(range(8)))
    LAST_RESULTS[0] = res
    out = np.empty((Bn, Sq, H), np.float32)
    for c in range(8):
        bq, p = c // 2, c % 2
        out[bq, TOWN * p:TOWN * (p + 1)] = np.asarray(res.results[c]["y_own"])
    return out


# version 11 (final)
# speedup vs baseline: 1.2630x; 1.0813x over previous
import math
import contextlib
import numpy as np
import concourse.bass as bass
import concourse.mybir as mybir
from concourse.bass_utils import run_bass_kernel_spmd

F32 = mybir.dt.float32
BF16 = mybir.dt.bfloat16
AF = mybir.ActivationFunctionType
ALU = mybir.AluOpType
AX = mybir.AxisListType

H = 2048
NFH = 4
FHD = 512
INTER = 5632
TOWN = 1024
NTB = TOWN // 512
EPS_LN = 1e-5
EPS_RMS = 1e-6
BASE_LR_INV = 0.001 + float(np.log(-np.expm1(-0.001)))
ASCALE = 1.0 / math.sqrt(128.0)
GROUPS = [[0, 1], [2, 3], [4, 5], [6, 7]]

DEBUG = False
LAST_RESULTS = [None]

ENGINES = ("pe", "dve", "act", "sync", "pool")
U = 512

DMA_SEMS = ("sIN", "sX0", "sX1", "sX2", "sX3", "sWA", "sKS", "sKH", "sVH", "sOW",
            "sOT", "sW0", "sW1", "sLR", "sAW", "sFW", "sST", "sFK", "sFV", "sFQ",
            "sRS", "sRL", "sON", "sAT", "sWD", "sXW", "sYW", "sOC", "sIP",
            "sW2", "sW3")
ENG_SEMS = ("PE", "DV", "AC", "CC", "GO")


class B:
    def __init__(self, nc):
        self.nc = nc
        self.th = {e: [] for e in ENGINES}
        self.sems = {}
        self.c = {}
        self.wm = {e: {} for e in ENGINES}
        self.flush_fn = None
        self.last_flush = {}

    def add_sem(self, name, handle):
        self.sems[name] = handle
        self.c[name] = 0

    def raw(self, eng, fn):
        self.th[eng].append(fn)

    def op(self, eng, build, sem=None, n=1):
        if sem is not None:
            self.c[sem] += n
            s, k = self.sems[sem], n

            def th(e, b=build, s=s, k=k):
                b(e).then_inc(s, k)
        else:
            def th(e, b=build):
                b(e)
        self.th[eng].append(th)
        return self.c[sem] if sem is not None else None

    def dma(self, eng, sem, build):
        return self.op(eng, build, sem=sem, n=16)

    def wait(self, eng, sem, val):
        assert val <= self.c[sem], f"wait {sem}>={val} emitted={self.c[sem]}"
        if val <= 0 or self.wm[eng].get(sem, -1) >= val:
            return
        if self.flush_fn is not None and val == self.c[sem] and \
                ((sem == "DV" and eng != "dve") or (sem == "AC" and eng != "act")):
            if self.last_flush.get(sem, -1) < val:
                self.flush_fn[sem]()
                self.last_flush[sem] = self.c[sem]
            val = self.c[sem]
            if self.wm[eng].get(sem, -1) >= val:
                return
        self.wm[eng][sem] = val
        self.th[eng].append(lambda e, s=self.sems[sem], v=val: e.wait_ge(s, v))

    def dwait(self, eng, sem):
        self.wait(eng, sem, self.c[sem])

    def barrier(self):
        snap = dict(self.c)
        for eng in ENGINES:
            for s, v in snap.items():
                self.wait(eng, s, v)


def build_program():
    nc = bass.Bass("TRN2", target_bir_lowering=False,
                   detect_race_conditions=False)

    def din(name, shape, dt=F32):
        return nc.dram_tensor(name, shape, dt, kind="ExternalInput").ap()

    I = {}
    for name, shape in (
        ("x_own", [TOWN, H]), ("wqkvT", [H, 3 * H]), ("woT", [H, H]),
        ("wmlpT", [H, 3 * H]), ("lrwT", [H, 12]), ("wmoT", [H, H]),
        ("w0T", [NFH, FHD, FHD]), ("w1om", [NFH, FHD, FHD]),
        ("w1T", [NFH, FHD, FHD]), ("w2T", [NFH, FHD, FHD]),
        ("gT", [H, INTER]), ("uT", [H, INTER]), ("dT", [INTER, H]),
        ("qnw", [128, 16]), ("knw", [128, 16]), ("mlpnw", [128, 16]),
        ("tttnw", [128, NFH]), ("gate_s", [128, 44]), ("up_s", [128, 44]),
        ("down_row", [1, H]), ("lrb_row", [1, 12]), ("eye_in", [128, 128]),
        ("psw_in", [128, 128]), ("cosT_in", [128, TOWN]), ("sinT_in", [128, TOWN]),
        ("mlo_in", [128, 128]), ("mhi_in", [128, 128]), ("hb_in", [128, 1]),
        ("ga_in", [1, 1]), ("gb_in", [1, 1]), ("gout_in", [1, 1]),
    ):
        I[name] = din(name, shape)

    y_own = nc.dram_tensor("y_own", [TOWN, H], F32, kind="ExternalOutput").ap()
    DBG = {}
    if DEBUG:
        for nm, shape, dt in (("dbg_ks", [H, TOWN], BF16), ("dbg_oT", [H, TOWN], BF16),
                              ("dbg_att", [TOWN, H], BF16), ("dbg_fq", [H, TOWN], BF16),
                              ("dbg_fk", [H, TOWN], BF16), ("dbg_fv", [H, TOWN], BF16),
                              ("dbg_oc", [H, TOWN], BF16), ("dbg_xn", [TOWN, H], F32),
                              ("dbg_rg", [16 * FHD, FHD], BF16)):
            DBG[nm] = nc.dram_tensor(nm, shape, dt, kind="ExternalOutput").ap()
        DBG["dbg_scr"] = nc.dram_tensor("dbg_scr", [128, 2048], BF16,
                                        kind="ExternalOutput").ap()
        DBG["dbg_lnx"] = nc.dram_tensor("dbg_lnx", [128, 8192], BF16,
                                        kind="ExternalOutput").ap()
        DBG["dbg_cols"] = nc.dram_tensor("dbg_cols", [128, 8], F32,
                                         kind="ExternalOutput").ap()

    S = {}
    for name, shape, dt in (
        ("ksend0", [H, 512], BF16), ("ksend1", [H, 512], BF16),
        ("vsend0", [512, H], BF16), ("vsend1", [512, H], BF16),
        ("kgat0", [2 * H, 512], BF16), ("kgat1", [2 * H, 512], BF16),
        ("vgat0", [2 * 512, H], BF16), ("vgat1", [2 * 512, H], BF16),
        ("oT_s", [H, TOWN], BF16), ("att_s", [TOWN, H], BF16),
        ("fqT_s", [H, TOWN], BF16), ("fkT_s", [H, TOWN], BF16),
        ("fvT_s", [H, TOWN], BF16), ("oc_s", [H, TOWN], BF16),
        ("xnew_s", [TOWN, H], F32),
        ("rsend", [16 * FHD, FHD], BF16), ("rgat", [16 * FHD, FHD], BF16),
    ):
        S[name] = nc.dram_tensor(name, shape, dt, kind="Internal").ap()

    st = contextlib.ExitStack()
    with st:
        def sb(name, shape, dt):
            return st.enter_context(nc.sbuf_tensor(name, shape, dt))

        sm = {}
        sm["eye"] = sb("eye", [128, 128], BF16)
        sm["psw"] = sb("psw", [128, 128], BF16)
        sm["mlo"] = sb("mlo", [128, 128], BF16)
        sm["mhi"] = sb("mhi", [128, 128], BF16)
        sm["cosT"] = sb("cosT", [128, TOWN], BF16)
        sm["sinT"] = sb("sinT", [128, TOWN], BF16)
        sm["onescol"] = sb("onescol", [128, 1], BF16)
        sm["onesrow32"] = sb("onesrow32", [1, 128], F32)
        for n_, w_ in (("qnw", 16), ("knw", 16), ("mlpnw", 16), ("tttnw", NFH),
                       ("gate_s", 44), ("up_s", 44)):
            sm[n_] = sb("g_" + n_, [128, w_], F32)
        sm["ds_bc"] = sb("ds_bc", [128, H], F32)
        sm["lrb_bc"] = sb("lrb_bc", [128, 12], F32)
        sm["hbcol"] = sb("hbcol", [128, 1], F32)
        sm["gcols"] = sb("gcols", [128, 4], F32)
        sm["lr_tok"] = sb("lr_tok", [128, 8 * 12], F32)
        sm["mucol"] = sb("mucol", [128, 16], F32)
        sm["rscol"] = sb("rscol", [128, 16], F32)
        sm["row"] = sb("row", [1, 2560], F32)
        sm["tiny"] = sb("tiny", [1, 16], F32)
        sm["w12"] = sb("w12", [128, 16], F32)
        sm["s12"] = sb("s12", [128, 16], F32)
        sm["stat"] = sb("stat", [128, 16], F32)
        sm["rbc"] = sb("rbc", [128, 512], F32)
        sm["rbc2"] = sb("rbc2", [128, 1024], F32)
        sm["cb"] = sb("cb", [128, 3], F32)
        sm["dbgc"] = sb("dbgc", [128, 8], F32)

        AR = sb("AR", [128, 142 * U], BF16)
        A32 = sb("A32", [128, 8704], F32)

        ps = [st.enter_context(nc.psum_tensor(f"ps{i}", [128, 512], F32))
              for i in range(6)]
        pst = [st.enter_context(nc.psum_tensor("pstA", [128, 1024], BF16)),
               st.enter_context(nc.psum_tensor("pstB", [128, 1024], BF16))]

        bld = B(nc)
        for s_ in ENG_SEMS + DMA_SEMS:
            bld.add_sem(s_, st.enter_context(nc.semaphore(s_)))

        emit_all(bld, I, S, sm, AR, A32, ps, pst, y_own, DBG)

        blk = st.enter_context(nc.Block())
        for eng_name, key in (("tensor", "pe"), ("vector", "dve"), ("scalar", "act"),
                              ("sync", "sync"), ("gpsimd", "pool")):
            def body(e, lst=bld.th[key]):
                for t in lst:
                    t(e)
            getattr(blk, eng_name)(body)

    return nc


def emit_all(b, I, S, sm, AR, A32, ps, pst, y_own, DBG):
    PE, DV, AC, CC = "PE", "DV", "AC", "CC"

    def ar(u0, nu):
        assert u0 + nu <= 142
        return AR[:, u0 * U:(u0 + nu) * U]

    def arv(u0, ntiles, w=512):
        nu = (ntiles * w + U - 1) // U
        assert u0 + nu <= 142
        t = AR[:, u0 * U:u0 * U + ntiles * w]
        return t.rearrange("p (a c) -> p a c", c=w)

    def mm(out, lhsT, rhs, start, stop, sem=None):
        return b.op("pe", lambda e, o=out, l=lhsT, r=rhs, s1=start, s2=stop:
                    e.matmul(o, l, r, start=s1, stop=s2), sem=sem)

    def tr(out, in_, sem=None):
        return b.op("pe", lambda e, o=out, i=in_: e.transpose(o, i, sm["eye"][:]),
                    sem=sem)

    def pslot(i):
        bank = pst[i % 2]
        c0 = 128 * ((i // 2) % 4)
        return bank[:, c0:c0 + 128]

    def dve(fn, sem=None):
        return b.op("dve", fn, sem=sem)

    def act(fn, sem=None):
        return b.op("act", fn, sem=sem)

    def _dv_flush():
        b.op("dve", lambda e: e.tensor_copy(sm["dbgc"][:, 7:8], sm["cb"][:, 0:1]),
             sem="DV")

    def _ac_flush():
        b.op("act", lambda e: e.copy(sm["dbgc"][:, 6:7], sm["cb"][:, 0:1]), sem="AC")

    b.flush_fn = {"DV": _dv_flush, "AC": _ac_flush}

    def rsqrt_row(n, scale):
        r = sm["row"][0:1, 2048:2048 + n]
        act(lambda e: e.activation(r, r, AF.Ln, scale=scale, bias=sm["cb"][0:1, 0:1]),
            sem=AC)
        b.wait("act", AC, b.c[AC])
        return act(lambda e: e.activation(r, r, AF.Exp, scale=-0.5), sem=AC)

    def bcast_row(n, psbank, dst):
        b.wait("pe", AC, b.c[AC])
        b.wait("pe", DV, b.c[DV])
        mm(psbank[:, 0:n], sm["onesrow32"][:], sm["row"][0:1, 2048:2048 + n],
           True, True, sem=PE)
        b.wait("dve", PE, b.c[PE])
        dve(lambda e: e.tensor_copy(dst[:, 0:n], psbank[:, 0:n]), sem=DV)
        b.wait("pe", DV, b.c[DV])

    def load_w(dst_u, srcT, rows, col0, w, sem, split=2, engs=("pool",)):
        nft = rows // 128
        dst = arv(dst_u, nft, w)
        view = srcT.rearrange("(ft p) c -> p ft c", p=128)
        step = (nft + split - 1) // split
        for si, s0 in enumerate(range(0, nft, step)):
            s1 = min(nft, s0 + step)
            b.dma(engs[si % len(engs)], sem, lambda e, a=s0, c=s1: e.dma_start(
                out=dst[:, a:c, :], in_=view[:, a:c, col0:col0 + w]))

    def load_const():
        for dst, src in ((sm["qnw"], I["qnw"]), (sm["knw"], I["knw"]),
                         (sm["mlpnw"], I["mlpnw"]), (sm["tttnw"], I["tttnw"]),
                         (sm["gate_s"], I["gate_s"]), (sm["up_s"], I["up_s"]),
                         (sm["hbcol"], I["hb_in"])):
            b.dma("sync", "sIN", lambda e, d=dst, s=src: e.dma_start(out=d[:], in_=s))
        b.dma("sync", "sIN", lambda e: e.dma_start(out=sm["row"][0:1, 0:H],
                                                   in_=I["down_row"]))
        b.dma("sync", "sIN", lambda e: e.dma_start(out=sm["tiny"][0:1, 0:12],
                                                   in_=I["lrb_row"]))
        b.dma("sync", "sIN", lambda e: e.dma_start(out=sm["tiny"][0:1, 12:13],
                                                   in_=I["ga_in"]))
        b.dma("sync", "sIN", lambda e: e.dma_start(out=sm["tiny"][0:1, 13:14],
                                                   in_=I["gb_in"]))
        b.dma("sync", "sIN", lambda e: e.dma_start(out=sm["tiny"][0:1, 14:15],
                                                   in_=I["gout_in"]))
        for dst, src in ((sm["eye"], I["eye_in"]), (sm["psw"], I["psw_in"]),
                         (sm["cosT"], I["cosT_in"]), (sm["sinT"], I["sinT_in"]),
                         (sm["mlo"], I["mlo_in"]), (sm["mhi"], I["mhi_in"])):
            b.dma("pool", "sIP", lambda e, d=dst, s=src: e.dma_start(out=d[:], in_=s))
        b.dwait("dve", "sIN")
        b.dwait("dve", "sIP")
        b.dwait("pe", "sIP")
        for nm in ("lr_tok", "mucol", "rscol", "w12", "s12",
                   "stat", "rbc", "rbc2"):
            dve(lambda e, nm=nm: e.memset(sm[nm][:], 0.0))
        dve(lambda e: e.memset(sm["onescol"][:], 1.0))
        dve(lambda e: e.memset(sm["onesrow32"][:], 1.0))
        dve(lambda e: e.memset(sm["cb"][:, 0:1], EPS_RMS))
        dve(lambda e: e.memset(sm["cb"][:, 1:2], 1.0))
        dve(lambda e: e.memset(sm["cb"][:, 2:3], EPS_LN))
        dve(lambda e: e.tensor_scalar_add(sm["row"][0:1, 2048:2060],
                                          sm["tiny"][0:1, 0:12], BASE_LR_INV), sem=DV)
        b.wait("pe", DV, b.c[DV])
        b.dwait("pe", "sIN")
        for j in range(4):
            mm(ps[j % 2][:], sm["onesrow32"][:], sm["row"][0:1, 512 * j:512 * (j + 1)],
               True, True, sem=PE)
            b.wait("dve", PE, b.c[PE])
            dve(lambda e, j=j, p=ps[j % 2]: e.tensor_copy(
                sm["ds_bc"][:, 512 * j:512 * (j + 1)], p[:]), sem=DV)
            b.wait("pe", DV, b.c[DV])
        b.wait("act", DV, b.c[DV])
        act(lambda e: e.activation(sm["s12"][:, 0:1], sm["cb"][:, 1:2], AF.Ln), sem=AC)
        b.wait("act", AC, b.c[AC])
        act(lambda e: e.activation(sm["s12"][:, 0:1], sm["cb"][:, 1:2], AF.Exp), sem=AC)
        b.wait("act", AC, b.c[AC])
        act(lambda e: e.activation(sm["s12"][:, 0:1], sm["cb"][:, 1:2], AF.Square),
            sem=AC)
        b.wait("act", AC, b.c[AC])
        act(lambda e: e.activation(sm["s12"][:, 0:1], sm["cb"][:, 1:2], AF.Identity),
            sem=AC)
        mm(ps[0][:, 0:12], sm["onesrow32"][:], sm["row"][0:1, 2048:2060], True, True,
           sem=PE)
        mm(ps[1][:, 0:3], sm["onesrow32"][:], sm["tiny"][0:1, 12:15], True, True, sem=PE)
        b.wait("dve", PE, b.c[PE])
        dve(lambda e: e.tensor_copy(sm["lrb_bc"][:], ps[0][:, 0:12]))
        dve(lambda e: e.tensor_copy(sm["gcols"][:, 0:3], ps[1][:, 0:3]), sem=DV)
        b.wait("pe", DV, b.c[DV])

    def phase_a():
        lnxT = arv(0, 16)
        scr = ar(16, 4)
        WB = 20
        qres = arv(46, 16, 1024)
        kres = arv(78, 16, 1024)
        vres = arv(110, 8, 2048)
        xt = [A32[:, 0:2048], A32[:, 2048:4096]]
        xsem = ["sX0", "sX1"]
        xgate = [0, 0]
        stt_ = sm["stat"]
        xv = I["x_own"].rearrange("(tt p) f -> p tt f", p=128)
        free_w = b.c[PE]
        bfree = [0, 0, 0]

        for gt0 in range(2):
            b.dma("sync", xsem[gt0 % 2], lambda e, gt0=gt0, d=xt[gt0 % 2]: e.dma_start(
                out=d, in_=xv[:, gt0, :]))
        for tb in range(NTB):
            for tt in range(4):
                gt = tb * 4 + tt
                b.dwait("act", xsem[tt % 2])
                b.dwait("dve", xsem[tt % 2])
                b.wait("dve", AC, b.c[AC])
                dve(lambda e: e.tensor_single_scalar(stt_[:, 1:2], sm["cb"][:, 0:1],
                                                     0.0, ALU.mult))
                dve(lambda e, x=xt[tt % 2]: e.reduce_sum(stt_[:, 0:1], x, axis=AX.X),
                    sem=DV)
                b.wait("act", DV, b.c[DV])
                act(lambda e, x=xt[tt % 2]: e.activation(
                    scr, x, AF.Square, accum_out=stt_[:, 1:2]), sem=AC)
                act(lambda e: e.activation(stt_[:, 2:3], stt_[:, 0:1], AF.Square,
                                           scale=1.0 / H), sem=AC)
                b.wait("act", AC, b.c[AC])
                act(lambda e: e.activation(stt_[:, 6:7], stt_[:, 0:1], AF.Identity,
                                           scale=1.0 / H), sem=AC)
                b.wait("dve", AC, b.c[AC])
                dve(lambda e: e.scalar_tensor_tensor(
                    stt_[:, 3:4], stt_[:, 1:2], 1.0 / H, stt_[:, 2:3],
                    ALU.mult, ALU.subtract), sem=DV)
                b.wait("act", DV, b.c[DV])
                act(lambda e: e.activation(stt_[:, 4:5], stt_[:, 3:4], AF.Ln,
                                           bias=sm["cb"][:, 2:3]), sem=AC)
                b.wait("act", AC, b.c[AC])
                act(lambda e: e.activation(stt_[:, 5:6], stt_[:, 4:5], AF.Exp,
                                           scale=-0.5), sem=AC)
                b.wait("dve", AC, b.c[AC])
                dve(lambda e, x=xt[tt % 2]: e.tensor_scalar(
                    scr, x, stt_[:, 6:7], stt_[:, 5:6], ALU.subtract, ALU.mult), sem=DV)
                if gt + 2 < 4 * NTB:
                    b.wait("sync", DV, b.c[DV])
                    b.dma("sync", xsem[tt % 2], lambda e, gt=gt, d=xt[tt % 2]:
                          e.dma_start(out=d, in_=xv[:, gt + 2, :]))
                if DEBUG and tb == 0 and tt == 0:
                    dve(lambda e: e.tensor_copy(sm["dbgc"][:, 0:6], stt_[:, 0:6]),
                        sem=DV)
                    b.wait("sync", DV, b.c[DV])
                    b.dma("sync", "sKS", lambda e: e.dma_start(
                        out=DBG["dbg_scr"], in_=scr))
                xgate[tt % 2] = b.c[DV]
                b.wait("pe", DV, b.c[DV])
                for ft in range(16):
                    tr(pslot(ft), scr[:, 128 * ft:128 * (ft + 1)], sem=PE)
                    b.wait("dve", PE, b.c[PE])
                    dve(lambda e, ft=ft, tt=tt: e.tensor_copy(
                        lnxT[:, ft, 128 * tt:128 * (tt + 1)], pslot(ft)), sem=DV)
                    b.wait("pe", DV, b.c[DV] - 1)
                b.wait("pe", DV, b.c[DV])

            if DEBUG and tb == 0:
                b.wait("sync", DV, b.c[DV])
                b.dma("sync", "sKS", lambda e: e.dma_start(
                    out=DBG["dbg_lnx"], in_=AR[:, 0:8192]))
            def do_wc(wc):
                nonlocal free_w
                b.wait("pool", PE, free_w)
                load_w(WB, I["wqkvT"], H, 512 * wc, 512, "sWA")
                b.dwait("pe", "sWA")
                wv = arv(WB, 16)
                if wc < 8:
                    dst = qres if wc < 4 else kres
                    for osub in range(4):
                        head = (wc % 4) * 4 + osub
                        bi = (4 * wc + osub) % 3
                        bank = ps[bi]
                        b.wait("pe", DV, bfree[bi])
                        for ft in range(16):
                            mm(bank[:], wv[:, ft, 128 * osub:128 * (osub + 1)],
                               lnxT[:, ft, :], ft == 0, ft == 15,
                               sem=PE if ft == 15 else None)
                        b.wait("dve", PE, b.c[PE])
                        dve(lambda e, d=dst, hd=head, tb=tb, p=bank: e.tensor_copy(
                            d[:, hd, 512 * tb:512 * (tb + 1)], p[:]), sem=DV)
                        bfree[bi] = b.c[DV]
                else:
                    for tt in range(4):
                        bi = (4 * wc + tt) % 3
                        bank = ps[bi]
                        b.wait("pe", DV, bfree[bi])
                        for ft in range(16):
                            mm(bank[:], lnxT[:, ft, 128 * tt:128 * (tt + 1)],
                               wv[:, ft, :], ft == 0, ft == 15,
                               sem=PE if ft == 15 else None)
                        b.wait("dve", PE, b.c[PE])
                        kt = tb * 4 + tt
                        dve(lambda e, kt=kt, wc=wc, p=bank: e.tensor_copy(
                            vres[:, kt, 512 * (wc - 8):512 * (wc - 7)], p[:]), sem=DV)
                        bfree[bi] = b.c[DV]
                free_w = b.c[PE]

            for wc in range(4, 12):
                do_wc(wc)
            b.wait("pe", DV, b.c[DV])

            def rope_body(res, gam):
                for ft in range(16):
                    if ft >= 4:
                        b.wait("dve", PE, b.c[PE] - 3)
                    dve(lambda e, ft=ft, r=res, tb=tb: e.tensor_mul(
                        scr[:, 512 * (ft % 4):512 * (ft % 4 + 1)],
                        r[:, ft, 512 * tb:512 * (tb + 1)],
                        r[:, ft, 512 * tb:512 * (tb + 1)]), sem=DV)
                    b.wait("pe", DV, b.c[DV])
                    mm(ps[3][0:1, 0:512], sm["onescol"][:],
                       scr[:, 512 * (ft % 4):512 * (ft % 4 + 1)], ft == 0, ft == 15,
                       sem=PE)
                b.wait("dve", PE, b.c[PE])
                dve(lambda e: e.tensor_copy(sm["row"][0:1, 2048:2560], ps[3][0:1, 0:512]),
                    sem=DV)
                b.wait("act", DV, b.c[DV])
                rsqrt_row(512, 1.0 / H)
                bcast_row(512, ps[3], sm["rbc"])
                for ft in range(16):
                    cs = slice(512 * tb, 512 * (tb + 1))
                    dve(lambda e, r=res, ft=ft, gm=gam, cs=cs: e.scalar_tensor_tensor(
                        scr[:, 0:512], r[:, ft, cs], gm[:, ft:ft + 1], sm["rbc"][:, 0:512],
                        ALU.mult, ALU.mult), sem=DV)
                    b.wait("pe", DV, b.c[DV])
                    mm(ps[4][:], sm["psw"][:], scr[:, 0:512], True, True, sem=PE)
                    b.wait("dve", PE, b.c[PE])
                    dve(lambda e, cs=cs: e.tensor_mul(
                        scr[:, 512:1024], scr[:, 0:512], sm["cosT"][:, cs]))
                    dve(lambda e, cs=cs: e.tensor_mul(
                        scr[:, 1024:1536], ps[4][:], sm["sinT"][:, cs]), sem=DV)
                    b.wait("pe", DV, b.c[DV])
                    dve(lambda e, ft=ft, r=res, cs=cs: e.tensor_add(
                        r[:, ft, cs], scr[:, 512:1024], scr[:, 1024:1536]), sem=DV)
                b.wait("pe", DV, b.c[DV])

            rope_body(kres, sm["knw"])
            b.wait("sync", DV, b.c[DV])
            b.dma("sync", "sKS", lambda e, tb=tb: e.dma_start(
                out=S[f"ksend{tb}"].rearrange("(ft p) t -> p ft t", p=128),
                in_=kres[:, :, 512 * tb:512 * (tb + 1)]))
            b.dma("sync", "sKS", lambda e, tb=tb: e.dma_start(
                out=S[f"vsend{tb}"].rearrange("(kt p) f -> p kt f", p=128),
                in_=vres[:, 4 * tb:4 * tb + 4, :]))
            b.dwait("pool", "sKS")
            b.op("pool", lambda e, tb=tb: e.collective_compute(
                "AllGather", ALU.bypass, replica_groups=GROUPS,
                ins=[S[f"ksend{tb}"].opt()], outs=[S[f"kgat{tb}"].opt()]), sem=CC)
            b.op("pool", lambda e, tb=tb: e.collective_compute(
                "AllGather", ALU.bypass, replica_groups=GROUPS,
                ins=[S[f"vsend{tb}"].opt()], outs=[S[f"vgat{tb}"].opt()]), sem=CC)

            for wc in range(4):
                do_wc(wc)
            b.wait("pe", DV, b.c[DV])
            rope_body(qres, sm["qnw"])

        return qres, kres, vres

    def phase_b(qres, kres, vres):
        kh = ar(0, 2)
        vh = arv(2, 8, 128)
        ptil = ar(4, 9)
        sqb = ar(14, 4)
        oTh = ar(18, 2)
        b.wait("sync", CC, b.c[CC])
        khg = [S[f"kgat{i}"][0:H, :].rearrange("(ft p) t -> p ft t", p=128)
               for i in range(2)]
        vhg = [S[f"vgat{i}"][0:512, :].rearrange("(kt p) f -> p kt f", p=128)
               for i in range(2)]

        pieces = []
        for j in range(12):
            s_lo, s_hi = max(0, j - 7), min(3, j - 1)
            if s_hi >= s_lo:
                pieces.append((j, 128 * s_lo, 128 * (s_hi - s_lo + 1), None))
            if j <= 3:
                pieces.append((j, 128 * j, 128, "lo"))
            if j >= 8:
                pieces.append((j, 128 * (j - 8), 128, "hi"))
        ppos, poff = [], 0
        for (j, c0, n, msk) in pieces:
            ppos.append((j, c0, n, msk, poff))
            poff += n
        assert poff == 4608

        pe_prev_head = b.c[PE]
        dv_prev_head = b.c[DV]
        for h in range(16):
            b.wait("sync", PE, pe_prev_head)
            b.wait("sync", DV, dv_prev_head)
            b.dwait("sync", "sOW")
            for half in range(2):
                b.dma("sync", "sKH", lambda e, h=h, half=half: e.dma_start(
                    out=kh[:, 512 * half:512 * (half + 1)], in_=khg[half][:, h, :]))
                b.dma("sync", "sVH", lambda e, h=h, half=half: e.dma_start(
                    out=vh[:, 4 * half:4 * half + 4, :],
                    in_=vhg[half][:, 0:4, 128 * h:128 * (h + 1)]))
            b.dwait("dve", "sKH")
            b.dwait("dve", "sVH")
            b.dwait("pe", "sKH")
            b.dwait("pe", "sVH")
            for idx4 in range(4):
                ksrc = kh[:, 512 * idx4:512 * (idx4 + 1)] if idx4 < 2 else \
                    kres[:, h, :][:, 512 * (idx4 - 2):512 * (idx4 - 1)]
                b.wait("dve", PE, b.c[PE])
                dve(lambda e, ksrc=ksrc: e.tensor_mul(sqb[:, 0:512], ksrc, ksrc), sem=DV)
                b.wait("pe", DV, b.c[DV])
                mm(ps[3][0:1, 0:512], sm["onescol"][:], sqb[:, 0:512], True, True, sem=PE)
                b.wait("dve", PE, b.c[PE])
                dve(lambda e, o=512 * idx4: e.tensor_copy(
                    sm["row"][0:1, o:o + 512], ps[3][0:1, 0:512]), sem=DV)

            for qb in range(NTB):
                q0 = 512 * qb
                b.wait("dve", PE, b.c[PE])
                dve(lambda e, h=h, q0=q0: e.tensor_mul(
                    sqb[:, 0:512], qres[:, h, :][:, q0:q0 + 512],
                    qres[:, h, :][:, q0:q0 + 512]), sem=DV)
                b.wait("pe", DV, b.c[DV])
                mm(ps[3][0:1, 0:512], sm["onescol"][:], sqb[:, 0:512], True, True, sem=PE)
                b.wait("dve", PE, b.c[PE])
                dve(lambda e: e.tensor_copy(sm["row"][0:1, 2048:2560], ps[3][0:1, 0:512]),
                    sem=DV)
                dve(lambda e: e.reduce_max(sm["tiny"][0:1, 4:5],
                                           sm["row"][0:1, 2048:2560], axis=AX.X))
                dve(lambda e, q0=q0: e.reduce_max(
                    sm["tiny"][0:1, 5:6], sm["row"][0:1, q0:q0 + 1536], axis=AX.X),
                    sem=DV)
                b.wait("act", DV, b.c[DV])
                act(lambda e: e.activation(sm["tiny"][0:1, 6:7], sm["tiny"][0:1, 4:5],
                                           AF.Identity, scale=sm["tiny"][0:1, 5:6]),
                    sem=AC)
                b.wait("act", AC, b.c[AC])
                act(lambda e: e.activation(sm["tiny"][0:1, 7:8], sm["tiny"][0:1, 6:7],
                                           AF.Ln), sem=AC)
                b.wait("act", AC, b.c[AC])
                act(lambda e: e.activation(sm["tiny"][0:1, 9:10], sm["tiny"][0:1, 7:8],
                                           AF.Exp, scale=0.5), sem=AC)
                b.wait("dve", AC, b.c[AC])
                dve(lambda e: e.tensor_scalar_mul(sm["tiny"][0:1, 10:11],
                                                  sm["tiny"][0:1, 9:10], -ASCALE),
                    sem=DV)
                b.wait("pe", DV, b.c[DV])
                mm(ps[3][:, 0:1], sm["onesrow32"][:], sm["tiny"][0:1, 10:11], True, True,
                   sem=PE)
                b.wait("dve", PE, b.c[PE])
                dve(lambda e: e.tensor_copy(sm["stat"][:, 3:4], ps[3][:, 0:1]), sem=DV)
                b.wait("act", DV, b.c[DV])
                act(lambda e: e.activation(sm["stat"][:, 2:3], sm["stat"][:, 3:4],
                                           AF.Identity, bias=sm["hbcol"][:]), sem=AC)
                b.wait("act", AC, b.c[AC])

                def kT_src(j, qb=qb, h=h):
                    kk0 = 512 * qb + 128 * j
                    if kk0 < 1024:
                        return kh[:, kk0:kk0 + 128]
                    return kres[:, h, :][:, kk0 - 1024:kk0 - 896]

                def v_src(j, qb=qb, h=h):
                    kk0 = 512 * qb + 128 * j
                    if kk0 < 1024:
                        return vh[:, kk0 // 128, :]
                    return vres[:, (kk0 - 1024) // 128, 128 * h:128 * (h + 1)]

                bank_free = [b.c[AC]] * 4
                bi = 0
                i0 = 0
                while i0 < len(ppos):
                    used, grp = 0, []
                    while i0 < len(ppos) and used + ppos[i0][2] <= 512:
                        grp.append(ppos[i0])
                        used += ppos[i0][2]
                        i0 += 1
                    bank = ps[bi]
                    b.wait("pe", AC, bank_free[bi])
                    ro = 0
                    for gi, (j, c0, n, msk, po) in enumerate(grp):
                        mm(bank[:, ro:ro + n], kT_src(j),
                           qres[:, h, :][:, q0 + c0:q0 + c0 + n], True, True,
                           sem=PE if gi == len(grp) - 1 else None)
                        ro += n
                    b.wait("act", PE, b.c[PE])
                    ro = 0
                    for (j, c0, n, msk, po) in grp:
                        halo = (512 * qb + 128 * j) < 1024
                        bias = sm["stat"][:, 2:3] if halo else sm["stat"][:, 3:4]
                        act(lambda e, ro=ro, n=n, po=po, bias=bias, bank=bank:
                            e.activation(ptil[:, po:po + n], bank[:, ro:ro + n], AF.Exp,
                                         scale=ASCALE, bias=bias), sem=AC)
                        ro += n
                    bank_free[bi] = b.c[AC]
                    bi = (bi + 1) % 4
                av = b.c[AC]
                b.wait("dve", AC, av)
                for (j, c0, n, msk, po) in ppos:
                    if msk is None:
                        continue
                    m = sm["mlo"] if msk == "lo" else sm["mhi"]
                    dve(lambda e, po=po, m=m: e.tensor_mul(
                        ptil[:, po:po + 128], ptil[:, po:po + 128], m[:]), sem=DV)
                dvv = b.c[DV]
                b.wait("pe", DV, dvv)
                b.wait("pe", AC, av)
                for pi, (j, c0, n, msk, po) in enumerate(ppos):
                    mm(ps[4][:, c0:c0 + n], v_src(j), ptil[:, po:po + n],
                       pi == 0, pi == len(ppos) - 1)
                for pi, (j, c0, n, msk, po) in enumerate(ppos):
                    mm(ps[5][0:1, c0:c0 + n], sm["onescol"][:], ptil[:, po:po + n],
                       pi == 0, pi == len(ppos) - 1,
                       sem=PE if pi == len(ppos) - 1 else None)
                b.wait("dve", PE, b.c[PE])
                dve(lambda e: e.tensor_scalar_add(sm["row"][0:1, 2048:2560],
                                                  ps[5][0:1, 0:512], 1e-38))
                dve(lambda e: e.reciprocal(sm["row"][0:1, 2048:2560],
                                           sm["row"][0:1, 2048:2560]), sem=DV)
                b.wait("pe", DV, b.c[DV])
                mm(ps[5][:, 0:512], sm["onesrow32"][:], sm["row"][0:1, 2048:2560],
                   True, True, sem=PE)
                b.wait("dve", PE, b.c[PE])
                dve(lambda e: e.tensor_copy(sm["rbc"][:], ps[5][:, 0:512]), sem=DV)
                dve(lambda e, qb=qb: e.tensor_mul(
                    oTh[:, 512 * qb:512 * (qb + 1)], ps[4][:, 0:512], sm["rbc"][:]),
                    sem=DV)
                b.wait("pe", DV, b.c[DV])
            pe_prev_head = b.c[PE]
            dv_prev_head = b.c[DV]
            b.wait("sync", DV, b.c[DV])
            b.dma("sync", "sOW", lambda e, h=h: e.dma_start(
                out=S["oT_s"].rearrange("(ft p) t -> p ft t", p=128)[:, h, :], in_=oTh))

    def phase_c():
        oTst = arv(0, 16)
        attn_sb = arv(16, 4, 2048)
        x2T = arv(32, 16)
        WB0, WB1 = 48, 64
        fst = arv(80, 48)
        scr = ar(128, 4)
        lrw_sb = arv(132, 16, 12)
        b.dma("pool", "sLR", lambda e: e.dma_start(
            out=lrw_sb[:, :, :], in_=I["lrwT"].rearrange("(ft p) c -> p ft c", p=128)))
        free_w = [b.c[PE]] * 2
        pe_prev = b.c[PE]
        wsem = ["sW0", "sW1"]
        bfree = [0] * 6
        afree = [0] * 6

        for tb in range(NTB):
            b.wait("sync", PE, pe_prev)
            b.dma("sync", "sOT", lambda e, tb=tb: e.dma_start(
                out=oTst[:, :, :], in_=S["oT_s"].rearrange("(ft p) t -> p ft t", p=128)
                [:, :, 512 * tb:512 * (tb + 1)]))
            b.dwait("pe", "sOT")
            b.dwait("dve", "sAW")
            for oc in range(4):
                b.wait("pool", PE, free_w[oc % 2])
                load_w((WB0, WB1)[oc % 2], I["woT"], H, 512 * oc, 512, wsem[oc % 2])
                b.dwait("pe", wsem[oc % 2])
                wv = arv((WB0, WB1)[oc % 2], 16)
                for tt in range(4):
                    bi = (4 * oc + tt) % 3
                    bank = ps[bi]
                    b.wait("pe", DV, bfree[bi])
                    b.wait("pe", AC, afree[bi])
                    for ft in range(16):
                        mm(bank[:], oTst[:, ft, 128 * tt:128 * (tt + 1)], wv[:, ft, :],
                           ft == 0, ft == 15, sem=PE if ft == 15 else None)
                    b.wait("dve", PE, b.c[PE])
                    dve(lambda e, tt=tt, oc=oc, p=bank: e.tensor_copy(
                        attn_sb[:, tt, 512 * oc:512 * (oc + 1)], p[:]), sem=DV)
                    bfree[bi] = b.c[DV]
                free_w[oc % 2] = b.c[PE]
            b.wait("sync", DV, b.c[DV])
            b.dma("sync", "sAW", lambda e, tb=tb: e.dma_start(
                out=S["att_s"].rearrange("(tt p) f -> p tt f", p=128)
                [:, 4 * tb:4 * tb + 4, :], in_=attn_sb[:, :, :]))
            dve(lambda e: e.tensor_single_scalar(sm["stat"][:, 0:4], sm["qnw"][:, 0:4],
                                                 0.0, ALU.mult), sem=DV)
            for tt in range(4):
                dve(lambda e, tt=tt: e.scalar_tensor_tensor(
                    scr[:, 0:2048], attn_sb[:, tt, :], 1.0, attn_sb[:, tt, :],
                    ALU.mult, ALU.mult, accum_out=sm["stat"][:, tt:tt + 1]), sem=DV)
            b.wait("act", DV, b.c[DV])
            act(lambda e: e.activation(sm["stat"][:, 4:8], sm["stat"][:, 0:4], AF.Ln,
                                       scale=1.0 / H, bias=sm["cb"][:, 0:1]), sem=AC)
            b.wait("act", AC, b.c[AC])
            act(lambda e: e.activation(sm["stat"][:, 4:8], sm["stat"][:, 4:8], AF.Exp,
                                       scale=-0.5), sem=AC)
            b.wait("dve", AC, b.c[AC])
            for tt in range(4):
                dve(lambda e, tt=tt: e.tensor_scalar_mul(
                    scr[:, 0:2048], attn_sb[:, tt, :], sm["stat"][:, 4 + tt:5 + tt]),
                    sem=DV)
                b.wait("pe", DV, b.c[DV])
                for ft in range(16):
                    tr(pslot(ft), scr[:, 128 * ft:128 * (ft + 1)], sem=PE)
                    b.wait("dve", PE, b.c[PE])
                    dve(lambda e, ft=ft, tt=tt: e.tensor_scalar_mul(
                        x2T[:, ft, 128 * tt:128 * (tt + 1)], pslot(ft),
                        sm["mlpnw"][:, ft:ft + 1]), sem=DV)
                    b.wait("pe", DV, b.c[DV] - 1)
                b.wait("pe", DV, b.c[DV])

            b.dwait("act", "sFW")
            for wc in range(12):
                b.wait("pool", PE, free_w[wc % 2])
                load_w((WB0, WB1)[wc % 2], I["wmlpT"], H, 512 * wc, 512, wsem[wc % 2])
                b.dwait("pe", wsem[wc % 2])
                wv = arv((WB0, WB1)[wc % 2], 16)
                for osub in range(4):
                    bi = (4 * wc + osub) % 3
                    bank = ps[bi]
                    b.wait("pe", AC, afree[bi])
                    b.wait("pe", DV, bfree[bi])
                    for ft in range(16):
                        mm(bank[:], wv[:, ft, 128 * osub:128 * (osub + 1)], x2T[:, ft, :],
                           ft == 0, ft == 15, sem=PE if ft == 15 else None)
                    b.wait("act", PE, b.c[PE])
                    act(lambda e, i=4 * wc + osub, p=bank: e.activation(
                        fst[:, i, :], p[:], AF.Silu), sem=AC)
                    afree[bi] = b.c[AC]
                free_w[wc % 2] = b.c[PE]
            ac_silu = b.c[AC]
            b.wait("dve", AC, ac_silu)

            for base in (0, 16):
                for hh in range(NFH):
                    for k in range(4):
                        i = base + 4 * hh + k
                        if k > 0:
                            b.wait("dve", PE, b.c[PE])
                        dve(lambda e, i=i: e.tensor_mul(scr[:, 0:512], fst[:, i, :],
                                                        fst[:, i, :]), sem=DV)
                        b.wait("pe", DV, b.c[DV])
                        mm(ps[3][0:1, 0:512], sm["onescol"][:], scr[:, 0:512],
                           k == 0, k == 3, sem=PE)
                    b.wait("dve", PE, b.c[PE])
                    dve(lambda e: e.tensor_copy(sm["row"][0:1, 2048:2560],
                                                ps[3][0:1, 0:512]), sem=DV)
                    b.wait("act", DV, b.c[DV])
                    act(lambda e: e.activation(sm["row"][0:1, 2048:2560],
                                               sm["row"][0:1, 2048:2560], AF.Ln), sem=AC)
                    b.wait("act", AC, b.c[AC])
                    act(lambda e: e.activation(sm["row"][0:1, 2048:2560],
                                               sm["row"][0:1, 2048:2560], AF.Exp,
                                               scale=0.5), sem=AC)
                    b.wait("dve", AC, b.c[AC])
                    dve(lambda e: e.tensor_scalar_max(sm["row"][0:1, 2048:2560],
                                                      sm["row"][0:1, 2048:2560], 1e-6))
                    dve(lambda e: e.reciprocal(sm["row"][0:1, 2048:2560],
                                               sm["row"][0:1, 2048:2560]), sem=DV)
                    bcast_row(512, ps[3], sm["rbc"])
                    for k in range(4):
                        i = base + 4 * hh + k
                        dve(lambda e, i=i: e.tensor_mul(fst[:, i, :], fst[:, i, :],
                                                        sm["rbc"][:, 0:512]), sem=DV)
            b.wait("sync", DV, b.c[DV])
            b.wait("sync", AC, ac_silu)
            for (dst_s, lo) in ((S["fqT_s"], 0), (S["fkT_s"], 16), (S["fvT_s"], 32)):
                b.dma("sync", "sFW", lambda e, d=dst_s, lo=lo, tb=tb: e.dma_start(
                    out=d.rearrange("(ft p) t -> p ft t", p=128)
                    [:, :, 512 * tb:512 * (tb + 1)], in_=fst[:, lo:lo + 16, :]))

            b.dwait("pe", "sLR")
            for tt in range(4):
                gt = 4 * tb + tt
                b.wait("pe", DV, b.c[DV])
                for ft in range(16):
                    mm(ps[4][:, 0:12], x2T[:, ft, 128 * tt:128 * (tt + 1)],
                       lrw_sb[:, ft, :], ft == 0, ft == 15, sem=PE if ft == 15 else None)
                b.wait("dve", PE, b.c[PE])
                dve(lambda e: e.tensor_add(sm["w12"][:, 0:12], ps[4][:, 0:12],
                                           sm["lrb_bc"][:]), sem=DV)
                b.wait("act", DV, b.c[DV])
                act(lambda e: e.activation(sm["s12"][:, 0:12], sm["w12"][:, 0:12],
                                           AF.Abs), sem=AC)
                b.wait("act", AC, b.c[AC])
                act(lambda e: e.activation(sm["s12"][:, 0:12], sm["s12"][:, 0:12],
                                           AF.Exp, scale=-1.0), sem=AC)
                b.wait("act", AC, b.c[AC])
                act(lambda e: e.activation(sm["s12"][:, 0:12], sm["s12"][:, 0:12],
                                           AF.Ln, bias=sm["cb"][:, 1:2]), sem=AC)
                b.wait("act", AC, b.c[AC])
                act(lambda e: e.activation(sm["w12"][:, 0:12], sm["w12"][:, 0:12],
                                           AF.Relu), sem=AC)
                b.wait("dve", AC, b.c[AC])
                dve(lambda e, gt=gt: e.tensor_add(
                    sm["lr_tok"][:, 12 * gt:12 * (gt + 1)], sm["w12"][:, 0:12],
                    sm["s12"][:, 0:12]), sem=DV)
            pe_prev = b.c[PE]

    def phase_d():
        mats = ("st0", "st2", "st1", "w1o")
        stu = {}
        u0 = 0
        for hh in range(NFH):
            for m in mats:
                stu[(hh, m)] = u0
                u0 += 4
        ST = lambda hh, m: arv(stu[(hh, m)], 4)
        fk_c = arv(64, 16)
        fvh = arv(80, 4)
        fqh = arv(84, 4)
        KC0, KC2, VC1, HID, DGP, DHB, SGK, GKS, HBM = [arv(88 + 4 * i, 4)
                                                       for i in range(9)]
        HQ = arv(124, 4)
        OCS = arv(128, 4)
        scr = ar(132, 4)

        def load_base():
            for hh in range(NFH):
                for m, src in zip(mats, (I["w0T"], I["w2T"], I["w1T"], I["w1om"])):
                    b.dma("pool", "sST", lambda e, hh=hh, m=m, src=src: e.dma_start(
                        out=ST(hh, m)[:, :, :],
                        in_=src[hh].rearrange("(dj p) e -> p dj e", p=128)))

        def kpath(hh, c, upd):
            fkh_t = lambda dj: fk_c[:, 4 * hh + dj, :]
            fvh_t = lambda dj: fvh[:, dj, :]
            for (srcf, outs) in ((fkh_t, ((KC0, 0), (KC2, 2))), (fvh_t, ((VC1, 1),))):
                for dj in range(4):
                    for cj in range(4):
                        tr(pslot(cj), srcf(dj)[:, 128 * cj:128 * (cj + 1)],
                           sem=PE)
                        b.wait("dve", PE, b.c[PE])
                        for (dst, li) in outs:
                            lrc = sm["lr_tok"][:, 12 * (4 * c + cj) + 4 * li + hh:
                                               12 * (4 * c + cj) + 4 * li + hh + 1]
                            dve(lambda e, dst=dst, cj=cj, dj=dj, lrc=lrc:
                                e.tensor_scalar_mul(
                                    dst[:, cj, 128 * dj:128 * (dj + 1)],
                                    pslot(cj), lrc), sem=DV)
                        b.wait("pe", DV, b.c[DV] - len(outs))
                b.wait("pe", DV, b.c[DV])
            for cj in range(4):
                for dj in range(4):
                    mm(ps[0][:], fkh_t(dj)[:, 128 * cj:128 * (cj + 1)],
                       ST(hh, "st0")[:, dj, :], dj == 0, dj == 3)
                for dj in range(4):
                    mm(ps[1][:], fkh_t(dj)[:, 128 * cj:128 * (cj + 1)],
                       ST(hh, "st2")[:, dj, :], dj == 0, dj == 3)
                for oj in range(4):
                    mm(ps[2][:], fvh_t(oj)[:, 128 * cj:128 * (cj + 1)],
                       ST(hh, "w1o")[:, oj, :], oj == 0, oj == 3,
                       sem=PE if oj == 3 else None)
                pv = b.c[PE]
                b.wait("act", PE, pv)
                act(lambda e, cj=cj: e.activation(SGK[:, cj, :], ps[0][:], AF.Sigmoid),
                    sem=AC)
                b.wait("dve", AC, b.c[AC])
                b.wait("dve", PE, pv)
                dve(lambda e, cj=cj: e.tensor_mul(GKS[:, cj, :], ps[0][:], SGK[:, cj, :]))
                dve(lambda e, cj=cj: e.tensor_sub(scr[:, 0:512], ps[0][:], GKS[:, cj, :]))
                dve(lambda e, cj=cj: e.tensor_copy(HBM[:, cj, :], ps[1][:]))
                dve(lambda e, cj=cj: e.tensor_mul(HID[:, cj, :], GKS[:, cj, :],
                                                  HBM[:, cj, :]))
                dve(lambda e, cj=cj: e.tensor_mul(scr[:, 512:1024], ps[2][:],
                                                  HBM[:, cj, :]))
                dve(lambda e, cj=cj: e.tensor_mul(DHB[:, cj, :], ps[2][:], GKS[:, cj, :]))
                dve(lambda e, cj=cj: e.scalar_tensor_tensor(
                    scr[:, 0:512], scr[:, 0:512], 1.0, SGK[:, cj, :], ALU.add, ALU.mult))
                dve(lambda e, cj=cj: e.tensor_mul(DGP[:, cj, :], scr[:, 0:512],
                                                  scr[:, 512:1024]), sem=DV)
                b.wait("pe", DV, b.c[DV])
            if not upd:
                return
            for (lhs, rhs, mat) in ((KC0, DGP, "st0"), (KC2, DHB, "st2"),
                                    (VC1, HID, "w1o"), (HID, VC1, "st1")):
                for dj in range(4):
                    bank = ps[3 + dj % 2]
                    for cj in range(4):
                        mm(bank[:], lhs[:, cj, 128 * dj:128 * (dj + 1)], rhs[:, cj, :],
                           cj == 0, cj == 3, sem=PE if cj == 3 else None)
                    b.wait("dve", PE, b.c[PE])
                    dve(lambda e, mat=mat, dj=dj, hh=hh, p=bank: e.tensor_add(
                        ST(hh, mat)[:, dj, :], ST(hh, mat)[:, dj, :], p[:]), sem=DV)
                    if dj >= 1:
                        b.wait("pe", DV, b.c[DV] - 1)
                b.wait("pe", DV, b.c[DV])

        def qpath(hh, c):
            for ej in range(4):
                for dj in range(4):
                    mm(ps[0][:], ST(hh, "st0")[:, dj, 128 * ej:128 * (ej + 1)],
                       fqh[:, dj, :], dj == 0, dj == 3)
                for dj in range(4):
                    mm(ps[1][:], ST(hh, "st2")[:, dj, 128 * ej:128 * (ej + 1)],
                       fqh[:, dj, :], dj == 0, dj == 3, sem=PE if dj == 3 else None)
                pv = b.c[PE]
                b.wait("act", PE, pv)
                act(lambda e: e.activation(scr[:, 0:512], ps[0][:], AF.Sigmoid), sem=AC)
                b.wait("dve", AC, b.c[AC])
                b.wait("dve", PE, pv)
                dve(lambda e: e.tensor_mul(scr[:, 512:1024], ps[0][:], scr[:, 0:512]))
                dve(lambda e, ej=ej: e.tensor_mul(HQ[:, ej, :], scr[:, 512:1024],
                                                  ps[1][:]), sem=DV)
                b.wait("pe", DV, b.c[DV])
            b.dwait("dve", "sOC")
            b.dwait("dve", "sRS")
            for oj in range(4):
                bank = ps[2 + oj % 2]
                for ej in range(4):
                    mm(bank[:], ST(hh, "st1")[:, ej, 128 * oj:128 * (oj + 1)],
                       HQ[:, ej, :], ej == 0, ej == 3, sem=PE if ej == 3 else None)
                b.wait("dve", PE, b.c[PE])
                dve(lambda e, oj=oj, p=bank: e.tensor_copy(OCS[:, oj, :], p[:]), sem=DV)
                if oj >= 1:
                    b.wait("pe", DV, b.c[DV] - 1)
            b.wait("sync", DV, b.c[DV])
            b.dma("sync", "sOC", lambda e, hh=hh, c=c: e.dma_start(
                out=S["oc_s"].rearrange("(q p) t -> p q t", p=128)
                [:, 4 * hh:4 * hh + 4, 512 * c:512 * (c + 1)], in_=OCS[:, :, :]))

        def stream_chunk(c):
            b.dma("sync", "sFK", lambda e, c=c: e.dma_start(
                out=fk_c[:, :, :], in_=S["fkT_s"].rearrange("(ft p) t -> p ft t", p=128)
                [:, :, 512 * c:512 * (c + 1)]))

        def stream_head(c, hh, src, dst, sem):
            b.dma("sync", sem, lambda e, c=c, hh=hh, src=src, dst=dst: e.dma_start(
                out=dst[:, :, :], in_=src.rearrange("(ft p) t -> p ft t", p=128)
                [:, 4 * hh:4 * hh + 4, 512 * c:512 * (c + 1)]))

        load_base()
        b.dwait("pe", "sST")
        b.dwait("dve", "sST")
        rsv = S["rsend"].rearrange("(i p) e -> p i e", p=128)
        ccv = {}
        for c in range(2):
            b.wait("sync", PE, b.c[PE])
            b.wait("sync", DV, b.c[DV])
            stream_chunk(c)
            for hh in range(NFH):
                b.wait("sync", PE, b.c[PE])
                stream_head(c, hh, S["fvT_s"], fvh, "sFV")
                b.dwait("pe", "sFK")
                b.dwait("dve", "sFK")
                b.dwait("pe", "sFV")
                b.dwait("dve", "sFV")
                kpath(hh, c, upd=True)
                if c == 1:
                    for mi, m in enumerate(mats):
                        b.dwait("dve", "sRS")
                        for dj in range(4):
                            dve(lambda e, hh=hh, m=m, dj=dj: e.tensor_scalar_mul(
                                OCS[:, dj, :], ST(hh, m)[:, dj, :],
                                sm["gcols"][:, 2:3]), sem=DV)
                        b.wait("sync", DV, b.c[DV])
                        idx = (hh * 4 + mi) * 4
                        b.dma("sync", "sRS", lambda e, idx=idx: e.dma_start(
                            out=rsv[:, idx:idx + 4, :], in_=OCS[:, :, :]))
                    b.dwait("pool", "sRS")
                    r0 = hh * 4 * FHD
                    b.op("pool", lambda e, r0=r0: e.collective_compute(
                        "AllReduce", ALU.add, replica_groups=GROUPS,
                        ins=[S["rsend"][r0:r0 + 4 * FHD, :].opt()],
                        outs=[S["rgat"][r0:r0 + 4 * FHD, :].opt()]), sem=CC)
                    ccv[hh] = b.c[CC]

        load_base()
        b.dwait("dve", "sST")
        rgv = S["rgat"].rearrange("(i p) e -> p i e", p=128)
        for c in range(2):
            b.wait("sync", PE, b.c[PE])
            b.wait("sync", DV, b.c[DV])
            if c == 0:
                stream_chunk(c)
            for hh in range(NFH):
                if c == 0:
                    b.wait("sync", CC, ccv[hh])
                    for mi, m in enumerate(mats):
                        idx = (hh * 4 + mi) * 4
                        b.wait("sync", DV, b.c[DV])
                        b.dma("sync", "sRL", lambda e, idx=idx: e.dma_start(
                            out=fqh[:, :, :], in_=rgv[:, idx:idx + 4, :]))
                        b.dwait("dve", "sRL")
                        for dj in range(4):
                            dve(lambda e, hh=hh, m=m, dj=dj: e.tensor_scalar_mul(
                                ST(hh, m)[:, dj, :], ST(hh, m)[:, dj, :],
                                sm["gcols"][:, 0:1]))
                            dve(lambda e, hh=hh, m=m, dj=dj: e.scalar_tensor_tensor(
                                ST(hh, m)[:, dj, :], fqh[:, dj, :], sm["gcols"][:, 1:2],
                                ST(hh, m)[:, dj, :], ALU.mult, ALU.add), sem=DV)
                    b.wait("pe", DV, b.c[DV])
                b.wait("sync", PE, b.c[PE])
                b.wait("sync", DV, b.c[DV])
                stream_head(c, hh, S["fqT_s"], fqh, "sFQ")
                if c == 0:
                    stream_head(c, hh, S["fvT_s"], fvh, "sFV")
                b.dwait("pe", "sFK")
                b.dwait("dve", "sFK")
                b.dwait("pe", "sFQ")
                b.dwait("pe", "sFV")
                b.dwait("dve", "sFV")
                qpath(hh, c)
                if c == 0:
                    kpath(hh, c, upd=True)

    def phase_e():
        onst = arv(0, 16)
        ln2T = arv(0, 16)
        hdn = arv(16, 44)
        WB0, WB1 = 60, 76
        WBD = 92
        scr = ar(136, 4)
        att4 = arv(16, 4, 2048)
        xn = [A32[:, 2048 * i:2048 * (i + 1)] for i in range(4)]
        xsem = ["sX0", "sX1", "sX2", "sX3"]
        outt = A32[:, 8192:8704]
        xv = I["x_own"].rearrange("(tt p) f -> p tt f", p=128)
        av = S["att_s"].rearrange("(tt p) f -> p tt f", p=128)
        xnv = S["xnew_s"].rearrange("(tt p) f -> p tt f", p=128)
        yv = y_own.rearrange("(tt p) f -> p tt f", p=128)
        free_w = [b.c[PE]] * 2
        wsem = ["sW0", "sW1"]
        pe_prev = b.c[PE]
        bfree = [0] * 6
        afree = [0] * 6

        for tb in range(NTB):
            b.wait("sync", PE, pe_prev)
            b.wait("sync", DV, b.c[DV])
            b.dma("sync", "sON", lambda e, tb=tb: e.dma_start(
                out=onst[:, :, :], in_=S["oc_s"].rearrange("(q p) t -> p q t", p=128)
                [:, :, 512 * tb:512 * (tb + 1)]))
            b.dwait("dve", "sON")
            b.dwait("pe", "sON")
            for hh in range(NFH):
                for k in range(4):
                    i = 4 * hh + k
                    if k > 0:
                        b.wait("dve", PE, b.c[PE])
                    dve(lambda e, i=i: e.tensor_mul(scr[:, 0:512], onst[:, i, :],
                                                    onst[:, i, :]), sem=DV)
                    b.wait("pe", DV, b.c[DV])
                    mm(ps[5][0:1, 0:512], sm["onescol"][:], scr[:, 0:512],
                       k == 0, k == 3, sem=PE)
                b.wait("dve", PE, b.c[PE])
                dve(lambda e: e.tensor_copy(sm["row"][0:1, 2048:2560],
                                            ps[5][0:1, 0:512]), sem=DV)
                b.wait("act", DV, b.c[DV])
                rsqrt_row(512, 1.0 / FHD)
                bcast_row(512, ps[5], sm["rbc"])
                for k in range(4):
                    i = 4 * hh + k
                    dve(lambda e, i=i, k=k: e.scalar_tensor_tensor(
                        onst[:, i, :], onst[:, i, :], sm["tttnw"][:, k:k + 1],
                        sm["rbc"][:, 0:512], ALU.mult, ALU.mult), sem=DV)
            b.wait("pe", DV, b.c[DV])
            b.dwait("sync", "sXW")
            for tt in range(4):
                gt = 4 * tb + tt
                b.dma("sync", xsem[tt], lambda e, gt=gt, d=xn[tt]: e.dma_start(
                    out=d, in_=xv[:, gt, :]))
            b.wait("sync", DV, b.c[DV])
            b.dma("sync", "sAT", lambda e, tb=tb: e.dma_start(
                out=att4[:, :, :], in_=av[:, 4 * tb:4 * tb + 4, :]))
            for tt in range(4):
                b.dwait("dve", xsem[tt])
            b.dwait("dve", "sAT")
            for mc in range(4):
                b.wait("pool", PE, free_w[mc % 2])
                load_w((WB0, WB1)[mc % 2], I["wmoT"], H, 512 * mc, 512, wsem[mc % 2])
                b.dwait("pe", wsem[mc % 2])
                wv = arv((WB0, WB1)[mc % 2], 16)
                cs = slice(512 * mc, 512 * (mc + 1))
                for tt in range(4):
                    bi = (4 * mc + tt) % 3
                    bank = ps[bi]
                    b.wait("pe", DV, bfree[bi])
                    for ft in range(16):
                        mm(bank[:], onst[:, ft, 128 * tt:128 * (tt + 1)], wv[:, ft, :],
                           ft == 0, ft == 15, sem=PE if ft == 15 else None)
                    b.wait("dve", PE, b.c[PE])
                    dve(lambda e, cs=cs, d=xn[tt], p=bank: e.tensor_add(
                        d[:, cs], d[:, cs], p[:]))
                    dve(lambda e, cs=cs, d=xn[tt], tt=tt: e.tensor_add(
                        d[:, cs], d[:, cs], att4[:, tt, cs]), sem=DV)
                    bfree[bi] = b.c[DV]
                free_w[mc % 2] = b.c[PE]
            for tt in range(4):
                gt = 4 * tb + tt
                dve(lambda e, gt=gt: e.tensor_single_scalar(
                    sm["rscol"][:, gt:gt + 1], sm["cb"][:, 0:1], 0.0, ALU.mult))
                dve(lambda e, d=xn[tt], gt=gt: e.reduce_sum(
                    sm["mucol"][:, gt:gt + 1], d, axis=AX.X), sem=DV)
                dve(lambda e, d=xn[tt], gt=gt: e.scalar_tensor_tensor(
                    scr[:, 0:2048], d, 1.0, d, ALU.mult, ALU.mult,
                    accum_out=sm["rscol"][:, gt:gt + 1]), sem=DV)
                b.wait("sync", DV, b.c[DV])
                b.dma("sync", "sXW", lambda e, gt=gt, d=xn[tt]: e.dma_start(
                    out=xnv[:, gt, :], in_=d))
            pe_prev = b.c[PE]
        b.wait("act", DV, b.c[DV])
        act(lambda e: e.activation(sm["rscol"][:, 8:16], sm["mucol"][:, 0:8],
                                   AF.Square, scale=1.0 / H), sem=AC)
        b.wait("act", AC, b.c[AC])
        act(lambda e: e.activation(sm["mucol"][:, 8:16], sm["mucol"][:, 0:8],
                                   AF.Identity, scale=1.0 / H), sem=AC)
        b.wait("dve", AC, b.c[AC])
        dve(lambda e: e.scalar_tensor_tensor(
            sm["rscol"][:, 0:8], sm["rscol"][:, 0:8], 1.0 / H, sm["rscol"][:, 8:16],
            ALU.mult, ALU.subtract), sem=DV)
        b.wait("act", DV, b.c[DV])
        act(lambda e: e.activation(sm["rscol"][:, 0:8], sm["rscol"][:, 0:8], AF.Ln,
                                   bias=sm["cb"][:, 2:3]), sem=AC)
        b.wait("act", AC, b.c[AC])
        act(lambda e: e.activation(sm["rscol"][:, 0:8], sm["rscol"][:, 0:8], AF.Exp,
                                   scale=-0.5), sem=AC)
        act(lambda e: e.copy(sm["mucol"][:, 0:8], sm["mucol"][:, 8:16]), sem=AC)
        b.wait("dve", AC, b.c[AC])

        for tb in range(NTB):
            for tt in range(4):
                gt = 4 * tb + tt
                b.dwait("sync", "sYW")
                b.dma("sync", xsem[tt], lambda e, gt=gt, d=xn[tt]: e.dma_start(
                    out=d, in_=xnv[:, gt, :]))
                b.dwait("dve", xsem[tt])
                b.wait("dve", PE, b.c[PE])
                dve(lambda e, d=xn[tt], gt=gt: e.tensor_scalar(
                    scr[:, 0:2048], d, sm["mucol"][:, gt:gt + 1],
                    sm["rscol"][:, gt:gt + 1], ALU.subtract, ALU.mult), sem=DV)
                b.wait("pe", DV, b.c[DV])
                for ft in range(16):
                    tr(pslot(ft), scr[:, 128 * ft:128 * (ft + 1)], sem=PE)
                    b.wait("dve", PE, b.c[PE])
                    dve(lambda e, ft=ft, tt=tt: e.tensor_copy(
                        ln2T[:, ft, 128 * tt:128 * (tt + 1)], pslot(ft)), sem=DV)
                    b.wait("pe", DV, b.c[DV] - 1)
                b.wait("pe", DV, b.c[DV])
            free_w[0] = free_w[1] = max(free_w)
            for wc in range(11):
                bs = wc % 2
                WG = (WB0, WBD)[bs]
                WU = (WB1, WBD + 16)[bs]
                gsem = ("sW0", "sW2")[bs]
                usem = ("sW1", "sW3")[bs]
                b.wait("pool", PE, free_w[bs])
                load_w(WG, I["gT"], H, 512 * wc, 512, gsem)
                load_w(WU, I["uT"], H, 512 * wc, 512, usem)
                b.dwait("pe", gsem)
                b.dwait("pe", usem)
                gvw, uvw = arv(WG, 16), arv(WU, 16)
                for isub in range(4):
                    icol = 4 * wc + isub
                    gb = isub % 2
                    sl = sm["rbc2"][:, 512 * gb:512 * (gb + 1)]
                    b.wait("pe", AC, afree[gb])
                    b.wait("pe", DV, bfree[gb])
                    for ft in range(16):
                        mm(ps[gb][:], gvw[:, ft, 128 * isub:128 * (isub + 1)],
                           ln2T[:, ft, :], ft == 0, ft == 15, sem=PE if ft == 15 else None)
                    b.wait("act", PE, b.c[PE])
                    b.wait("act", DV, bfree[gb])
                    act(lambda e, p=ps[gb], icol=icol, sl=sl: e.activation(
                        sl, p[:], AF.Silu, scale=sm["gate_s"][:, icol:icol + 1]), sem=AC)
                    acv = b.c[AC]
                    afree[gb] = acv
                    b.wait("pe", DV, bfree[2 + gb])
                    for ft in range(16):
                        mm(ps[2 + gb][:], uvw[:, ft, 128 * isub:128 * (isub + 1)],
                           ln2T[:, ft, :], ft == 0, ft == 15, sem=PE if ft == 15 else None)
                    b.wait("dve", PE, b.c[PE])
                    b.wait("dve", AC, acv)
                    dve(lambda e, icol=icol, p=ps[2 + gb], sl=sl: e.scalar_tensor_tensor(
                        hdn[:, icol, :], p[:], sm["up_s"][:, icol:icol + 1],
                        sl, ALU.mult, ALU.mult), sem=DV)
                    bfree[gb] = b.c[DV]
                    bfree[2 + gb] = b.c[DV]
                free_w[bs] = b.c[PE]
            b.wait("pe", DV, b.c[DV])
            free_w[0] = free_w[1] = max(free_w)
            for mc in range(4):
                b.wait("pool", PE, free_w[0])
                load_w(WBD, I["dT"], INTER, 512 * mc, 512, "sWD", split=4)
                b.dwait("pe", "sWD")
                dwv = arv(WBD, 44)
                for tt in range(4):
                    bi = tt % 2
                    bank = ps[bi]
                    b.wait("pe", DV, bfree[bi])
                    b.wait("pe", AC, afree[bi])
                    for ki in range(44):
                        mm(bank[:], hdn[:, ki, 128 * tt:128 * (tt + 1)], dwv[:, ki, :],
                           ki == 0, ki == 43, sem=PE if ki == 43 else None)
                    b.wait("dve", PE, b.c[PE])
                    cs = slice(512 * mc, 512 * (mc + 1))
                    dve(lambda e, cs=cs, p=bank: e.tensor_mul(
                        outt[:, 0:512], p[:], sm["ds_bc"][:, cs]))
                    dve(lambda e, cs=cs, tt=tt: e.tensor_add(
                        xn[tt][:, cs], xn[tt][:, cs], outt[:, 0:512]), sem=DV)
                    bfree[bi] = b.c[DV]
                free_w[0] = b.c[PE]
            b.wait("sync", DV, b.c[DV])
            for tt in range(4):
                gt = 4 * tb + tt
                b.dma("sync", "sYW", lambda e, gt=gt, tt=tt: e.dma_start(
                    out=yv[:, gt, :], in_=xn[tt]))
        b.dwait("sync", "sYW")
        if DEBUG:
            b.dma("sync", "sYW", lambda e: e.dma_start(out=DBG["dbg_cols"],
                                                       in_=sm["dbgc"][:]))
            for nm, srcs in (("dbg_ks", "ksend"), ("dbg_oT", "oT_s"),
                             ("dbg_att", "att_s"), ("dbg_fq", "fqT_s"),
                             ("dbg_fk", "fkT_s"), ("dbg_fv", "fvT_s"),
                             ("dbg_oc", "oc_s"), ("dbg_xn", "xnew_s"),
                             ("dbg_rg", "rgat")):
                b.dma("sync", "sYW", lambda e, d=DBG[nm], s=S[srcs]: e.dma_start(
                    out=d, in_=s))
            b.dwait("sync", "sYW")

    nums = sorted(h.num for h in b.sems.values())
    lo, hi = nums[0], nums[-1]
    assert nums == list(range(lo, hi + 1)), nums
    b.raw("pool", lambda e, lo=lo, hi=hi: e.sem_clear(range(lo, hi + 1)))
    b.raw("pool", lambda e: e.sem_inc(b.sems["GO"], 1))
    b.c["GO"] = 1
    for _eng in ("pe", "dve", "act", "sync"):
        b.wait(_eng, "GO", 1)

    load_const()
    b.barrier()
    qres, kres, vres = phase_a()
    b.barrier()
    phase_b(qres, kres, vres)
    b.barrier()
    phase_c()
    b.barrier()
    phase_d()
    b.barrier()
    phase_e()



def kernel(**inp):
    x = np.ascontiguousarray(np.asarray(inp["L0_attn_hidden_states"], np.float32))
    Bn, Sq, _ = x.shape

    T = lambda a: np.ascontiguousarray(np.asarray(a, np.float32).T)
    shared = dict(
        wqkvT=T(inp["L0_attn_attn_qkv_w"]), woT=T(inp["L0_attn_attn_o_w"]),
        wmlpT=T(inp["L0_attn_mlp_qkv_w"]), lrwT=T(inp["L0_attn_lr_w"]),
        wmoT=T(inp["L0_attn_mlp_o_w"]),
        w0T=np.ascontiguousarray(np.asarray(inp["L0_attn_w0"], np.float32)
                                 .transpose(0, 2, 1)),
        w1om=np.ascontiguousarray(np.asarray(inp["L0_attn_w1"], np.float32)),
        w1T=np.ascontiguousarray(np.asarray(inp["L0_attn_w1"], np.float32)
                                 .transpose(0, 2, 1)),
        w2T=np.ascontiguousarray(np.asarray(inp["L0_attn_w2"], np.float32)
                                 .transpose(0, 2, 1)),
        gT=T(inp["L0_ffn_gate_w"]), uT=T(inp["L0_ffn_up_w"]), dT=T(inp["L0_ffn_down_w"]),
    )
    col = lambda a, n: np.ascontiguousarray(np.asarray(a, np.float32).reshape(n, 128).T)
    shared["qnw"] = col(inp["L0_attn_q_norm_w"], 16)
    shared["knw"] = col(inp["L0_attn_k_norm_w"], 16)
    shared["mlpnw"] = col(inp["L0_attn_mlp_norm_w"], 16)
    shared["tttnw"] = col(inp["L0_attn_ttt_norm_w"], 4)
    shared["gate_s"] = col(inp["L0_ffn_gate_s"], 44)
    shared["up_s"] = col(inp["L0_ffn_up_s"], 44)
    shared["down_row"] = np.asarray(inp["L0_ffn_down_s"], np.float32).reshape(1, -1)
    shared["lrb_row"] = np.asarray(inp["L0_attn_lr_b"], np.float32).reshape(1, -1)
    shared["eye_in"] = np.eye(128, dtype=np.float32)
    psw = np.zeros((128, 128), np.float32)
    for d in range(64):
        psw[d + 64, d] = -1.0
        psw[d, d + 64] = 1.0
    shared["psw_in"] = psw
    ii = np.arange(128)[:, None]
    cc = np.arange(128)[None, :]
    shared["mlo_in"] = (ii > cc).astype(np.float32)
    shared["mhi_in"] = (ii <= cc).astype(np.float32)

    invf = 1.0 / (500000.0 ** (np.arange(0, 128, 2, dtype=np.float64) / 128.0))
    fr = np.outer(np.arange(Sq, dtype=np.float64), invf)
    cosT = np.concatenate([np.cos(fr).T, np.cos(fr).T], axis=0).astype(np.float32)
    sinT = np.concatenate([np.sin(fr).T, np.sin(fr).T], axis=0).astype(np.float32)

    nc = build_program()

    in_maps = []
    for c in range(8):
        bq, p = c // 2, c % 2
        t0 = TOWN * p
        m = dict(shared)
        m["x_own"] = np.ascontiguousarray(x[bq, t0:t0 + TOWN])
        m["cosT_in"] = np.ascontiguousarray(cosT[:, t0:t0 + TOWN])
        m["sinT_in"] = np.ascontiguousarray(sinT[:, t0:t0 + TOWN])
        m["hb_in"] = np.full((128, 1), 0.0 if p == 1 else -1e30, np.float32)
        m["ga_in"] = np.array([[1.0 if p == 0 else 0.0]], np.float32)
        m["gb_in"] = np.array([[0.0 if p == 0 else 1.0]], np.float32)
        m["gout_in"] = np.array([[1.0 if p == 0 else 0.0]], np.float32)
        in_maps.append(m)

    res = run_bass_kernel_spmd(nc, in_maps, core_ids=list(range(8)))
    LAST_RESULTS[0] = res
    out = np.empty((Bn, Sq, H), np.float32)
    for c in range(8):
        bq, p = c // 2, c % 2
        out[bq, TOWN * p:TOWN * (p + 1)] = np.asarray(res.results[c]["y_own"])
    return out
